# version 22
# baseline (speedup 1.0000x reference)
import math
import os
from contextlib import ExitStack

import numpy as np

import concourse.bass as bass
import concourse.mybir as mybir
from concourse.bass_utils import run_bass_kernel_spmd

F32 = mybir.dt.float32
BF16 = mybir.dt.bfloat16
I32 = mybir.dt.int32
AF = mybir.ActivationFunctionType
ALU = mybir.AluOpType
AX = mybir.AxisListType

S = 4096
D = 1024
INNER = 4096
NT = S // 128
NCH = D // 128
NO = INNER // 128
LN_EPS = 1e-5
INVSQ = 1.0 / 32.0
C_INV = math.log(10000.0) / 512.0
_CW1 = float(np.float32(6.28125))
_CW2 = float(np.float32(2 * math.pi - _CW1))
_CW3 = float(np.float32(2 * math.pi - _CW1 - float(_CW2)))
PI = math.pi
PI_2 = math.pi / 2.0

_DEBUG_TAP = None
KNT = int(os.environ.get("KNT", "32"))
KPH = int(os.environ.get("KPH", "9"))


def _build(tap=None):
    nc = bass.Bass("TRN2", target_bir_lowering=False,
                   detect_race_conditions=(os.environ.get("KDRC") == "1"))
    es = ExitStack()

    t_x = nc.dram_tensor("x", [S, D], F32, kind="ExternalInput").ap()
    t_wqT = nc.dram_tensor("wqT", [D, D], F32, kind="ExternalInput").ap()
    t_wkT = nc.dram_tensor("wkT", [D, D], F32, kind="ExternalInput").ap()
    t_wvT = nc.dram_tensor("wvT", [D, D], F32, kind="ExternalInput").ap()
    t_woT = nc.dram_tensor("woT", [D, D], F32, kind="ExternalInput").ap()
    t_wgT = nc.dram_tensor("wgT", [D, INNER], F32, kind="ExternalInput").ap()
    t_wuT = nc.dram_tensor("wuT", [D, INNER], F32, kind="ExternalInput").ap()
    t_wdT = nc.dram_tensor("wdT", [INNER, D], F32, kind="ExternalInput").ap()
    t_ks = nc.dram_tensor("ks", [S, 512], F32, kind="ExternalInput").ap()
    t_kc = nc.dram_tensor("kc", [S, 512], F32, kind="ExternalInput").ap()
    t_id = nc.dram_tensor("idn", [128, 128], F32, kind="ExternalInput").ap()
    t_out = nc.dram_tensor("out", [S, D], F32, kind="ExternalOutput").ap()
    _ik = "ExternalOutput" if tap == "B0" else "Internal"
    t_qfs = nc.dram_tensor("qfs", [NT, 128, D], BF16, kind=_ik).ap()
    t_x1s = nc.dram_tensor("x1s", [S, D], F32, kind=_ik).ap()
    t_gqs = nc.dram_tensor("gqs", [NT, 128, INNER], BF16, kind=_ik).ap()
    t_dbg = (nc.dram_tensor("dbg", [S, D], F32, kind="ExternalOutput").ap()
             if tap == "x1" else None)
    if tap == "B0":
        t_dK = nc.dram_tensor("dK", [128, NCH * S], BF16, kind=_ik).ap()
        t_dV = nc.dram_tensor("dV", [128, NT * D], BF16, kind=_ik).ap()
        t_dP = nc.dram_tensor("dP", [128, S], BF16, kind=_ik).ap()
        t_dPT = nc.dram_tensor("dPT", [128, S], BF16, kind=_ik).ap()
        t_dAV = nc.dram_tensor("dAV", [128, D], BF16, kind=_ik).ap()
        t_dY = nc.dram_tensor("dY", [128, D], F32, kind=_ik).ap()
        t_dB = nc.dram_tensor("dB", [128, 10], F32, kind=_ik).ap()
        t_dWD = nc.dram_tensor("dWD", [128, NO * D], BF16, kind=_ik).ap()
        t_dED = nc.dram_tensor("dED", [128, NT], F32, kind=_ik).ap()
        t_dY3 = nc.dram_tensor("dY3", [128, D], F32, kind=_ik).ap()
        t_dTH = nc.dram_tensor("dTH", [128, 9], F32, kind=_ik).ap()
        t_dOF = nc.dram_tensor("dOF", [128, D], BF16, kind=_ik).ap()
        t_dWO = nc.dram_tensor("dWO", [128, NCH * D], BF16, kind=_ik).ap()

    class Sem:
        __slots__ = ("h", "n")

        def __init__(self, name):
            self.h = es.enter_context(nc.semaphore(name))
            self.n = 0

        def inc(self, instr, k=1):
            instr.then_inc(self.h, k)
            self.n += k
            return instr

        def dinc(self, instr):
            return self.inc(instr, 16)

        def wait(self, eng, n=None):
            eng.wait_ge(self.h, self.n if n is None else n)

        def mark(self):
            return (self, self.n)

    def wmark(eng, mk):
        if mk is not None:
            mk[0].wait(eng, mk[1])

    sX0, sX1 = Sem("x0"), Sem("x1")
    sK0, sK1 = Sem("k0"), Sem("k1")
    sW = Sem("w")
    sT = Sem("t")
    sV = Sem("v")
    sS = Sem("s")
    sG = Sem("g")
    sQF = Sem("qf")
    sX1O = Sem("x1o")
    sGQ = Sem("gq")
    sFIN = Sem("fin")
    sVSP = Sem("vsp")
    sTD = (Sem("td0"), Sem("td1"))
    sWM = (Sem("wm0"), Sem("wm1"))
    SLX = (sX0, sX1)
    SLK = (sK0, sK1)

    cpool = es.enter_context(nc.sbuf_tensor("cpool", [128, 16], F32))
    ident = es.enter_context(nc.sbuf_tensor("ident", [128, 128], BF16))
    ln1_m = es.enter_context(nc.sbuf_tensor("ln1_m", [128, NT], F32))
    ln1_rs = es.enter_context(nc.sbuf_tensor("ln1_rs", [128, NT], F32))
    ln2_m = es.enter_context(nc.sbuf_tensor("ln2_m", [128, NT], F32))
    ln2_rs = es.enter_context(nc.sbuf_tensor("ln2_rs", [128, NT], F32))
    q2s = es.enter_context(nc.sbuf_tensor("q2s", [128, NT], F32))
    ed_buf = es.enter_context(nc.sbuf_tensor("ed_buf", [128, NT], F32))
    scr = es.enter_context(nc.sbuf_tensor("scr", [128, 64], F32))
    xin = [es.enter_context(nc.sbuf_tensor(f"xin{i}", [128, D], F32))
           for i in range(2)]

    consts = {}

    def creg(v):
        v = float(v)
        if v not in consts:
            i = len(consts)
            assert i < 16
            nc.vector.memset(cpool[:, i:i + 1], v)
            nc.const_aps.register(F32, v, cpool[:, i:i + 1])
            consts[v] = i

    for v in (LN_EPS, -0.5, 0.5, INVSQ, -C_INV, PI_2, -PI, PI,
              2 * PI, 0.0, 1.0, -1.0):
        creg(v)

    def vsp():
        sVSP.inc(nc.vector.memset(scr[:, 63:64], 0.0))
        nc.vector.wait_ge(sVSP.h, sVSP.n)

    vsp()
    sV.inc(nc.vector.memset(scr[:, 63:64], 0.0))
    pass
    with nc.sbuf_tensor("idf", [128, 128], F32) as idf, \
         nc.sbuf_tensor("pst", [128, 16], F32) as pst:
        sW.dinc(nc.sync.dma_start(out=idf[:, :], in_=t_id[:, :]))
        nc.gpsimd.iota(scr[:, 0:1], pattern=[[0, 1]], base=0,
                       channel_multiplier=1,
                       allow_small_or_imprecise_dtypes=True)
        sG.inc(nc.gpsimd.engine_nop())
        sW.wait(nc.vector)
        sV.inc(nc.vector.tensor_copy(out=ident[:, :], in_=idf[:, :]))

        xcons = [None, None]
        for T in range(KNT + 1):
            if T < KNT:
                sl = T % 2
                wmark(nc.sync, xcons[sl])
                SLX[sl].dinc(nc.sync.dma_start(
                    out=xin[sl][:, :], in_=t_x[T * 128:(T + 1) * 128, :]))
            if T == 0:
                continue
            Tp, sl = T - 1, (T - 1) % 2
            SLX[sl].wait(nc.vector)
            xt = xin[sl]
            nc.vector.bn_stats(pst[:, 0:6], xt[:, 0:512])
            nc.vector.bn_stats(pst[:, 6:12], xt[:, 512:1024])
            vsp()
            nc.vector.bn_aggr(pst[:, 12:14], pst[:, 0:12])
            vsp()
            sV.inc(nc.vector.tensor_copy(out=ln1_m[:, Tp:Tp + 1],
                                         in_=pst[:, 12:13]))
            xcons[sl] = sV.mark()
            sV.wait(nc.scalar)
            nc.scalar.activation(out=pst[:, 14:15], in_=pst[:, 13:14],
                                 func=AF.Ln, bias=LN_EPS, scale=1.0)
            sS.inc(nc.scalar.activation(out=ln1_rs[:, Tp:Tp + 1],
                                        in_=pst[:, 14:15],
                                        func=AF.Exp, scale=-0.5))
        sS.wait(nc.vector)
        nc.all_engine_barrier()
    if KPH <= 0:
        return nc

    def stage_bf16(dst, src_t, n_chunks, width, stg):
        for c in range(n_chunks):
            sV.wait(nc.sync)
            sW.dinc(nc.sync.dma_start(
                out=stg[:, 0:width], in_=src_t[c * 128:(c + 1) * 128, :]))
            sW.wait(nc.vector)
            sV.inc(nc.vector.tensor_copy(out=dst[:, c * width:(c + 1) * width],
                                         in_=stg[:, 0:width]))

    stKV = ExitStack()
    K_feat = stKV.enter_context(nc.sbuf_tensor("K_feat", [128, NCH * S], BF16))

    stA = ExitStack()
    inv_row = stA.enter_context(nc.sbuf_tensor("inv_row", [128, 512], F32))
    wq_bf = stA.enter_context(nc.sbuf_tensor("wq_bf", [128, NCH * D], BF16))
    wk_bf = stA.enter_context(nc.sbuf_tensor("wk_bf", [128, NCH * D], BF16))
    kin = stA.enter_context(nc.sbuf_tensor("kin", [128, 1024], F32))
    h_bf = [stA.enter_context(nc.sbuf_tensor(f"h_bf{i}", [128, D], BF16))
            for i in range(2)]
    h_feat = stA.enter_context(nc.sbuf_tensor("h_feat", [128, D], BF16))
    q_bf = [stA.enter_context(nc.sbuf_tensor(f"q_bf{i}", [128, D], BF16))
            for i in range(2)]
    k_bf = [stA.enter_context(nc.sbuf_tensor(f"k_bf{i}", [128, D], BF16))
            for i in range(2)]
    q_rot = stA.enter_context(nc.sbuf_tensor("q_rot", [128, D], BF16))
    k_rot = stA.enter_context(nc.sbuf_tensor("k_rot", [128, D], BF16))
    q_feat = stA.enter_context(nc.sbuf_tensor("q_feat", [128, D], BF16))
    angs = stA.enter_context(nc.sbuf_tensor("angs", [128, 512], F32))
    angc = stA.enter_context(nc.sbuf_tensor("angc", [128, 512], F32))
    cwa = stA.enter_context(nc.sbuf_tensor("cwa", [128, 512], F32))
    cwb = stA.enter_context(nc.sbuf_tensor("cwb", [128, 512], F32))
    cosb = [stA.enter_context(nc.sbuf_tensor(f"cosb{i}", [128, 512], BF16))
            for i in range(3)]
    sinb = [stA.enter_context(nc.sbuf_tensor(f"sinb{i}", [128, 512], BF16))
            for i in range(3)]
    sq_bf = stA.enter_context(nc.sbuf_tensor("sq_bf", [128, D], BF16))
    m14 = stA.enter_context(nc.sbuf_tensor("m14", [128, 2048], BF16))

    sG.inc(nc.gpsimd.iota(inv_row[:, :], pattern=[[1, 512]], base=0,
                          channel_multiplier=0,
                          allow_small_or_imprecise_dtypes=True))
    sG.wait(nc.scalar)
    sS.inc(nc.scalar.activation(out=inv_row[:, :], in_=inv_row[:, :],
                                func=AF.Exp, scale=-C_INV))
    pass
    stage_bf16(wq_bf, t_wqT, NCH, D, xin[0])
    stage_bf16(wk_bf, t_wkT, NCH, D, xin[0])
    sV.wait(nc.tensor)
    sV.wait(nc.scalar)
    sV.wait(nc.sync)

    paH = stA.enter_context(nc.psum_tensor("paH", [128, 1024], BF16))
    paQ = stA.enter_context(nc.psum_tensor("paQ", [128, 1024], F32))
    paK = stA.enter_context(nc.psum_tensor("paK", [128, 1024], F32))
    paT = stA.enter_context(nc.psum_tensor("paT", [128, 1024], BF16))
    paR = stA.enter_context(nc.psum_tensor("paR", [128, 1024], BF16))

    xcons = [None, None]
    kcons = None
    qfree = None
    evdone = None

    def p1_a1(T):
        nonlocal kcons
        sl = T % 2
        wmark(nc.sync, xcons[sl])
        SLX[sl].dinc(nc.sync.dma_start(
            out=xin[sl][:, :], in_=t_x[T * 128:(T + 1) * 128, :]))
        wmark(nc.sync, kcons)
        SLK[0].dinc(nc.sync.dma_start(
            out=kin[:, 0:512], in_=t_ks[T * 128:(T + 1) * 128, :]))
        SLK[0].dinc(nc.sync.dma_start(
            out=kin[:, 512:1024], in_=t_kc[T * 128:(T + 1) * 128, :]))
        SLX[sl].wait(nc.vector)
        sV.inc(nc.vector.tensor_scalar(
            out=h_bf[sl][:, :], in0=xin[sl][:, :],
            scalar1=ln1_m[:, T:T + 1], scalar2=ln1_rs[:, T:T + 1],
            op0=ALU.subtract, op1=ALU.mult))
        hmk[sl] = sV.mark()
        xcons[sl] = sV.mark()
        tc = 1 + 3 * sl
        nc.vector.tensor_scalar(out=scr[:, tc:tc + 1], in0=scr[:, 0:1],
                                scalar1=float(128 * T), scalar2=None,
                                op0=ALU.add)
        vsp()
        sV.inc(nc.vector.memset(scr[:, 63:64], 0.0))
        sV.wait(nc.scalar)
        nc.scalar.activation(out=angs[:, :], in_=inv_row[:, :], func=AF.Copy,
                             scale=scr[:, tc:tc + 1])
        sS.inc(nc.scalar.activation(out=angc[:, :], in_=inv_row[:, :],
                                    func=AF.Identity, scale=scr[:, tc:tc + 1],
                                    bias=PI_2))
        SLK[0].wait(nc.vector)
        sS.wait(nc.vector)
        nc.vector.scalar_tensor_tensor(out=cwa[:, :], in0=kin[:, 0:512],
                                       scalar=-_CW1, in1=angs[:, :],
                                       op0=ALU.mult, op1=ALU.add)
        nc.vector.scalar_tensor_tensor(out=cwb[:, :], in0=kin[:, 512:1024],
                                       scalar=-_CW1, in1=angc[:, :],
                                       op0=ALU.mult, op1=ALU.add)
        nc.vector.scalar_tensor_tensor(out=angs[:, :], in0=kin[:, 0:512],
                                       scalar=-_CW2, in1=cwa[:, :],
                                       op0=ALU.mult, op1=ALU.add)
        nc.vector.scalar_tensor_tensor(out=angc[:, :], in0=kin[:, 512:1024],
                                       scalar=-_CW2, in1=cwb[:, :],
                                       op0=ALU.mult, op1=ALU.add)
        nc.vector.scalar_tensor_tensor(out=cwa[:, :], in0=kin[:, 0:512],
                                       scalar=-_CW3, in1=angs[:, :],
                                       op0=ALU.mult, op1=ALU.add)
        sV.inc(nc.vector.scalar_tensor_tensor(
            out=cwb[:, :], in0=kin[:, 512:1024], scalar=-_CW3,
            in1=angc[:, :], op0=ALU.mult, op1=ALU.add))
        kcons = sV.mark()
        s3 = T % 3
        sV.wait(nc.scalar)
        wmark(nc.scalar, cs_cons[s3])
        nc.scalar.activation(out=angs[:, :], in_=cwa[:, :], func=AF.Relu,
                             scale=-1.0, bias=PI)
        nc.scalar.activation(out=angc[:, :], in_=cwb[:, :], func=AF.Relu,
                             scale=-1.0, bias=PI)
        nc.scalar.activation(out=cwa[:, :], in_=angs[:, :], func=AF.Relu,
                             scale=-1.0, bias=2 * PI)
        nc.scalar.activation(out=cwb[:, :], in_=angc[:, :], func=AF.Relu,
                             scale=-1.0, bias=2 * PI)
        nc.scalar.activation(out=sinb[s3][:, :], in_=cwa[:, :], func=AF.Sin,
                             bias=-PI, scale=1.0)
        sS.inc(nc.scalar.activation(out=cosb[s3][:, :], in_=cwb[:, :],
                                    func=AF.Sin, bias=-PI, scale=1.0))
        csmk[s3] = sS.mark()

    def pM_a1(T):
        nonlocal evdone
        sl = T % 2
        wmark(nc.tensor, hmk[sl])
        wmark(nc.tensor, evdone)
        wmark(nc.tensor, ropemk[sl])
        for c in range(NCH):
            i = nc.tensor.transpose(paH[:, c * 128:(c + 1) * 128],
                                    h_bf[sl][:, c * 128:(c + 1) * 128],
                                    ident[:, :])
        sT.inc(i)
        sT.wait(nc.scalar)
        sS.inc(nc.scalar.copy(out=h_feat[:, :], in_=paH[:, :]))
        sS.wait(nc.tensor)
        for c in range(NCH):
            lhs = h_feat[:, c * 128:(c + 1) * 128]
            nc.tensor.matmul(paQ[:, 0:512], lhs, wq_bf[:, c * D:c * D + 512],
                             start=(c == 0), stop=(c == NCH - 1))
            nc.tensor.matmul(paQ[:, 512:1024], lhs,
                             wq_bf[:, c * D + 512:(c + 1) * D],
                             start=(c == 0), stop=(c == NCH - 1))
        for c in range(NCH):
            lhs = h_feat[:, c * 128:(c + 1) * 128]
            nc.tensor.matmul(paK[:, 0:512], lhs, wk_bf[:, c * D:c * D + 512],
                             start=(c == 0), stop=(c == NCH - 1))
            i = nc.tensor.matmul(paK[:, 512:1024], lhs,
                                 wk_bf[:, c * D + 512:(c + 1) * D],
                                 start=(c == 0), stop=(c == NCH - 1))
        sT.inc(i)
        sT.wait(nc.scalar)
        nc.scalar.copy(out=q_bf[sl][:, :], in_=paQ[:, :])
        sS.inc(nc.scalar.copy(out=k_bf[sl][:, :], in_=paK[:, :]))
        qkmk[sl] = sS.mark()

    def pR_a1(T):
        nonlocal qfree, evdone, qkT
        sl = T % 2
        s3 = T % 3
        wmark(nc.vector, qkmk[sl])
        wmark(nc.vector, csmk[s3])
        wmark(nc.vector, qkT)
        for srcb, dst in ((q_bf[sl], q_rot), (k_bf[sl], k_rot)):
            xr, xi = srcb[:, 0:512], srcb[:, 512:1024]
            nc.vector.tensor_tensor(out=m14[:, 0:512], in0=xr,
                                    in1=cosb[s3][:, :], op=ALU.mult)
            nc.vector.tensor_tensor(out=m14[:, 512:1024], in0=xi,
                                    in1=sinb[s3][:, :], op=ALU.mult)
            nc.vector.tensor_tensor(out=m14[:, 1024:1536], in0=xr,
                                    in1=sinb[s3][:, :], op=ALU.mult)
            nc.vector.tensor_tensor(out=m14[:, 1536:2048], in0=xi,
                                    in1=cosb[s3][:, :], op=ALU.mult)
            nc.vector.tensor_tensor(out=dst[:, 0:512], in0=m14[:, 0:512],
                                    in1=m14[:, 512:1024], op=ALU.subtract)
            i = nc.vector.tensor_tensor(out=dst[:, 512:1024],
                                        in0=m14[:, 1024:1536],
                                        in1=m14[:, 1536:2048], op=ALU.add)
        sV.inc(i)
        ropemk[sl] = sV.mark()
        cs_cons[s3] = sV.mark()
        sV.wait(nc.scalar)
        nc.scalar.activation(out=sq_bf[:, :], in_=q_rot[:, :], func=AF.Square,
                             accum_out=q2s[:, T:T + 1])
        sS.inc(nc.scalar.activation(out=sq_bf[:, :], in_=k_rot[:, :],
                                    func=AF.Square,
                                    accum_out=scr[:, 3:4]))
        sS.wait(nc.vector)
        if T == 0:
            nc.vector.tensor_copy(out=scr[:, 2:3], in_=scr[:, 3:4])
        else:
            nc.vector.tensor_tensor(out=scr[:, 2:3], in0=scr[:, 2:3],
                                    in1=scr[:, 3:4], op=ALU.max)
        vsp()
        sV.inc(nc.vector.memset(scr[:, 63:64], 0.0))
        sV.wait(nc.tensor)
        wmark(nc.tensor, evdone)
        for c in range(NCH):
            i = nc.tensor.transpose(paT[:, c * 128:(c + 1) * 128],
                                    q_rot[:, c * 128:(c + 1) * 128],
                                    ident[:, :])
        sT.inc(i)
        sT.wait(nc.scalar)
        wmark(nc.scalar, qfree)
        sS.inc(nc.scalar.copy(out=q_feat[:, :], in_=paT[:, :]))
        sS.wait(nc.tensor)
        for c in range(NCH):
            i = nc.tensor.transpose(paR[:, c * 128:(c + 1) * 128],
                                    k_rot[:, c * 128:(c + 1) * 128],
                                    ident[:, :])
        sT.inc(i)
        qkT = sT.mark()
        sT.wait(nc.scalar)
        kf = K_feat[:, :].rearrange("p (c k) -> p c k", c=NCH)
        pa = paR[:, :].rearrange("p (c k) -> p c k", c=NCH)
        sS.inc(nc.scalar.copy(out=kf[:, :, T * 128:(T + 1) * 128], in_=pa))
        evdone = sS.mark()
        sS.wait(nc.gpsimd)
        sQF.dinc(nc.gpsimd.dma_start(out=t_qfs[T, :, :], in_=q_feat[:, :]))
        qfree = sQF.mark()

    hmk = [None, None]
    csmk = [None, None, None]
    cs_cons = [None, None, None]
    qkmk = [None, None]
    ropemk = [None, None]
    qkT = None
    for T in range(KNT + 2):
        if T < KNT:
            p1_a1(T)
        if 1 <= T <= KNT:
            pM_a1(T - 1)
        if 2 <= T <= KNT + 1:
            pR_a1(T - 2)
    sQF.wait(nc.gpsimd)
    nc.all_engine_barrier()
    stA.close()
    if KPH <= 1:
        return nc

    stKV2 = ExitStack()
    V_tok = stKV2.enter_context(nc.sbuf_tensor("V_tok", [128, NT * D], BF16))
    stA2 = ExitStack()
    wv_bf = stA2.enter_context(nc.sbuf_tensor("wv_bf", [128, NCH * D], BF16))
    h2bf = stA2.enter_context(nc.sbuf_tensor("h2bf", [128, D], BF16))
    h2ft = stA2.enter_context(nc.sbuf_tensor("h2ft", [128, D], BF16))
    stage_bf16(wv_bf, t_wvT, NCH, D, xin[0])
    sV.wait(nc.sync)
    sV.wait(nc.tensor)

    paH2 = stA2.enter_context(nc.psum_tensor("paH2", [128, 1024], BF16))
    paV = stA2.enter_context(nc.psum_tensor("paV", [128, 1024], F32))
    xcons = [None, None]
    evdone = None
    hcons = None
    for T in range(KNT + 1):
        if T < KNT:
            sl = T % 2
            wmark(nc.sync, xcons[sl])
            SLX[sl].dinc(nc.sync.dma_start(
                out=xin[sl][:, :], in_=t_x[T * 128:(T + 1) * 128, :]))
        if T == 0:
            continue
        Tp, sl = T - 1, (T - 1) % 2
        SLX[sl].wait(nc.vector)
        wmark(nc.vector, hcons)
        sV.inc(nc.vector.tensor_scalar(
            out=h2bf[:, :], in0=xin[sl][:, :],
            scalar1=ln1_m[:, Tp:Tp + 1], scalar2=ln1_rs[:, Tp:Tp + 1],
            op0=ALU.subtract, op1=ALU.mult))
        xcons[sl] = sV.mark()
        sV.wait(nc.tensor)
        wmark(nc.tensor, evdone)
        for c in range(NCH):
            i = nc.tensor.transpose(paH2[:, c * 128:(c + 1) * 128],
                                    h2bf[:, c * 128:(c + 1) * 128],
                                    ident[:, :])
        sT.inc(i)
        hcons = sT.mark()
        sT.wait(nc.scalar)
        sS.inc(nc.scalar.copy(out=h2ft[:, :], in_=paH2[:, :]))
        sS.wait(nc.tensor)
        for c in range(NCH):
            lhs = h2ft[:, c * 128:(c + 1) * 128]
            nc.tensor.matmul(paV[:, 0:512], lhs, wv_bf[:, c * D:c * D + 512],
                             start=(c == 0), stop=(c == NCH - 1))
            i = nc.tensor.matmul(paV[:, 512:1024], lhs,
                                 wv_bf[:, c * D + 512:(c + 1) * D],
                                 start=(c == 0), stop=(c == NCH - 1))
        sT.inc(i)
        sT.wait(nc.scalar)
        sS.inc(nc.scalar.copy(out=V_tok[:, Tp * D:(Tp + 1) * D],
                              in_=paV[:, :]))
        evdone = sS.mark()
    sS.wait(nc.vector)
    nc.all_engine_barrier()
    if tap == "B0":
        sFIN.dinc(nc.gpsimd.dma_start(out=t_dK[:, :], in_=K_feat[:, :]))
        sFIN.dinc(nc.gpsimd.dma_start(out=t_dV[:, :], in_=V_tok[:, :]))
        sFIN.wait(nc.gpsimd)
        nc.all_engine_barrier()
    stA2.close()
    if KPH <= 2:
        return nc

    stB = ExitStack()
    wo_bf = stB.enter_context(nc.sbuf_tensor("wo_bf", [128, NCH * D], BF16))
    qf_in = [stB.enter_context(nc.sbuf_tensor(f"qf{i}", [128, D], BF16))
             for i in range(2)]
    probs = stB.enter_context(nc.sbuf_tensor("probs", [128, S], BF16))
    probsT = stB.enter_context(nc.sbuf_tensor("probsT", [128, S], BF16))
    av_bf = stB.enter_context(nc.sbuf_tensor("av_bf", [128, D], BF16))
    of_bf = stB.enter_context(nc.sbuf_tensor("of_bf", [128, D], BF16))
    y1 = stB.enter_context(nc.sbuf_tensor("y1", [128, D], F32))
    x1t = stB.enter_context(nc.sbuf_tensor("x1t", [128, D], F32))
    wmst = stB.enter_context(nc.sbuf_tensor("wmst", [128, 4096], F32))
    bst = stB.enter_context(nc.sbuf_tensor("bst", [128, 32], F32))

    pass
    stage_bf16(wo_bf, t_woT, NCH, D, xin[0])
    sV.wait(nc.sync)

    with nc.sbuf_tensor("bc", [128, 128], BF16) as bc, \
         nc.sbuf_tensor("onr", [1, 128], F32) as onr, \
         nc.psum_tensor("pbx", [128, 128], BF16) as pbx, \
         nc.psum_tensor("pbb", [128, 1], F32) as pbb:
        nc.vector.memset(bc[:, :], 0.0)
        nc.vector.memset(onr[:, :], 1.0)
        vsp()
        sV.inc(nc.vector.memset(scr[:, 63:64], 0.0))
        sV.wait(nc.scalar)
        sS.inc(nc.scalar.copy(out=bc[:, 0:1], in_=scr[:, 2:3]))
        sS.wait(nc.tensor)
        sT.inc(nc.tensor.transpose(pbx[:, :], bc[:, :], ident[:, :]))
        sT.wait(nc.vector)
        sV.inc(nc.vector.reduce_max(out=scr[0:1, 9:10], in_=pbx[0:1, :],
                                    axis=AX.X))
        sV.wait(nc.scalar)
        nc.scalar.activation(out=scr[0:1, 10:11], in_=scr[0:1, 9:10],
                             func=AF.Ln)
        sS.inc(nc.scalar.activation(out=scr[0:1, 11:12], in_=scr[0:1, 10:11],
                                    func=AF.Exp, scale=0.5))
        sS.wait(nc.vector)
        sV.inc(nc.vector.tensor_scalar(out=scr[0:1, 12:13],
                                       in0=scr[0:1, 11:12],
                                       scalar1=-INVSQ, scalar2=None,
                                       op0=ALU.mult))
        sV.wait(nc.tensor)
        sT.inc(nc.tensor.matmul(pbb[:, :], onr[:, :], scr[0:1, 12:13],
                                start=True, stop=True))
        sT.wait(nc.vector)
        sV.inc(nc.vector.tensor_copy(out=scr[:, 13:14], in_=pbb[:, :]))

    wm_chunks = []
    for tt, col in ((t_wgT, 16), (t_wuT, 17), (t_wdT, 18)):
        rows, cols = (D, INNER) if tt is not t_wdT else (INNER, D)
        for r0 in range(0, rows, 128):
            for c0 in range(0, cols, 2048):
                w_ = min(2048, cols - c0)
                wm_chunks.append((tt[r0:r0 + 128, c0:c0 + w_], col, w_))
    nc.vector.memset(scr[:, 16:19], 0.0)
    vsp()
    sV.inc(nc.vector.memset(scr[:, 63:64], 0.0))

    psS4 = stB.enter_context(nc.psum_tensor("psS4", [128, 2048], F32))
    pbT0 = stB.enter_context(nc.psum_tensor("pbT0", [128, 1024], BF16))
    pbT1 = stB.enter_context(nc.psum_tensor("pbT1", [128, 1024], BF16))
    pbAP = stB.enter_context(nc.psum_tensor("pbAP", [128, 1024], F32))

    qcons = [None, None]
    xcons = [None, None]
    x1free = None
    expdone = None
    wmfree = [None, None]
    ebmk = [None, None]
    summk = [None]

    def p1_b(T):
        sl = T % 2
        wmark(nc.sync, qcons[sl])
        SLK[sl].dinc(nc.sync.dma_start(out=qf_in[sl][:, :],
                                       in_=t_qfs[T, :, :]))
        wmark(nc.sync, xcons[sl])
        SLX[sl].dinc(nc.sync.dma_start(
            out=xin[sl][:, :], in_=t_x[T * 128:(T + 1) * 128, :]))
        sS.wait(nc.scalar)
        nc.scalar.activation(out=scr[:, 20:21], in_=q2s[:, T:T + 1],
                             func=AF.Ln)
        sS.inc(nc.scalar.activation(out=scr[:, 21:22], in_=scr[:, 20:21],
                                    func=AF.Exp, scale=0.5))
        sS.wait(nc.vector)
        sV.inc(nc.vector.tensor_tensor(out=bst[:, 24 + sl:25 + sl],
                                       in0=scr[:, 21:22], in1=scr[:, 13:14],
                                       op=ALU.mult))
        ebmk[sl] = sV.mark()

    def p2_b(T):
        nonlocal x1free, expdone
        sl = T % 2
        for j in range((len(wm_chunks) * T) // NT,
                       (len(wm_chunks) * (T + 1)) // NT):
            ap, col, w_ = wm_chunks[j]
            wmark(nc.gpsimd, wmfree[j % 2])
            sWM[j % 2].dinc(nc.gpsimd.dma_start(
                out=wmst[:, 2048 * (j % 2):2048 * (j % 2) + w_], in_=ap))
        SLK[sl].wait(nc.tensor)
        qf = qf_in[sl]
        for half in range(2):
            wmark(nc.tensor, expdone)
            for kb2 in range(4):
                kb = half * 4 + kb2
                for c in range(NCH):
                    i = nc.tensor.matmul(
                        psS4[:, kb2 * 512:(kb2 + 1) * 512],
                        qf[:, c * 128:(c + 1) * 128],
                        K_feat[:, c * S + kb * 512:c * S + (kb + 1) * 512],
                        start=(c == 0), stop=(c == NCH - 1))
            sT.inc(i)
            if half == 1:
                qcons[sl] = sT.mark()
            sT.wait(nc.scalar)
            if half == 0:
                wmark(nc.scalar, ebmk[sl])
                wmark(nc.scalar, summk[0])
            for kb2 in range(4):
                kb = half * 4 + kb2
                i = nc.scalar.activation(
                    out=probs[:, kb * 512:(kb + 1) * 512],
                    in_=psS4[:, kb2 * 512:(kb2 + 1) * 512], func=AF.Exp,
                    bias=bst[:, 24 + sl:25 + sl], scale=INVSQ,
                    accum_out=bst[:, kb:kb + 1])
            sS.inc(i)
            expdone = sS.mark()
        sS.wait(nc.vector)
        nc.vector.reduce_sum(out=bst[:, 8:9], in_=bst[:, 0:8], axis=AX.X)
        vsp()
        nc.vector.reciprocal(out=bst[:, 9:10], in_=bst[:, 8:9])
        vsp()
        sV.inc(nc.vector.memset(scr[:, 63:64], 0.0))
        summk[0] = sV.mark()
        if tap == "B0" and T == 0:
            sS.wait(nc.gpsimd)
            sV.wait(nc.gpsimd)
            sFIN.dinc(nc.gpsimd.dma_start(out=t_dP[:, :], in_=probs[:, :]))
            sFIN.dinc(nc.gpsimd.dma_start(out=t_dB[:, :], in_=bst[:, 0:10]))
        evmk = [None, None]
        for grp in range(4):
            pb = (pbT0, pbT1)[grp % 2]
            wmark(nc.tensor, evmk[grp % 2])
            for cc in range(8):
                c = grp * 8 + cc
                i = nc.tensor.transpose(pb[:, cc * 128:(cc + 1) * 128],
                                        probs[:, c * 128:(c + 1) * 128],
                                        ident[:, :])
            sT.inc(i)
            sT.wait(nc.scalar)
            sS.inc(nc.scalar.copy(out=probsT[:, grp * 1024:(grp + 1) * 1024],
                                  in_=pb[:, :]))
            evmk[grp % 2] = sS.mark()
            wmark(nc.tensor, evmk[grp % 2])
            for cc in range(8):
                c = grp * 8 + cc
                nc.tensor.matmul(pbAP[:, 0:512],
                                 probsT[:, c * 128:(c + 1) * 128],
                                 V_tok[:, c * 1024:c * 1024 + 512],
                                 start=(c == 0), stop=(c == NO - 1))
                i = nc.tensor.matmul(pbAP[:, 512:1024],
                                     probsT[:, c * 128:(c + 1) * 128],
                                     V_tok[:, c * 1024 + 512:(c + 1) * 1024],
                                     start=(c == 0), stop=(c == NO - 1))
        sT.inc(i)
        sT.wait(nc.scalar)
        sS.inc(nc.scalar.copy(out=av_bf[:, :], in_=pbAP[:, :]))
        sS.wait(nc.tensor)
        for c in range(NCH):
            i = nc.tensor.transpose(pbT0[:, c * 128:(c + 1) * 128],
                                    av_bf[:, c * 128:(c + 1) * 128],
                                    ident[:, :])
        sT.inc(i)
        sT.wait(nc.scalar)
        sS.inc(nc.scalar.copy(out=of_bf[:, :], in_=pbT0[:, :]))
        sS.wait(nc.tensor)
        for c in range(NCH):
            lhs = of_bf[:, c * 128:(c + 1) * 128]
            nc.tensor.matmul(pbAP[:, 0:512], lhs,
                             wo_bf[:, c * D:c * D + 512],
                             start=(c == 0), stop=(c == NCH - 1))
            i = nc.tensor.matmul(pbAP[:, 512:1024], lhs,
                                 wo_bf[:, c * D + 512:(c + 1) * D],
                                 start=(c == 0), stop=(c == NCH - 1))
        sT.inc(i)
        sT.wait(nc.scalar)
        sS.inc(nc.scalar.activation(out=y1[:, :], in_=pbAP[:, :],
                                    func=AF.Identity,
                                    scale=bst[:, 9:10]))
        if tap == "B0" and T == 0:
            sS.wait(nc.gpsimd)
            sFIN.dinc(nc.gpsimd.dma_start(out=t_dPT[:, :], in_=probsT[:, :]))
            sFIN.dinc(nc.gpsimd.dma_start(out=t_dAV[:, :], in_=av_bf[:, :]))
            sFIN.dinc(nc.gpsimd.dma_start(out=t_dY[:, :], in_=y1[:, :]))
            sFIN.dinc(nc.gpsimd.dma_start(out=t_dOF[:, :], in_=of_bf[:, :]))
            sFIN.dinc(nc.gpsimd.dma_start(out=t_dWO[:, :], in_=wo_bf[:, :]))
            sFIN.wait(nc.vector)
        sS.wait(nc.vector)
        nc.vector.bn_stats(bst[:, 10:16], y1[:, 0:512])
        nc.vector.bn_stats(bst[:, 16:22], y1[:, 512:1024])
        vsp()
        nc.vector.bn_aggr(bst[:, 22:24], bst[:, 10:22])
        vsp()
        sV.inc(nc.vector.memset(scr[:, 63:64], 0.0))
        sV.wait(nc.scalar)
        nc.scalar.activation(out=scr[:, 26:27], in_=bst[:, 23:24], func=AF.Ln,
                             bias=LN_EPS, scale=1.0)
        sS.inc(nc.scalar.activation(out=scr[:, 27:28], in_=scr[:, 26:27],
                                    func=AF.Exp, scale=-0.5))
        sS.wait(nc.vector)
        SLX[sl].wait(nc.vector)
        wmark(nc.vector, x1free)
        nc.vector.tensor_scalar(out=y1[:, :], in0=y1[:, :],
                                scalar1=bst[:, 22:23], scalar2=scr[:, 27:28],
                                op0=ALU.subtract, op1=ALU.mult)
        vsp()
        sV.inc(nc.vector.tensor_tensor(out=x1t[:, :], in0=y1[:, :],
                                       in1=xin[sl][:, :], op=ALU.add))
        xcons[sl] = sV.mark()
        nc.vector.bn_stats(bst[:, 10:16], x1t[:, 0:512])
        nc.vector.bn_stats(bst[:, 16:22], x1t[:, 512:1024])
        vsp()
        nc.vector.bn_aggr(bst[:, 22:24], bst[:, 10:22])
        vsp()
        sV.inc(nc.vector.tensor_copy(out=ln2_m[:, T:T + 1],
                                     in_=bst[:, 22:23]))
        sV.wait(nc.scalar)
        nc.scalar.activation(out=scr[:, 26:27], in_=bst[:, 23:24], func=AF.Ln,
                             bias=LN_EPS, scale=1.0)
        sS.inc(nc.scalar.activation(out=ln2_rs[:, T:T + 1],
                                    in_=scr[:, 26:27], func=AF.Exp,
                                    scale=-0.5))
        sV.wait(nc.gpsimd)
        sX1O.dinc(nc.gpsimd.dma_start(
            out=t_x1s[T * 128:(T + 1) * 128, :], in_=x1t[:, :]))
        x1free = sX1O.mark()
        lo = (len(wm_chunks) * T) // NT
        hi = (len(wm_chunks) * (T + 1)) // NT
        for j in range(lo, hi):
            ap, col, w_ = wm_chunks[j]
            sWM[j % 2].wait(nc.vector)
            nc.vector.tensor_reduce(out=scr[:, 30:31],
                                    in_=wmst[:, 2048 * (j % 2):
                                             2048 * (j % 2) + w_],
                                    axis=AX.X, op=ALU.add,
                                    apply_absolute_value=True)
            vsp()
            sV.inc(nc.vector.tensor_tensor(out=scr[:, col:col + 1],
                                           in0=scr[:, col:col + 1],
                                           in1=scr[:, 30:31], op=ALU.add))
            wmfree[j % 2] = sV.mark()

    for T in range(KNT + 1):
        if T < KNT:
            p1_b(T)
        if T >= 1:
            p2_b(T - 1)
    sX1O.wait(nc.gpsimd)
    nc.all_engine_barrier()
    stB.close()
    stKV2.close()
    stKV.close()
    if KPH <= 3:
        return nc

    stTH = ExitStack()
    thr_g = stTH.enter_context(nc.sbuf_tensor("thr_g", [128, 12], F32))
    with nc.sbuf_tensor("onc", [128, 1], F32) as onc, \
         nc.sbuf_tensor("onr2", [1, 128], F32) as onr2, \
         nc.psum_tensor("pth", [128, 8], F32) as pth:
        nc.vector.memset(onc[:, :], 1.0)
        nc.vector.memset(onr2[:, :], 1.0)
        vsp()
        sV.inc(nc.vector.memset(scr[:, 63:64], 0.0))
        sV.wait(nc.tensor)
        for i_, col in enumerate((16, 17, 18)):
            i = nc.tensor.matmul(pth[0:1, i_:i_ + 1], scr[:, col:col + 1],
                                 onc[:, :], start=True, stop=True)
        sT.inc(i)
        sT.wait(nc.vector)
        sV.inc(nc.vector.tensor_scalar(out=scr[0:1, 32:35], in0=pth[0:1, 0:3],
                                       scalar1=1.0 / (D * INNER),
                                       scalar2=1e-5, op0=ALU.mult,
                                       op1=ALU.max))
        sV.wait(nc.tensor)
        sT.inc(nc.tensor.matmul(pth[:, 3:6], onr2[:, :], scr[0:1, 32:35],
                                start=True, stop=True))
        sT.wait(nc.vector)
        nc.vector.tensor_copy(out=thr_g[:, 3:6], in_=pth[:, 3:6])
        vsp()
        nc.vector.tensor_scalar(out=thr_g[:, 0:3], in0=thr_g[:, 3:6],
                                scalar1=0.5, scalar2=None, op0=ALU.mult)
        nc.vector.tensor_scalar(out=thr_g[:, 6:9], in0=thr_g[:, 3:6],
                                scalar1=-0.5, scalar2=None, op0=ALU.mult)
        vsp()
        sV.inc(nc.vector.memset(scr[:, 63:64], 0.0))
    sV.wait(nc.tensor)
    sV.wait(nc.scalar)
    sV.wait(nc.sync)
    sV.wait(nc.gpsimd)

    stF1 = ExitStack()
    wg_t = stF1.enter_context(nc.sbuf_tensor("wg_t", [128, NCH * INNER],
                                             BF16))
    wu_t = stF1.enter_context(nc.sbuf_tensor("wu_t", [128, NCH * INNER],
                                             BF16))
    pass

    def ternarize(wtst, tcmp, jobs):
        free = [None, None]
        k = 0
        for dst, src_t, tc, ntc, n_chunks, width in jobs:
            for c in range(n_chunks):
                for c0 in range(0, width, 2048):
                    s = k % 2
                    wmark(nc.sync, free[s])
                    sTD[s].dinc(nc.sync.dma_start(
                        out=wtst[s][:, :],
                        in_=src_t[c * 128:(c + 1) * 128, c0:c0 + 2048]))
                    sTD[s].wait(nc.vector)
                    nc.vector.tensor_scalar(out=tcmp[:, 0:2048],
                                            in0=wtst[s][:, :],
                                            scalar1=thr_g[:, tc:tc + 1],
                                            scalar2=None, op0=ALU.is_gt)
                    nc.vector.tensor_scalar(out=tcmp[:, 2048:4096],
                                            in0=wtst[s][:, :],
                                            scalar1=thr_g[:, ntc:ntc + 1],
                                            scalar2=None, op0=ALU.is_lt)
                    sV.inc(nc.vector.tensor_tensor(
                        out=dst[:, c * width + c0:c * width + c0 + 2048],
                        in0=tcmp[:, 0:2048], in1=tcmp[:, 2048:4096],
                        op=ALU.subtract))
                    free[s] = sV.mark()
                    k += 1

    with nc.sbuf_tensor("wtstA", [128, 2048], F32) as wtstA, \
         nc.sbuf_tensor("wtstB", [128, 2048], F32) as wtstB, \
         nc.sbuf_tensor("tcmpA", [128, 4096], F32) as tcmpA:
        ternarize([wtstA, wtstB], tcmpA,
                  [(wg_t, t_wgT, 0, 6, NCH, INNER),
                   (wu_t, t_wuT, 1, 7, NCH, INNER)])
    sV.wait(nc.tensor)
    sV.wait(nc.scalar)

    h2t = [stF1.enter_context(nc.sbuf_tensor(f"h2t{i}", [128, D], F32))
           for i in range(2)]
    Xbf = [stF1.enter_context(nc.sbuf_tensor(f"Xbf{i}", [128, D], BF16))
           for i in range(2)]
    graw = stF1.enter_context(nc.sbuf_tensor("graw", [128, INNER], F32))
    silu_b = [stF1.enter_context(nc.sbuf_tensor(f"silu_b{i}", [128, 512],
                                                 F32)) for i in range(2)]
    qq = [stF1.enter_context(nc.sbuf_tensor(f"qq{i}", [128, 512], F32))
          for i in range(4)]
    qi = stF1.enter_context(nc.sbuf_tensor("qi", [128, 512], I32))
    qc = [stF1.enter_context(nc.sbuf_tensor(f"qc{i}", [128, 512], BF16))
          for i in range(2)]
    Gq = stF1.enter_context(nc.sbuf_tensor("Gq", [128, INNER], BF16))
    Xft2 = [stF1.enter_context(nc.sbuf_tensor(f"Xft2{i}", [128, D], BF16))
            for i in range(2)]
    gqf = [stF1.enter_context(nc.sbuf_tensor("gqf0", [128, INNER], BF16))] * 2
    mx8 = stF1.enter_context(nc.sbuf_tensor("mx8", [128, 16], F32))

    pg = [stF1.enter_context(nc.psum_tensor(f"pg{i}", [128, 512], F32))
          for i in range(2)]
    pu = [stF1.enter_context(nc.psum_tensor(f"pu{i}", [128, 512], F32))
          for i in range(2)]
    pxT = stF1.enter_context(nc.psum_tensor("pxT", [128, 1024], BF16))
    pgT = [stF1.enter_context(nc.psum_tensor(f"pgT{i}", [128, 1024], BF16))
           for i in range(2)]

    xcons = [None, None]
    gqfree = [None, None]
    xtdone = None
    xbf_mk = [None, None]
    gq_mk = None

    def p1_f1(T):
        sl = T % 2
        base = 40 - 8 * sl
        wmark(nc.sync, xcons[sl])
        SLX[sl].dinc(nc.sync.dma_start(
            out=xin[sl][:, :], in_=t_x1s[T * 128:(T + 1) * 128, :]))
        SLX[sl].wait(nc.vector)
        nc.vector.tensor_scalar(out=h2t[sl][:, :], in0=xin[sl][:, :],
                                scalar1=ln2_m[:, T:T + 1],
                                scalar2=ln2_rs[:, T:T + 1],
                                op0=ALU.subtract, op1=ALU.mult)
        vsp()
        sV.inc(nc.vector.memset(scr[:, 63:64], 0.0))
        xcons[sl] = sV.mark()
        nc.vector.tensor_reduce(out=scr[:, base:base + 1], in_=h2t[sl][:, :],
                                axis=AX.X, op=ALU.max,
                                apply_absolute_value=True)
        vsp()
        nc.vector.tensor_scalar(out=scr[:, base + 1:base + 2],
                                in0=scr[:, base:base + 1],
                                scalar1=1e-5, scalar2=None, op0=ALU.max)
        vsp()
        nc.vector.reciprocal(out=scr[:, base + 2:base + 3],
                             in_=scr[:, base + 1:base + 2])
        nc.vector.tensor_tensor(out=scr[:, base + 4:base + 5],
                                in0=scr[:, base + 1:base + 2],
                                in1=thr_g[:, 3:4], op=ALU.mult)
        nc.vector.tensor_tensor(out=scr[:, base + 5:base + 6],
                                in0=scr[:, base + 1:base + 2],
                                in1=thr_g[:, 4:5], op=ALU.mult)
        vsp()
        nc.vector.tensor_scalar(out=scr[:, base + 3:base + 4],
                                in0=scr[:, base + 2:base + 3],
                                scalar1=127.0, scalar2=None, op0=ALU.mult)
        sV.inc(nc.vector.tensor_scalar(out=scr[:, base + 6:base + 8],
                                       in0=scr[:, base + 4:base + 6],
                                       scalar1=1.0 / 127.0, scalar2=None,
                                       op0=ALU.mult))
        sV.wait(nc.scalar)
        for b in range(2):
            hb = h2t[sl][:, b * 512:(b + 1) * 512]
            nc.scalar.activation(out=qi[:, :], in_=hb, func=AF.Copy,
                                 scale=scr[:, base + 3:base + 4], bias=-0.5)
            nc.scalar.activation(out=qq[2 * b][:, :], in_=hb, func=AF.Copy,
                                 scale=scr[:, base + 3:base + 4])
            i = nc.scalar.activation(out=qq[2 * b + 1][:, :], in_=qi[:, :],
                                     func=AF.Copy, bias=0.5)
        sS.inc(i)
        sS.wait(nc.vector)
        nc.vector.tensor_tensor(out=qc[0][:, :], in0=qq[0][:, :],
                                in1=qq[1][:, :], op=ALU.is_gt)
        nc.vector.tensor_tensor(out=qc[1][:, :], in0=qq[2][:, :],
                                in1=qq[3][:, :], op=ALU.is_gt)
        nc.vector.scalar_tensor_tensor(out=Xbf[sl][:, 0:512],
                                       in0=qc[0][:, :], scalar=-0.5,
                                       in1=qq[1][:, :],
                                       op0=ALU.add, op1=ALU.add)
        sV.inc(nc.vector.scalar_tensor_tensor(
            out=Xbf[sl][:, 512:1024], in0=qc[1][:, :], scalar=-0.5,
            in1=qq[3][:, :], op0=ALU.add, op1=ALU.add))
        xbf_mk[sl] = sV.mark()

    def pX_f1(T):
        nonlocal xtdone
        sl = T % 2
        wmark(nc.tensor, xbf_mk[sl])
        wmark(nc.tensor, xtdone)
        for c in range(NCH):
            i = nc.tensor.transpose(pxT[:, c * 128:(c + 1) * 128],
                                    Xbf[sl][:, c * 128:(c + 1) * 128],
                                    ident[:, :])
        sT.inc(i)
        sT.wait(nc.scalar)
        wmark(nc.scalar, pamk[sl])
        sS.inc(nc.scalar.copy(out=Xft2[sl][:, :], in_=pxT[:, :]))
        xtdone = sS.mark()

    def pA_f1(T):
        sl = T % 2
        base = 40 - 8 * sl
        Xft = Xft2[sl]
        wmark(nc.tensor, xtdone_t[T])
        evmk = [None, None]
        for ob in range(8):
            g_, u_ = pg[ob % 2], pu[ob % 2]
            wmark(nc.tensor, evmk[ob % 2])
            for c in range(NCH):
                nc.tensor.matmul(g_[:, :], Xft[:, c * 128:(c + 1) * 128],
                                 wg_t[:, c * INNER + ob * 512:
                                      c * INNER + (ob + 1) * 512],
                                 start=(c == 0), stop=(c == NCH - 1))
            for c in range(NCH):
                i = nc.tensor.matmul(u_[:, :], Xft[:, c * 128:(c + 1) * 128],
                                     wu_t[:, c * INNER + ob * 512:
                                          c * INNER + (ob + 1) * 512],
                                     start=(c == 0), stop=(c == NCH - 1))
            sT.inc(i)
            if ob == 7:
                pamk[sl] = sT.mark()
            sT.wait(nc.scalar)
            sS.inc(nc.scalar.activation(out=silu_b[ob % 2][:, :],
                                        in_=g_[:, :], func=AF.Silu,
                                        scale=scr[:, base + 6:base + 7]))
            sS.wait(nc.vector)
            nc.vector.tensor_tensor(out=graw[:, ob * 512:(ob + 1) * 512],
                                    in0=silu_b[ob % 2][:, :], in1=u_[:, :],
                                    op=ALU.mult)
            vsp()
            sV.inc(nc.vector.tensor_reduce(
                out=mx8[:, ob:ob + 1], in_=graw[:, ob * 512:(ob + 1) * 512],
                axis=AX.X, op=ALU.max, apply_absolute_value=True))
            evmk[ob % 2] = sV.mark()
        vsp()
        nc.vector.reduce_max(out=scr[:, 48:49], in_=mx8[:, 0:8], axis=AX.X)
        vsp()
        nc.vector.tensor_tensor(out=scr[:, 49:50], in0=scr[:, 48:49],
                                in1=scr[:, base + 7:base + 8], op=ALU.mult)
        vsp()
        nc.vector.tensor_scalar(out=scr[:, 50:51], in0=scr[:, 49:50],
                                scalar1=1e-5, scalar2=None, op0=ALU.max)
        vsp()
        nc.vector.reciprocal(out=scr[:, 51:52], in_=scr[:, 50:51])
        nc.vector.tensor_tensor(out=scr[:, 54:55], in0=scr[:, 50:51],
                                in1=thr_g[:, 5:6], op=ALU.mult)
        vsp()
        nc.vector.tensor_scalar(out=scr[:, 52:53], in0=scr[:, 51:52],
                                scalar1=127.0, scalar2=None, op0=ALU.mult)
        nc.vector.tensor_scalar(out=ed_buf[:, T:T + 1], in0=scr[:, 54:55],
                                scalar1=1.0 / 127.0, scalar2=None,
                                op0=ALU.mult)
        vsp()
        sV.inc(nc.vector.tensor_tensor(out=scr[:, 53:54], in0=scr[:, 52:53],
                                       in1=scr[:, base + 7:base + 8],
                                       op=ALU.mult))

    def pC_f1(T):
        nonlocal gq_mk
        sV.wait(nc.scalar)
        wmark(nc.vector, gqT_mk[0])
        gdone = [None, None]
        for ob in range(8):
            gb = graw[:, ob * 512:(ob + 1) * 512]
            r = ob % 2
            wmark(nc.scalar, gdone[r])
            nc.scalar.activation(out=qi[:, :], in_=gb, func=AF.Copy,
                                 scale=scr[:, 53:54], bias=-0.5)
            nc.scalar.activation(out=qq[2 * r][:, :], in_=gb, func=AF.Copy,
                                 scale=scr[:, 53:54])
            sS.inc(nc.scalar.activation(out=qq[2 * r + 1][:, :],
                                        in_=qi[:, :], func=AF.Copy,
                                        bias=0.5))
            sS.wait(nc.vector)
            nc.vector.tensor_tensor(out=qc[r][:, :], in0=qq[2 * r][:, :],
                                    in1=qq[2 * r + 1][:, :], op=ALU.is_gt)
            vsp()
            sV.inc(nc.vector.scalar_tensor_tensor(
                out=Gq[:, ob * 512:(ob + 1) * 512], in0=qc[r][:, :],
                scalar=-0.5, in1=qq[2 * r + 1][:, :],
                op0=ALU.add, op1=ALU.add))
            gdone[r] = sV.mark()
        gq_mk = sV.mark()

    def pB_f1(T):
        sl = T % 2
        g_out = gqf[sl]
        wmark(nc.tensor, gq_mk_t[T])
        for grp in range(4):
            pb = pgT[grp % 2]
            wmark(nc.tensor, evmk2[grp % 2])
            for cc2 in range(8):
                c = grp * 8 + cc2
                i = nc.tensor.transpose(pb[:, cc2 * 128:(cc2 + 1) * 128],
                                        Gq[:, c * 128:(c + 1) * 128],
                                        ident[:, :])
            sT.inc(i)
            if grp == 3:
                gqT_mk[0] = sT.mark()
            sT.wait(nc.scalar)
            wmark(nc.scalar, gqfree[sl])
            sS.inc(nc.scalar.copy(
                out=g_out[:, grp * 1024:(grp + 1) * 1024], in_=pb[:, :]))
            evmk2[grp % 2] = sS.mark()
            sS.wait(nc.gpsimd)
            sGQ.dinc(nc.gpsimd.dma_start(
                out=t_gqs[T, :, grp * 1024:(grp + 1) * 1024],
                in_=g_out[:, grp * 1024:(grp + 1) * 1024]))
        gqfree[sl] = sGQ.mark()

    gq_mk_t = {}
    xtdone_t = {}
    pamk = [None, None]
    gqT_mk = [None]
    evmk2 = [None, None]
    for T in range(KNT + 2):
        if T < KNT:
            p1_f1(T)
        if T < KNT:
            pX_f1(T)
            xtdone_t[T] = xtdone
        if 1 <= T <= KNT:
            pA_f1(T - 1)
        if 2 <= T <= KNT + 1:
            pB_f1(T - 2)
        if 1 <= T <= KNT:
            pC_f1(T - 1)
            gq_mk_t[T - 1] = gq_mk
    sGQ.wait(nc.gpsimd)
    nc.all_engine_barrier()
    stF1.close()
    if KPH <= 4:
        return nc

    stF2 = ExitStack()
    wd_t = stF2.enter_context(nc.sbuf_tensor("wd_t", [128, NO * D], BF16))
    wtsd = [stF2.enter_context(nc.sbuf_tensor(f"wtsd{i}", [128, D], F32))
            for i in range(2)]
    tcm2 = stF2.enter_context(nc.sbuf_tensor("tcm2", [128, 2048], F32))
    gq_in = [stF2.enter_context(nc.sbuf_tensor(f"gqi{i}", [128, INNER],
                                               BF16)) for i in range(2)]
    y3 = stF2.enter_context(nc.sbuf_tensor("y3", [128, D], F32))
    outt = [stF2.enter_context(nc.sbuf_tensor(f"outt{i}", [128, D], F32))
            for i in range(2)]
    bs2 = stF2.enter_context(nc.sbuf_tensor("bs2", [128, 16], F32))

    pass

    sWD = Sem("wd")
    wdfree = [None, None]
    for c in range(NO):
        s = c % 2
        wmark(nc.sync, wdfree[s])
        sTD[s].dinc(nc.sync.dma_start(out=wtsd[s][:, :],
                                      in_=t_wdT[c * 128:(c + 1) * 128, :]))
        sTD[s].wait(nc.vector)
        nc.vector.tensor_scalar(out=tcm2[:, 0:1024], in0=wtsd[s][:, :],
                                scalar1=thr_g[:, 2:3], scalar2=None,
                                op0=ALU.is_gt)
        nc.vector.tensor_scalar(out=tcm2[:, 1024:2048], in0=wtsd[s][:, :],
                                scalar1=thr_g[:, 8:9], scalar2=None,
                                op0=ALU.is_lt)
        sWD.inc(nc.vector.tensor_tensor(out=wd_t[:, c * D:(c + 1) * D],
                                        in0=tcm2[:, 0:1024],
                                        in1=tcm2[:, 1024:2048],
                                        op=ALU.subtract))
        wdfree[s] = sWD.mark()

    if tap == "B0":
        nc.vector.wait_ge(sWD.h, NO)
        vsp()
        sV.inc(nc.vector.memset(scr[:, 63:64], 0.0))
        sV.wait(nc.gpsimd)
        sFIN.dinc(nc.gpsimd.dma_start(out=t_dWD[:, :], in_=wd_t[:, :]))
        sFIN.dinc(nc.gpsimd.dma_start(out=t_dED[:, :], in_=ed_buf[:, :]))
        sFIN.dinc(nc.gpsimd.dma_start(out=t_dTH[:, :], in_=thr_g[:, 0:9]))
    pd = [stF2.enter_context(nc.psum_tensor(f"pd{i}", [128, 1024], F32))
          for i in range(2)]
    xcons = [None, None]
    gcons = [None, None]
    ofree = [None, None]
    ofin = [None, None]
    for T in range(KNT + 1):
        if T < KNT:
            sl = T % 2
            wmark(nc.sync, gcons[sl])
            SLK[sl].dinc(nc.sync.dma_start(out=gq_in[sl][:, :],
                                           in_=t_gqs[T, :, :]))
            wmark(nc.sync, xcons[sl])
            SLX[sl].dinc(nc.sync.dma_start(
                out=xin[sl][:, :], in_=t_x1s[T * 128:(T + 1) * 128, :]))
        if T == 0:
            continue
        Tp, sl = T - 1, (T - 1) % 2
        p_ = pd[sl]
        SLK[sl].wait(nc.tensor)
        wmark(nc.tensor, ofree[sl])
        gq_c = gq_in[sl]
        for c in range(NO):
            if Tp == 0:
                nc.tensor.wait_ge(sWD.h, c + 1)
            nc.tensor.matmul(p_[:, 0:512], gq_c[:, c * 128:(c + 1) * 128],
                             wd_t[:, c * D:c * D + 512],
                             start=(c == 0), stop=(c == NO - 1))
            i = nc.tensor.matmul(p_[:, 512:1024],
                                 gq_c[:, c * 128:(c + 1) * 128],
                                 wd_t[:, c * D + 512:(c + 1) * D],
                                 start=(c == 0), stop=(c == NO - 1))
        sT.inc(i)
        gcons[sl] = sT.mark()
        sT.wait(nc.scalar)
        sS.inc(nc.scalar.activation(out=y3[:, :], in_=p_[:, :],
                                    func=AF.Identity,
                                    scale=ed_buf[:, Tp:Tp + 1]))
        ofree[sl] = sS.mark()
        if tap == "B0" and Tp == 0:
            sS.wait(nc.gpsimd)
            sFIN.dinc(nc.gpsimd.dma_start(out=t_dY3[:, :], in_=y3[:, :]))
        sS.wait(nc.vector)
        nc.vector.bn_stats(bs2[:, 0:6], y3[:, 0:512])
        nc.vector.bn_stats(bs2[:, 6:12], y3[:, 512:1024])
        vsp()
        nc.vector.bn_aggr(bs2[:, 12:14], bs2[:, 0:12])
        vsp()
        sV.inc(nc.vector.memset(scr[:, 63:64], 0.0))
        sV.wait(nc.scalar)
        nc.scalar.activation(out=scr[:, 56:57], in_=bs2[:, 13:14], func=AF.Ln,
                             bias=LN_EPS, scale=1.0)
        sS.inc(nc.scalar.activation(out=scr[:, 57:58], in_=scr[:, 56:57],
                                    func=AF.Exp, scale=-0.5))
        sS.wait(nc.vector)
        SLX[sl].wait(nc.vector)
        nc.vector.tensor_scalar(out=y3[:, :], in0=y3[:, :],
                                scalar1=bs2[:, 12:13], scalar2=scr[:, 57:58],
                                op0=ALU.subtract, op1=ALU.mult)
        vsp()
        wmark(nc.vector, ofin[sl])
        sV.inc(nc.vector.tensor_tensor(out=outt[sl][:, :], in0=y3[:, :],
                                       in1=xin[sl][:, :], op=ALU.add))
        xcons[sl] = sV.mark()
        sV.wait(nc.gpsimd)
        sFIN.dinc(nc.gpsimd.dma_start(
            out=t_out[Tp * 128:(Tp + 1) * 128, :], in_=outt[sl][:, :]))
        ofin[sl] = sFIN.mark()

    if tap == "x1":
        for T in range(NT):
            sFIN.dinc(nc.sync.dma_start(
                out=t_dbg[T * 128:(T + 1) * 128, :],
                in_=t_x1s[T * 128:(T + 1) * 128, :]))
    sFIN.wait(nc.gpsimd)
    stF2.close()
    stTH.close()
    es.close()
    return nc


_NC_CACHE = {}


def _get_nc(tap):
    if tap not in _NC_CACHE:
        _NC_CACHE[tap] = _build(tap)
    return _NC_CACHE[tap]


def _host_tables():
    j = np.arange(512, dtype=np.float64)
    inv = np.exp(-C_INV * j)
    t = np.arange(S, dtype=np.float64)
    ang = t[:, None] * inv[None, :]
    ks = np.round(ang / (2 * math.pi)).astype(np.float32)
    kc = np.round((ang + PI_2) / (2 * math.pi)).astype(np.float32)
    return np.ascontiguousarray(ks), np.ascontiguousarray(kc)


def kernel(L0_attn_x, L0_attn_wq, L0_attn_wk, L0_attn_wv, L0_attn_wo,
           L0_ffn_w_gate, L0_ffn_w_up, L0_ffn_w_down):
    x = np.ascontiguousarray(np.asarray(L0_attn_x, dtype=np.float32))
    wq = np.asarray(L0_attn_wq, dtype=np.float32)
    wk = np.asarray(L0_attn_wk, dtype=np.float32)
    wv = np.asarray(L0_attn_wv, dtype=np.float32)
    wo = np.asarray(L0_attn_wo, dtype=np.float32)
    wg = np.asarray(L0_ffn_w_gate, dtype=np.float32)
    wu = np.asarray(L0_ffn_w_up, dtype=np.float32)
    wd = np.asarray(L0_ffn_w_down, dtype=np.float32)

    perm = np.concatenate([np.arange(0, D, 2), np.arange(1, D, 2)])
    common = dict(
        wqT=np.ascontiguousarray(wq[perm, :].T),
        wkT=np.ascontiguousarray(wk[perm, :].T),
        wvT=np.ascontiguousarray(wv.T),
        woT=np.ascontiguousarray(wo.T),
        wgT=np.ascontiguousarray(wg.T),
        wuT=np.ascontiguousarray(wu.T),
        wdT=np.ascontiguousarray(wd.T),
        idn=np.eye(128, dtype=np.float32),
    )
    common["ks"], common["kc"] = _host_tables()

    tap = _DEBUG_TAP
    nc = _get_nc(tap)
    in_maps = [dict(x=np.ascontiguousarray(x[b]), **common) for b in range(8)]
    res = run_bass_kernel_spmd(nc, in_maps, core_ids=list(range(8)))
    out = np.stack([np.asarray(r["out"]) for r in res.results], axis=0)
    if tap is not None:
        kernel._dbg = np.stack([np.asarray(r["dbg"]) for r in res.results], 0)
    return out


# version 24
# speedup vs baseline: 1.0206x; 1.0206x over previous
import math
import os
from contextlib import ExitStack

import numpy as np

import concourse.bass as bass
import concourse.mybir as mybir
from concourse.bass_utils import run_bass_kernel_spmd

F32 = mybir.dt.float32
BF16 = mybir.dt.bfloat16
I32 = mybir.dt.int32
AF = mybir.ActivationFunctionType
ALU = mybir.AluOpType
AX = mybir.AxisListType

S = 4096
D = 1024
INNER = 4096
NT = S // 128
NCH = D // 128
NO = INNER // 128
LN_EPS = 1e-5
INVSQ = 1.0 / 32.0
C_INV = math.log(10000.0) / 512.0
_CW1 = float(np.float32(6.28125))
_CW2 = float(np.float32(2 * math.pi - _CW1))
_CW3 = float(np.float32(2 * math.pi - _CW1 - float(_CW2)))
PI = math.pi
PI_2 = math.pi / 2.0

_DEBUG_TAP = None
KNT = int(os.environ.get("KNT", "32"))
KPH = int(os.environ.get("KPH", "9"))


def _build(tap=None):
    nc = bass.Bass("TRN2", target_bir_lowering=False,
                   detect_race_conditions=(os.environ.get("KDRC") == "1"))
    es = ExitStack()

    t_x = nc.dram_tensor("x", [S, D], F32, kind="ExternalInput").ap()
    t_wqT = nc.dram_tensor("wqT", [D, D], F32, kind="ExternalInput").ap()
    t_wkT = nc.dram_tensor("wkT", [D, D], F32, kind="ExternalInput").ap()
    t_wvT = nc.dram_tensor("wvT", [D, D], F32, kind="ExternalInput").ap()
    t_woT = nc.dram_tensor("woT", [D, D], F32, kind="ExternalInput").ap()
    t_wgT = nc.dram_tensor("wgT", [D, INNER], F32, kind="ExternalInput").ap()
    t_wuT = nc.dram_tensor("wuT", [D, INNER], F32, kind="ExternalInput").ap()
    t_wdT = nc.dram_tensor("wdT", [INNER, D], F32, kind="ExternalInput").ap()
    t_ks = nc.dram_tensor("ks", [S, 512], F32, kind="ExternalInput").ap()
    t_kc = nc.dram_tensor("kc", [S, 512], F32, kind="ExternalInput").ap()
    t_id = nc.dram_tensor("idn", [128, 128], F32, kind="ExternalInput").ap()
    t_out = nc.dram_tensor("out", [S, D], F32, kind="ExternalOutput").ap()
    _ik = "ExternalOutput" if tap == "B0" else "Internal"
    t_qfs = nc.dram_tensor("qfs", [NT, 128, D], BF16, kind=_ik).ap()
    t_x1s = nc.dram_tensor("x1s", [S, D], F32, kind=_ik).ap()
    t_gqs = nc.dram_tensor("gqs", [NT, 128, INNER], BF16, kind=_ik).ap()
    t_xfs = nc.dram_tensor("xfs", [NT, 128, D], BF16, kind=_ik).ap()
    t_dbg = (nc.dram_tensor("dbg", [S, D], F32, kind="ExternalOutput").ap()
             if tap == "x1" else None)
    if tap == "B0":
        t_dK = nc.dram_tensor("dK", [128, NCH * S], BF16, kind=_ik).ap()
        t_dV = nc.dram_tensor("dV", [128, NT * D], BF16, kind=_ik).ap()
        t_dP = nc.dram_tensor("dP", [128, S], BF16, kind=_ik).ap()
        t_dPT = nc.dram_tensor("dPT", [128, S], BF16, kind=_ik).ap()
        t_dAV = nc.dram_tensor("dAV", [128, D], BF16, kind=_ik).ap()
        t_dY = nc.dram_tensor("dY", [128, D], F32, kind=_ik).ap()
        t_dB = nc.dram_tensor("dB", [128, 10], F32, kind=_ik).ap()
        t_dWD = nc.dram_tensor("dWD", [128, NO * D], BF16, kind=_ik).ap()
        t_dED = nc.dram_tensor("dED", [128, NT], F32, kind=_ik).ap()
        t_dY3 = nc.dram_tensor("dY3", [128, D], F32, kind=_ik).ap()
        t_dTH = nc.dram_tensor("dTH", [128, 9], F32, kind=_ik).ap()
        t_dOF = nc.dram_tensor("dOF", [128, D], BF16, kind=_ik).ap()
        t_dWO = nc.dram_tensor("dWO", [128, NCH * D], BF16, kind=_ik).ap()

    class Sem:
        __slots__ = ("h", "n")

        def __init__(self, name):
            self.h = es.enter_context(nc.semaphore(name))
            self.n = 0

        def inc(self, instr, k=1):
            instr.then_inc(self.h, k)
            self.n += k
            return instr

        def dinc(self, instr):
            return self.inc(instr, 16)

        def wait(self, eng, n=None):
            eng.wait_ge(self.h, self.n if n is None else n)

        def mark(self):
            return (self, self.n)

    def wmark(eng, mk):
        if mk is not None:
            mk[0].wait(eng, mk[1])

    sX0, sX1 = Sem("x0"), Sem("x1")
    sK0, sK1 = Sem("k0"), Sem("k1")
    sW = Sem("w")
    sT = Sem("t")
    sV = Sem("v")
    sS = Sem("s")
    sG = Sem("g")
    sQF = Sem("qf")
    sX1O = Sem("x1o")
    sGQ = Sem("gq")
    sFIN = Sem("fin")
    sVSP = Sem("vsp")
    sTD = (Sem("td0"), Sem("td1"))
    sWM = (Sem("wm0"), Sem("wm1"))
    SLX = (sX0, sX1)
    SLK = (sK0, sK1)

    cpool = es.enter_context(nc.sbuf_tensor("cpool", [128, 16], F32))
    ident = es.enter_context(nc.sbuf_tensor("ident", [128, 128], BF16))
    ln1_m = es.enter_context(nc.sbuf_tensor("ln1_m", [128, NT], F32))
    ln1_rs = es.enter_context(nc.sbuf_tensor("ln1_rs", [128, NT], F32))
    ln2_m = es.enter_context(nc.sbuf_tensor("ln2_m", [128, NT], F32))
    ln2_rs = es.enter_context(nc.sbuf_tensor("ln2_rs", [128, NT], F32))
    q2s = es.enter_context(nc.sbuf_tensor("q2s", [128, NT], F32))
    ed_buf = es.enter_context(nc.sbuf_tensor("ed_buf", [128, NT], F32))
    h2clip = es.enter_context(nc.sbuf_tensor("h2clip", [128, NT], F32))
    scr = es.enter_context(nc.sbuf_tensor("scr", [128, 64], F32))
    xin = [es.enter_context(nc.sbuf_tensor(f"xin{i}", [128, D], F32))
           for i in range(2)]

    consts = {}

    def creg(v):
        v = float(v)
        if v not in consts:
            i = len(consts)
            assert i < 16
            nc.vector.memset(cpool[:, i:i + 1], v)
            nc.const_aps.register(F32, v, cpool[:, i:i + 1])
            consts[v] = i

    for v in (LN_EPS, -0.5, 0.5, INVSQ, -C_INV, PI_2, -PI, PI,
              2 * PI, 0.0, 1.0, -1.0):
        creg(v)

    def vsp():
        sVSP.inc(nc.vector.memset(scr[:, 63:64], 0.0))
        nc.vector.wait_ge(sVSP.h, sVSP.n)

    vsp()
    sV.inc(nc.vector.memset(scr[:, 63:64], 0.0))
    pass
    with nc.sbuf_tensor("idf", [128, 128], F32) as idf, \
         nc.sbuf_tensor("pst", [128, 16], F32) as pst:
        sW.dinc(nc.sync.dma_start(out=idf[:, :], in_=t_id[:, :]))
        nc.gpsimd.iota(scr[:, 0:1], pattern=[[0, 1]], base=0,
                       channel_multiplier=1,
                       allow_small_or_imprecise_dtypes=True)
        sG.inc(nc.gpsimd.engine_nop())
        sW.wait(nc.vector)
        sV.inc(nc.vector.tensor_copy(out=ident[:, :], in_=idf[:, :]))

        xcons = [None, None]
        for T in range(KNT + 1):
            if T < KNT:
                sl = T % 2
                wmark(nc.sync, xcons[sl])
                SLX[sl].dinc(nc.sync.dma_start(
                    out=xin[sl][:, :], in_=t_x[T * 128:(T + 1) * 128, :]))
            if T == 0:
                continue
            Tp, sl = T - 1, (T - 1) % 2
            SLX[sl].wait(nc.vector)
            xt = xin[sl]
            nc.vector.bn_stats(pst[:, 0:6], xt[:, 0:512])
            nc.vector.bn_stats(pst[:, 6:12], xt[:, 512:1024])
            vsp()
            nc.vector.bn_aggr(pst[:, 12:14], pst[:, 0:12])
            vsp()
            sV.inc(nc.vector.tensor_copy(out=ln1_m[:, Tp:Tp + 1],
                                         in_=pst[:, 12:13]))
            xcons[sl] = sV.mark()
            sV.wait(nc.scalar)
            nc.scalar.activation(out=pst[:, 14:15], in_=pst[:, 13:14],
                                 func=AF.Ln, bias=LN_EPS, scale=1.0)
            sS.inc(nc.scalar.activation(out=ln1_rs[:, Tp:Tp + 1],
                                        in_=pst[:, 14:15],
                                        func=AF.Exp, scale=-0.5))
        sS.wait(nc.vector)
        nc.all_engine_barrier()
    if KPH <= 0:
        return nc

    def stage_bf16(dst, src_t, n_chunks, width, stg):
        for c in range(n_chunks):
            sV.wait(nc.sync)
            sW.dinc(nc.sync.dma_start(
                out=stg[:, 0:width], in_=src_t[c * 128:(c + 1) * 128, :]))
            sW.wait(nc.vector)
            sV.inc(nc.vector.tensor_copy(out=dst[:, c * width:(c + 1) * width],
                                         in_=stg[:, 0:width]))

    stKV = ExitStack()
    K_feat = stKV.enter_context(nc.sbuf_tensor("K_feat", [128, NCH * S], BF16))

    stA = ExitStack()
    inv_row = stA.enter_context(nc.sbuf_tensor("inv_row", [128, 512], F32))
    wq_bf = stA.enter_context(nc.sbuf_tensor("wq_bf", [128, NCH * D], BF16))
    wk_bf = stA.enter_context(nc.sbuf_tensor("wk_bf", [128, NCH * D], BF16))
    kin = stA.enter_context(nc.sbuf_tensor("kin", [128, 1024], F32))
    h_bf = [stA.enter_context(nc.sbuf_tensor(f"h_bf{i}", [128, D], BF16))
            for i in range(2)]
    h_feat = stA.enter_context(nc.sbuf_tensor("h_feat", [128, D], BF16))
    q_bf = [stA.enter_context(nc.sbuf_tensor(f"q_bf{i}", [128, D], BF16))
            for i in range(2)]
    k_bf = [stA.enter_context(nc.sbuf_tensor(f"k_bf{i}", [128, D], BF16))
            for i in range(2)]
    q_rot = stA.enter_context(nc.sbuf_tensor("q_rot", [128, D], BF16))
    k_rot = stA.enter_context(nc.sbuf_tensor("k_rot", [128, D], BF16))
    q_feat = stA.enter_context(nc.sbuf_tensor("q_feat", [128, D], BF16))
    angs = stA.enter_context(nc.sbuf_tensor("angs", [128, 512], F32))
    angc = stA.enter_context(nc.sbuf_tensor("angc", [128, 512], F32))
    cwa = stA.enter_context(nc.sbuf_tensor("cwa", [128, 512], F32))
    cwb = stA.enter_context(nc.sbuf_tensor("cwb", [128, 512], F32))
    cosb = [stA.enter_context(nc.sbuf_tensor(f"cosb{i}", [128, 512], BF16))
            for i in range(3)]
    sinb = [stA.enter_context(nc.sbuf_tensor(f"sinb{i}", [128, 512], BF16))
            for i in range(3)]
    sq_bf = stA.enter_context(nc.sbuf_tensor("sq_bf", [128, D], BF16))
    m14 = stA.enter_context(nc.sbuf_tensor("m14", [128, 2048], BF16))

    sG.inc(nc.gpsimd.iota(inv_row[:, :], pattern=[[1, 512]], base=0,
                          channel_multiplier=0,
                          allow_small_or_imprecise_dtypes=True))
    sG.wait(nc.scalar)
    sS.inc(nc.scalar.activation(out=inv_row[:, :], in_=inv_row[:, :],
                                func=AF.Exp, scale=-C_INV))
    pass
    stage_bf16(wq_bf, t_wqT, NCH, D, xin[0])
    stage_bf16(wk_bf, t_wkT, NCH, D, xin[0])
    sV.wait(nc.tensor)
    sV.wait(nc.scalar)
    sV.wait(nc.sync)

    paH = stA.enter_context(nc.psum_tensor("paH", [128, 1024], BF16))
    paQ = stA.enter_context(nc.psum_tensor("paQ", [128, 1024], F32))
    paK = stA.enter_context(nc.psum_tensor("paK", [128, 1024], F32))
    paT = stA.enter_context(nc.psum_tensor("paT", [128, 1024], BF16))
    paR = stA.enter_context(nc.psum_tensor("paR", [128, 1024], BF16))

    xcons = [None, None]
    kcons = None
    qfree = None
    evdone = None

    def p1_a1(T):
        nonlocal kcons
        sl = T % 2
        wmark(nc.sync, xcons[sl])
        SLX[sl].dinc(nc.sync.dma_start(
            out=xin[sl][:, :], in_=t_x[T * 128:(T + 1) * 128, :]))
        wmark(nc.sync, kcons)
        SLK[0].dinc(nc.sync.dma_start(
            out=kin[:, 0:512], in_=t_ks[T * 128:(T + 1) * 128, :]))
        SLK[0].dinc(nc.sync.dma_start(
            out=kin[:, 512:1024], in_=t_kc[T * 128:(T + 1) * 128, :]))
        SLX[sl].wait(nc.vector)
        sV.inc(nc.vector.tensor_scalar(
            out=h_bf[sl][:, :], in0=xin[sl][:, :],
            scalar1=ln1_m[:, T:T + 1], scalar2=ln1_rs[:, T:T + 1],
            op0=ALU.subtract, op1=ALU.mult))
        hmk[sl] = sV.mark()
        xcons[sl] = sV.mark()
        tc = 1 + 3 * sl
        nc.vector.tensor_scalar(out=scr[:, tc:tc + 1], in0=scr[:, 0:1],
                                scalar1=float(128 * T), scalar2=None,
                                op0=ALU.add)
        vsp()
        sV.inc(nc.vector.memset(scr[:, 63:64], 0.0))
        sV.wait(nc.scalar)
        nc.scalar.activation(out=angs[:, :], in_=inv_row[:, :], func=AF.Copy,
                             scale=scr[:, tc:tc + 1])
        sS.inc(nc.scalar.activation(out=angc[:, :], in_=inv_row[:, :],
                                    func=AF.Identity, scale=scr[:, tc:tc + 1],
                                    bias=PI_2))
        SLK[0].wait(nc.vector)
        sS.wait(nc.vector)
        nc.vector.scalar_tensor_tensor(out=cwa[:, :], in0=kin[:, 0:512],
                                       scalar=-_CW1, in1=angs[:, :],
                                       op0=ALU.mult, op1=ALU.add)
        nc.vector.scalar_tensor_tensor(out=cwb[:, :], in0=kin[:, 512:1024],
                                       scalar=-_CW1, in1=angc[:, :],
                                       op0=ALU.mult, op1=ALU.add)
        nc.vector.scalar_tensor_tensor(out=angs[:, :], in0=kin[:, 0:512],
                                       scalar=-_CW2, in1=cwa[:, :],
                                       op0=ALU.mult, op1=ALU.add)
        nc.vector.scalar_tensor_tensor(out=angc[:, :], in0=kin[:, 512:1024],
                                       scalar=-_CW2, in1=cwb[:, :],
                                       op0=ALU.mult, op1=ALU.add)
        nc.vector.scalar_tensor_tensor(out=cwa[:, :], in0=kin[:, 0:512],
                                       scalar=-_CW3, in1=angs[:, :],
                                       op0=ALU.mult, op1=ALU.add)
        sV.inc(nc.vector.scalar_tensor_tensor(
            out=cwb[:, :], in0=kin[:, 512:1024], scalar=-_CW3,
            in1=angc[:, :], op0=ALU.mult, op1=ALU.add))
        kcons = sV.mark()
        s3 = T % 3
        sV.wait(nc.scalar)
        wmark(nc.scalar, cs_cons[s3])
        nc.scalar.activation(out=angs[:, :], in_=cwa[:, :], func=AF.Relu,
                             scale=-1.0, bias=PI)
        nc.scalar.activation(out=angc[:, :], in_=cwb[:, :], func=AF.Relu,
                             scale=-1.0, bias=PI)
        nc.scalar.activation(out=cwa[:, :], in_=angs[:, :], func=AF.Relu,
                             scale=-1.0, bias=2 * PI)
        nc.scalar.activation(out=cwb[:, :], in_=angc[:, :], func=AF.Relu,
                             scale=-1.0, bias=2 * PI)
        nc.scalar.activation(out=sinb[s3][:, :], in_=cwa[:, :], func=AF.Sin,
                             bias=-PI, scale=1.0)
        sS.inc(nc.scalar.activation(out=cosb[s3][:, :], in_=cwb[:, :],
                                    func=AF.Sin, bias=-PI, scale=1.0))
        csmk[s3] = sS.mark()

    def pM_a1(T):
        nonlocal evdone
        sl = T % 2
        wmark(nc.tensor, hmk[sl])
        wmark(nc.tensor, evdone)
        wmark(nc.tensor, ropemk[sl])
        for c in range(NCH):
            i = nc.tensor.transpose(paH[:, c * 128:(c + 1) * 128],
                                    h_bf[sl][:, c * 128:(c + 1) * 128],
                                    ident[:, :])
        sT.inc(i)
        sT.wait(nc.scalar)
        sS.inc(nc.scalar.copy(out=h_feat[:, :], in_=paH[:, :]))
        sS.wait(nc.tensor)
        for c in range(NCH):
            lhs = h_feat[:, c * 128:(c + 1) * 128]
            nc.tensor.matmul(paQ[:, 0:512], lhs, wq_bf[:, c * D:c * D + 512],
                             start=(c == 0), stop=(c == NCH - 1))
            nc.tensor.matmul(paQ[:, 512:1024], lhs,
                             wq_bf[:, c * D + 512:(c + 1) * D],
                             start=(c == 0), stop=(c == NCH - 1))
        for c in range(NCH):
            lhs = h_feat[:, c * 128:(c + 1) * 128]
            nc.tensor.matmul(paK[:, 0:512], lhs, wk_bf[:, c * D:c * D + 512],
                             start=(c == 0), stop=(c == NCH - 1))
            i = nc.tensor.matmul(paK[:, 512:1024], lhs,
                                 wk_bf[:, c * D + 512:(c + 1) * D],
                                 start=(c == 0), stop=(c == NCH - 1))
        sT.inc(i)
        sT.wait(nc.scalar)
        nc.scalar.copy(out=q_bf[sl][:, :], in_=paQ[:, :])
        sS.inc(nc.scalar.copy(out=k_bf[sl][:, :], in_=paK[:, :]))
        qkmk[sl] = sS.mark()

    def pR_a1(T):
        nonlocal qfree, evdone, qkT
        sl = T % 2
        s3 = T % 3
        wmark(nc.vector, qkmk[sl])
        wmark(nc.vector, csmk[s3])
        wmark(nc.vector, qkT)
        for srcb, dst in ((q_bf[sl], q_rot), (k_bf[sl], k_rot)):
            xr, xi = srcb[:, 0:512], srcb[:, 512:1024]
            nc.vector.tensor_tensor(out=m14[:, 0:512], in0=xr,
                                    in1=cosb[s3][:, :], op=ALU.mult)
            nc.vector.tensor_tensor(out=m14[:, 512:1024], in0=xi,
                                    in1=sinb[s3][:, :], op=ALU.mult)
            nc.vector.tensor_tensor(out=m14[:, 1024:1536], in0=xr,
                                    in1=sinb[s3][:, :], op=ALU.mult)
            nc.vector.tensor_tensor(out=m14[:, 1536:2048], in0=xi,
                                    in1=cosb[s3][:, :], op=ALU.mult)
            nc.vector.tensor_tensor(out=dst[:, 0:512], in0=m14[:, 0:512],
                                    in1=m14[:, 512:1024], op=ALU.subtract)
            i = nc.vector.tensor_tensor(out=dst[:, 512:1024],
                                        in0=m14[:, 1024:1536],
                                        in1=m14[:, 1536:2048], op=ALU.add)
        sV.inc(i)
        ropemk[sl] = sV.mark()
        cs_cons[s3] = sV.mark()
        sV.wait(nc.scalar)
        nc.scalar.activation(out=sq_bf[:, :], in_=q_rot[:, :], func=AF.Square,
                             accum_out=q2s[:, T:T + 1])
        sS.inc(nc.scalar.activation(out=sq_bf[:, :], in_=k_rot[:, :],
                                    func=AF.Square,
                                    accum_out=scr[:, 3:4]))
        sS.wait(nc.vector)
        if T == 0:
            nc.vector.tensor_copy(out=scr[:, 2:3], in_=scr[:, 3:4])
        else:
            nc.vector.tensor_tensor(out=scr[:, 2:3], in0=scr[:, 2:3],
                                    in1=scr[:, 3:4], op=ALU.max)
        vsp()
        sV.inc(nc.vector.memset(scr[:, 63:64], 0.0))
        sV.wait(nc.tensor)
        wmark(nc.tensor, evdone)
        for c in range(NCH):
            i = nc.tensor.transpose(paT[:, c * 128:(c + 1) * 128],
                                    q_rot[:, c * 128:(c + 1) * 128],
                                    ident[:, :])
        sT.inc(i)
        sT.wait(nc.scalar)
        wmark(nc.scalar, qfree)
        sS.inc(nc.scalar.copy(out=q_feat[:, :], in_=paT[:, :]))
        sS.wait(nc.tensor)
        for c in range(NCH):
            i = nc.tensor.transpose(paR[:, c * 128:(c + 1) * 128],
                                    k_rot[:, c * 128:(c + 1) * 128],
                                    ident[:, :])
        sT.inc(i)
        qkT = sT.mark()
        sT.wait(nc.scalar)
        kf = K_feat[:, :].rearrange("p (c k) -> p c k", c=NCH)
        pa = paR[:, :].rearrange("p (c k) -> p c k", c=NCH)
        sS.inc(nc.scalar.copy(out=kf[:, :, T * 128:(T + 1) * 128], in_=pa))
        evdone = sS.mark()
        sS.wait(nc.gpsimd)
        sQF.dinc(nc.gpsimd.dma_start(out=t_qfs[T, :, :], in_=q_feat[:, :]))
        qfree = sQF.mark()

    hmk = [None, None]
    csmk = [None, None, None]
    cs_cons = [None, None, None]
    qkmk = [None, None]
    ropemk = [None, None]
    qkT = None
    for T in range(KNT + 2):
        if T < KNT:
            p1_a1(T)
        if 1 <= T <= KNT:
            pM_a1(T - 1)
        if 2 <= T <= KNT + 1:
            pR_a1(T - 2)
    sQF.wait(nc.gpsimd)
    nc.all_engine_barrier()
    stA.close()
    if KPH <= 1:
        return nc

    stKV2 = ExitStack()
    V_tok = stKV2.enter_context(nc.sbuf_tensor("V_tok", [128, NT * D], BF16))
    stA2 = ExitStack()
    wv_bf = stA2.enter_context(nc.sbuf_tensor("wv_bf", [128, NCH * D], BF16))
    h2bf = stA2.enter_context(nc.sbuf_tensor("h2bf", [128, D], BF16))
    h2ft = stA2.enter_context(nc.sbuf_tensor("h2ft", [128, D], BF16))
    stage_bf16(wv_bf, t_wvT, NCH, D, xin[0])
    sV.wait(nc.sync)
    sV.wait(nc.tensor)

    paH2 = stA2.enter_context(nc.psum_tensor("paH2", [128, 1024], BF16))
    paV = stA2.enter_context(nc.psum_tensor("paV", [128, 1024], F32))
    xcons = [None, None]
    evdone = None
    hcons = None
    for T in range(KNT + 1):
        if T < KNT:
            sl = T % 2
            wmark(nc.sync, xcons[sl])
            SLX[sl].dinc(nc.sync.dma_start(
                out=xin[sl][:, :], in_=t_x[T * 128:(T + 1) * 128, :]))
        if T == 0:
            continue
        Tp, sl = T - 1, (T - 1) % 2
        SLX[sl].wait(nc.vector)
        wmark(nc.vector, hcons)
        sV.inc(nc.vector.tensor_scalar(
            out=h2bf[:, :], in0=xin[sl][:, :],
            scalar1=ln1_m[:, Tp:Tp + 1], scalar2=ln1_rs[:, Tp:Tp + 1],
            op0=ALU.subtract, op1=ALU.mult))
        xcons[sl] = sV.mark()
        sV.wait(nc.tensor)
        wmark(nc.tensor, evdone)
        for c in range(NCH):
            i = nc.tensor.transpose(paH2[:, c * 128:(c + 1) * 128],
                                    h2bf[:, c * 128:(c + 1) * 128],
                                    ident[:, :])
        sT.inc(i)
        hcons = sT.mark()
        sT.wait(nc.scalar)
        sS.inc(nc.scalar.copy(out=h2ft[:, :], in_=paH2[:, :]))
        sS.wait(nc.tensor)
        for c in range(NCH):
            lhs = h2ft[:, c * 128:(c + 1) * 128]
            nc.tensor.matmul(paV[:, 0:512], lhs, wv_bf[:, c * D:c * D + 512],
                             start=(c == 0), stop=(c == NCH - 1))
            i = nc.tensor.matmul(paV[:, 512:1024], lhs,
                                 wv_bf[:, c * D + 512:(c + 1) * D],
                                 start=(c == 0), stop=(c == NCH - 1))
        sT.inc(i)
        sT.wait(nc.scalar)
        sS.inc(nc.scalar.copy(out=V_tok[:, Tp * D:(Tp + 1) * D],
                              in_=paV[:, :]))
        evdone = sS.mark()
    sS.wait(nc.vector)
    nc.all_engine_barrier()
    if tap == "B0":
        sFIN.dinc(nc.gpsimd.dma_start(out=t_dK[:, :], in_=K_feat[:, :]))
        sFIN.dinc(nc.gpsimd.dma_start(out=t_dV[:, :], in_=V_tok[:, :]))
        sFIN.wait(nc.gpsimd)
        nc.all_engine_barrier()
    stA2.close()
    if KPH <= 2:
        return nc

    stB = ExitStack()
    wo_bf = stB.enter_context(nc.sbuf_tensor("wo_bf", [128, NCH * D], BF16))
    qf_in = [stB.enter_context(nc.sbuf_tensor(f"qf{i}", [128, D], BF16))
             for i in range(2)]
    probs = stB.enter_context(nc.sbuf_tensor("probs", [128, S], BF16))
    probsT = stB.enter_context(nc.sbuf_tensor("probsT", [128, S], BF16))
    av_bf = stB.enter_context(nc.sbuf_tensor("av_bf", [128, D], BF16))
    of_bf = stB.enter_context(nc.sbuf_tensor("of_bf", [128, D], BF16))
    y1 = stB.enter_context(nc.sbuf_tensor("y1", [128, D], F32))
    x1t = stB.enter_context(nc.sbuf_tensor("x1t", [128, D], F32))
    wmst = stB.enter_context(nc.sbuf_tensor("wmst", [128, 4096], F32))
    xqi = stB.enter_context(nc.sbuf_tensor("xqi", [128, 512], I32))
    Xb2 = stB.enter_context(nc.sbuf_tensor("Xb2", [128, D], BF16))
    bst = stB.enter_context(nc.sbuf_tensor("bst", [128, 32], F32))

    pass
    stage_bf16(wo_bf, t_woT, NCH, D, xin[0])
    sV.wait(nc.sync)

    with nc.sbuf_tensor("bc", [128, 128], BF16) as bc, \
         nc.sbuf_tensor("onr", [1, 128], F32) as onr, \
         nc.psum_tensor("pbx", [128, 128], BF16) as pbx, \
         nc.psum_tensor("pbb", [128, 1], F32) as pbb:
        nc.vector.memset(bc[:, :], 0.0)
        nc.vector.memset(onr[:, :], 1.0)
        vsp()
        sV.inc(nc.vector.memset(scr[:, 63:64], 0.0))
        sV.wait(nc.scalar)
        sS.inc(nc.scalar.copy(out=bc[:, 0:1], in_=scr[:, 2:3]))
        sS.wait(nc.tensor)
        sT.inc(nc.tensor.transpose(pbx[:, :], bc[:, :], ident[:, :]))
        sT.wait(nc.vector)
        sV.inc(nc.vector.reduce_max(out=scr[0:1, 9:10], in_=pbx[0:1, :],
                                    axis=AX.X))
        sV.wait(nc.scalar)
        nc.scalar.activation(out=scr[0:1, 10:11], in_=scr[0:1, 9:10],
                             func=AF.Ln)
        sS.inc(nc.scalar.activation(out=scr[0:1, 11:12], in_=scr[0:1, 10:11],
                                    func=AF.Exp, scale=0.5))
        sS.wait(nc.vector)
        sV.inc(nc.vector.tensor_scalar(out=scr[0:1, 12:13],
                                       in0=scr[0:1, 11:12],
                                       scalar1=-INVSQ, scalar2=None,
                                       op0=ALU.mult))
        sV.wait(nc.tensor)
        sT.inc(nc.tensor.matmul(pbb[:, :], onr[:, :], scr[0:1, 12:13],
                                start=True, stop=True))
        sT.wait(nc.vector)
        sV.inc(nc.vector.tensor_copy(out=scr[:, 13:14], in_=pbb[:, :]))

    wm_chunks = []
    for tt, col in ((t_wgT, 16), (t_wuT, 17), (t_wdT, 18)):
        rows, cols = (D, INNER) if tt is not t_wdT else (INNER, D)
        for r0 in range(0, rows, 128):
            for c0 in range(0, cols, 2048):
                w_ = min(2048, cols - c0)
                wm_chunks.append((tt[r0:r0 + 128, c0:c0 + w_], col, w_))
    nc.vector.memset(scr[:, 16:19], 0.0)
    vsp()
    sV.inc(nc.vector.memset(scr[:, 63:64], 0.0))

    psS4 = stB.enter_context(nc.psum_tensor("psS4", [128, 2048], F32))
    pbT0 = stB.enter_context(nc.psum_tensor("pbT0", [128, 1024], BF16))
    pbT1 = stB.enter_context(nc.psum_tensor("pbT1", [128, 1024], BF16))
    pbAP = stB.enter_context(nc.psum_tensor("pbAP", [128, 1024], F32))

    qcons = [None, None]
    xcons = [None, None]
    x1free = None
    expdone = None
    wmfree = [None, None]
    ebmk = [None, None]
    summk = [None]
    xffree = [None]

    def p1_b(T):
        sl = T % 2
        wmark(nc.sync, qcons[sl])
        SLK[sl].dinc(nc.sync.dma_start(out=qf_in[sl][:, :],
                                       in_=t_qfs[T, :, :]))
        wmark(nc.sync, xcons[sl])
        SLX[sl].dinc(nc.sync.dma_start(
            out=xin[sl][:, :], in_=t_x[T * 128:(T + 1) * 128, :]))
        sS.wait(nc.scalar)
        nc.scalar.activation(out=scr[:, 20:21], in_=q2s[:, T:T + 1],
                             func=AF.Ln)
        sS.inc(nc.scalar.activation(out=scr[:, 21:22], in_=scr[:, 20:21],
                                    func=AF.Exp, scale=0.5))
        sS.wait(nc.vector)
        sV.inc(nc.vector.tensor_tensor(out=bst[:, 24 + sl:25 + sl],
                                       in0=scr[:, 21:22], in1=scr[:, 13:14],
                                       op=ALU.mult))
        ebmk[sl] = sV.mark()

    def p2_b(T):
        nonlocal x1free, expdone
        sl = T % 2
        for j in range((len(wm_chunks) * T) // NT,
                       (len(wm_chunks) * (T + 1)) // NT):
            ap, col, w_ = wm_chunks[j]
            wmark(nc.gpsimd, wmfree[j % 2])
            sWM[j % 2].dinc(nc.gpsimd.dma_start(
                out=wmst[:, 2048 * (j % 2):2048 * (j % 2) + w_], in_=ap))
        SLK[sl].wait(nc.tensor)
        qf = qf_in[sl]
        for half in range(2):
            wmark(nc.tensor, expdone)
            for kb2 in range(4):
                kb = half * 4 + kb2
                for c in range(NCH):
                    i = nc.tensor.matmul(
                        psS4[:, kb2 * 512:(kb2 + 1) * 512],
                        qf[:, c * 128:(c + 1) * 128],
                        K_feat[:, c * S + kb * 512:c * S + (kb + 1) * 512],
                        start=(c == 0), stop=(c == NCH - 1))
            sT.inc(i)
            if half == 1:
                qcons[sl] = sT.mark()
            sT.wait(nc.scalar)
            if half == 0:
                wmark(nc.scalar, ebmk[sl])
                wmark(nc.scalar, summk[0])
            for kb2 in range(4):
                kb = half * 4 + kb2
                i = nc.scalar.activation(
                    out=probs[:, kb * 512:(kb + 1) * 512],
                    in_=psS4[:, kb2 * 512:(kb2 + 1) * 512], func=AF.Exp,
                    bias=bst[:, 24 + sl:25 + sl], scale=INVSQ,
                    accum_out=bst[:, kb:kb + 1])
            sS.inc(i)
            expdone = sS.mark()
        sS.wait(nc.vector)
        nc.vector.reduce_sum(out=bst[:, 8:9], in_=bst[:, 0:8], axis=AX.X)
        vsp()
        nc.vector.reciprocal(out=bst[:, 9:10], in_=bst[:, 8:9])
        vsp()
        sV.inc(nc.vector.memset(scr[:, 63:64], 0.0))
        if tap == "B0" and T == 0:
            sS.wait(nc.gpsimd)
            sV.wait(nc.gpsimd)
            sFIN.dinc(nc.gpsimd.dma_start(out=t_dP[:, :], in_=probs[:, :]))
            sFIN.dinc(nc.gpsimd.dma_start(out=t_dB[:, :], in_=bst[:, 0:10]))
        evmk = [None, None]
        for grp in range(4):
            pb = (pbT0, pbT1)[grp % 2]
            wmark(nc.tensor, evmk[grp % 2])
            for cc in range(8):
                c = grp * 8 + cc
                i = nc.tensor.transpose(pb[:, cc * 128:(cc + 1) * 128],
                                        probs[:, c * 128:(c + 1) * 128],
                                        ident[:, :])
            sT.inc(i)
            sT.wait(nc.scalar)
            sS.inc(nc.scalar.copy(out=probsT[:, grp * 1024:(grp + 1) * 1024],
                                  in_=pb[:, :]))
            evmk[grp % 2] = sS.mark()
        sS.wait(nc.tensor)
        for c in range(NO):
            nc.tensor.matmul(pbAP[:, 0:512],
                             probsT[:, c * 128:(c + 1) * 128],
                             V_tok[:, c * 1024:c * 1024 + 512],
                             start=(c == 0), stop=(c == NO - 1))
            i = nc.tensor.matmul(pbAP[:, 512:1024],
                                 probsT[:, c * 128:(c + 1) * 128],
                                 V_tok[:, c * 1024 + 512:(c + 1) * 1024],
                                 start=(c == 0), stop=(c == NO - 1))
        sT.inc(i)
        sT.wait(nc.scalar)
        sS.inc(nc.scalar.copy(out=av_bf[:, :], in_=pbAP[:, :]))
        sS.wait(nc.tensor)
        for c in range(NCH):
            i = nc.tensor.transpose(pbT0[:, c * 128:(c + 1) * 128],
                                    av_bf[:, c * 128:(c + 1) * 128],
                                    ident[:, :])
        sT.inc(i)
        sT.wait(nc.scalar)
        sS.inc(nc.scalar.copy(out=of_bf[:, :], in_=pbT0[:, :]))
        sS.wait(nc.tensor)
        for c in range(NCH):
            lhs = of_bf[:, c * 128:(c + 1) * 128]
            nc.tensor.matmul(pbAP[:, 0:512], lhs,
                             wo_bf[:, c * D:c * D + 512],
                             start=(c == 0), stop=(c == NCH - 1))
            i = nc.tensor.matmul(pbAP[:, 512:1024], lhs,
                                 wo_bf[:, c * D + 512:(c + 1) * D],
                                 start=(c == 0), stop=(c == NCH - 1))
        sT.inc(i)
        sT.wait(nc.scalar)
        sS.inc(nc.scalar.activation(out=y1[:, :], in_=pbAP[:, :],
                                    func=AF.Identity,
                                    scale=bst[:, 9:10]))
        if tap == "B0" and T == 0:
            sS.wait(nc.gpsimd)
            sFIN.dinc(nc.gpsimd.dma_start(out=t_dPT[:, :], in_=probsT[:, :]))
            sFIN.dinc(nc.gpsimd.dma_start(out=t_dAV[:, :], in_=av_bf[:, :]))
            sFIN.dinc(nc.gpsimd.dma_start(out=t_dY[:, :], in_=y1[:, :]))
            sFIN.dinc(nc.gpsimd.dma_start(out=t_dOF[:, :], in_=of_bf[:, :]))
            sFIN.dinc(nc.gpsimd.dma_start(out=t_dWO[:, :], in_=wo_bf[:, :]))
            sFIN.wait(nc.vector)
        sS.wait(nc.vector)
        nc.vector.bn_stats(bst[:, 10:16], y1[:, 0:512])
        nc.vector.bn_stats(bst[:, 16:22], y1[:, 512:1024])
        vsp()
        nc.vector.bn_aggr(bst[:, 22:24], bst[:, 10:22])
        vsp()
        sV.inc(nc.vector.memset(scr[:, 63:64], 0.0))
        sV.wait(nc.scalar)
        nc.scalar.activation(out=scr[:, 26:27], in_=bst[:, 23:24], func=AF.Ln,
                             bias=LN_EPS, scale=1.0)
        sS.inc(nc.scalar.activation(out=scr[:, 27:28], in_=scr[:, 26:27],
                                    func=AF.Exp, scale=-0.5))
        sS.wait(nc.vector)
        SLX[sl].wait(nc.vector)
        wmark(nc.vector, x1free)
        nc.vector.tensor_scalar(out=y1[:, :], in0=y1[:, :],
                                scalar1=bst[:, 22:23], scalar2=scr[:, 27:28],
                                op0=ALU.subtract, op1=ALU.mult)
        vsp()
        sV.inc(nc.vector.tensor_tensor(out=x1t[:, :], in0=y1[:, :],
                                       in1=xin[sl][:, :], op=ALU.add))
        xcons[sl] = sV.mark()
        nc.vector.bn_stats(bst[:, 10:16], x1t[:, 0:512])
        nc.vector.bn_stats(bst[:, 16:22], x1t[:, 512:1024])
        vsp()
        nc.vector.bn_aggr(bst[:, 22:24], bst[:, 10:22])
        vsp()
        sV.inc(nc.vector.tensor_copy(out=ln2_m[:, T:T + 1],
                                     in_=bst[:, 22:23]))
        sV.wait(nc.scalar)
        nc.scalar.activation(out=scr[:, 26:27], in_=bst[:, 23:24], func=AF.Ln,
                             bias=LN_EPS, scale=1.0)
        sS.inc(nc.scalar.activation(out=ln2_rs[:, T:T + 1],
                                    in_=scr[:, 26:27], func=AF.Exp,
                                    scale=-0.5))
        sV.wait(nc.gpsimd)
        sX1O.dinc(nc.gpsimd.dma_start(
            out=t_x1s[T * 128:(T + 1) * 128, :], in_=x1t[:, :]))
        x1free = sX1O.mark()
        h2B = probs[:, 0:1024]
        xqy = probs[:, 1024:1536]
        xqb = probs[:, 1536:2048]
        xqc = probs[:, 2048:2560]
        sS.wait(nc.vector)
        nc.vector.tensor_scalar(out=h2B, in0=x1t[:, :],
                                scalar1=ln2_m[:, T:T + 1],
                                scalar2=ln2_rs[:, T:T + 1],
                                op0=ALU.subtract, op1=ALU.mult)
        vsp()
        nc.vector.tensor_reduce(out=scr[:, 33:34], in_=h2B, axis=AX.X,
                                op=ALU.max, apply_absolute_value=True)
        vsp()
        nc.vector.tensor_scalar(out=h2clip[:, T:T + 1], in0=scr[:, 33:34],
                                scalar1=1e-5, scalar2=None, op0=ALU.max)
        vsp()
        nc.vector.reciprocal(out=scr[:, 34:35], in_=h2clip[:, T:T + 1])
        vsp()
        sV.inc(nc.vector.tensor_scalar(out=scr[:, 35:36], in0=scr[:, 34:35],
                                       scalar1=127.0, scalar2=None,
                                       op0=ALU.mult))
        sV.wait(nc.scalar)
        wmark(nc.vector, xffree[0])
        for b in range(2):
            if b == 1:
                sV.wait(nc.scalar)
            hb = h2B[:, b * 512:(b + 1) * 512]
            nc.scalar.activation(out=xqi[:, :], in_=hb, func=AF.Copy,
                                 scale=scr[:, 35:36], bias=-0.5)
            nc.scalar.activation(out=xqy, in_=hb, func=AF.Copy,
                                 scale=scr[:, 35:36])
            sS.inc(nc.scalar.activation(out=xqb, in_=xqi[:, :],
                                        func=AF.Copy, bias=0.5))
            sS.wait(nc.vector)
            nc.vector.tensor_tensor(out=xqc, in0=xqy,
                                    in1=xqb, op=ALU.is_gt)
            sV.inc(nc.vector.scalar_tensor_tensor(
                out=Xb2[:, b * 512:(b + 1) * 512], in0=xqc,
                scalar=-0.5, in1=xqb, op0=ALU.add, op1=ALU.add))
        summk[0] = sV.mark()
        sV.wait(nc.tensor)
        for c in range(NCH):
            i = nc.tensor.transpose(pbT0[:, c * 128:(c + 1) * 128],
                                    Xb2[:, c * 128:(c + 1) * 128],
                                    ident[:, :])
        sT.inc(i)
        sT.wait(nc.scalar)
        sS.inc(nc.scalar.copy(out=Xb2[:, :], in_=pbT0[:, :]))
        sS.wait(nc.gpsimd)
        sQF.dinc(nc.gpsimd.dma_start(out=t_xfs[T, :, :], in_=Xb2[:, :]))
        xffree[0] = sQF.mark()
        lo = (len(wm_chunks) * T) // NT
        hi = (len(wm_chunks) * (T + 1)) // NT
        for j in range(lo, hi):
            ap, col, w_ = wm_chunks[j]
            sWM[j % 2].wait(nc.vector)
            nc.vector.tensor_reduce(out=scr[:, 30:31],
                                    in_=wmst[:, 2048 * (j % 2):
                                             2048 * (j % 2) + w_],
                                    axis=AX.X, op=ALU.add,
                                    apply_absolute_value=True)
            vsp()
            sV.inc(nc.vector.tensor_tensor(out=scr[:, col:col + 1],
                                           in0=scr[:, col:col + 1],
                                           in1=scr[:, 30:31], op=ALU.add))
            wmfree[j % 2] = sV.mark()

    for T in range(KNT + 1):
        if T < KNT:
            p1_b(T)
        if T >= 1:
            p2_b(T - 1)
    sX1O.wait(nc.gpsimd)
    sQF.wait(nc.gpsimd)
    nc.all_engine_barrier()
    stB.close()
    stKV2.close()
    stKV.close()
    if KPH <= 3:
        return nc

    stTH = ExitStack()
    thr_g = stTH.enter_context(nc.sbuf_tensor("thr_g", [128, 12], F32))
    with nc.sbuf_tensor("onc", [128, 1], F32) as onc, \
         nc.sbuf_tensor("onr2", [1, 128], F32) as onr2, \
         nc.psum_tensor("pth", [128, 8], F32) as pth:
        nc.vector.memset(onc[:, :], 1.0)
        nc.vector.memset(onr2[:, :], 1.0)
        vsp()
        sV.inc(nc.vector.memset(scr[:, 63:64], 0.0))
        sV.wait(nc.tensor)
        for i_, col in enumerate((16, 17, 18)):
            i = nc.tensor.matmul(pth[0:1, i_:i_ + 1], scr[:, col:col + 1],
                                 onc[:, :], start=True, stop=True)
        sT.inc(i)
        sT.wait(nc.vector)
        sV.inc(nc.vector.tensor_scalar(out=scr[0:1, 32:35], in0=pth[0:1, 0:3],
                                       scalar1=1.0 / (D * INNER),
                                       scalar2=1e-5, op0=ALU.mult,
                                       op1=ALU.max))
        sV.wait(nc.tensor)
        sT.inc(nc.tensor.matmul(pth[:, 3:6], onr2[:, :], scr[0:1, 32:35],
                                start=True, stop=True))
        sT.wait(nc.vector)
        nc.vector.tensor_copy(out=thr_g[:, 3:6], in_=pth[:, 3:6])
        vsp()
        nc.vector.tensor_scalar(out=thr_g[:, 0:3], in0=thr_g[:, 3:6],
                                scalar1=0.5, scalar2=None, op0=ALU.mult)
        nc.vector.tensor_scalar(out=thr_g[:, 6:9], in0=thr_g[:, 3:6],
                                scalar1=-0.5, scalar2=None, op0=ALU.mult)
        vsp()
        sV.inc(nc.vector.memset(scr[:, 63:64], 0.0))
    sV.wait(nc.tensor)
    sV.wait(nc.scalar)
    sV.wait(nc.sync)
    sV.wait(nc.gpsimd)

    stF1 = ExitStack()
    wg_t = stF1.enter_context(nc.sbuf_tensor("wg_t", [128, NCH * INNER],
                                             BF16))
    wu_t = stF1.enter_context(nc.sbuf_tensor("wu_t", [128, NCH * INNER],
                                             BF16))
    pass

    def ternarize(wtst, tcmp, jobs):
        free = [None, None]
        k = 0
        for dst, src_t, tc, ntc, n_chunks, width in jobs:
            for c in range(n_chunks):
                for c0 in range(0, width, 2048):
                    s = k % 2
                    wmark(nc.sync, free[s])
                    sTD[s].dinc(nc.sync.dma_start(
                        out=wtst[s][:, :],
                        in_=src_t[c * 128:(c + 1) * 128, c0:c0 + 2048]))
                    sTD[s].wait(nc.vector)
                    nc.vector.tensor_scalar(out=tcmp[:, 0:2048],
                                            in0=wtst[s][:, :],
                                            scalar1=thr_g[:, tc:tc + 1],
                                            scalar2=None, op0=ALU.is_gt)
                    nc.vector.tensor_scalar(out=tcmp[:, 2048:4096],
                                            in0=wtst[s][:, :],
                                            scalar1=thr_g[:, ntc:ntc + 1],
                                            scalar2=None, op0=ALU.is_lt)
                    sV.inc(nc.vector.tensor_tensor(
                        out=dst[:, c * width + c0:c * width + c0 + 2048],
                        in0=tcmp[:, 0:2048], in1=tcmp[:, 2048:4096],
                        op=ALU.subtract))
                    free[s] = sV.mark()
                    k += 1

    with nc.sbuf_tensor("wtstA", [128, 2048], F32) as wtstA, \
         nc.sbuf_tensor("wtstB", [128, 2048], F32) as wtstB, \
         nc.sbuf_tensor("tcmpA", [128, 4096], F32) as tcmpA:
        ternarize([wtstA, wtstB], tcmpA,
                  [(wg_t, t_wgT, 0, 6, NCH, INNER),
                   (wu_t, t_wuT, 1, 7, NCH, INNER)])
    sV.wait(nc.tensor)
    sV.wait(nc.scalar)

    graw = stF1.enter_context(nc.sbuf_tensor("graw", [128, INNER], F32))
    silu_b = [stF1.enter_context(nc.sbuf_tensor(f"silu_b{i}", [128, 512],
                                                 F32)) for i in range(2)]
    qq = [stF1.enter_context(nc.sbuf_tensor(f"qq{i}", [128, 512], F32))
          for i in range(4)]
    qi = stF1.enter_context(nc.sbuf_tensor("qi", [128, 512], I32))
    qc = [stF1.enter_context(nc.sbuf_tensor(f"qc{i}", [128, 512], BF16))
          for i in range(2)]
    Gq = stF1.enter_context(nc.sbuf_tensor("Gq", [128, INNER], BF16))
    Xft2 = [stF1.enter_context(nc.sbuf_tensor(f"Xft2{i}", [128, D], BF16))
            for i in range(2)]
    gqf = [stF1.enter_context(nc.sbuf_tensor("gqf0", [128, INNER], BF16))] * 2
    mx8 = stF1.enter_context(nc.sbuf_tensor("mx8", [128, 16], F32))

    pg = [stF1.enter_context(nc.psum_tensor(f"pg{i}", [128, 512], F32))
          for i in range(2)]
    pu = [stF1.enter_context(nc.psum_tensor(f"pu{i}", [128, 512], F32))
          for i in range(2)]
    pgT = [stF1.enter_context(nc.psum_tensor(f"pgT{i}", [128, 1024], BF16))
           for i in range(2)]

    gqfree = [None, None]
    egmk = [None, None]
    gq_mk = None

    def p1_f1(T):
        sl = T % 2
        base = 40 - 8 * sl
        wmark(nc.sync, pamk[sl])
        SLX[sl].dinc(nc.sync.dma_start(out=Xft2[sl][:, :],
                                       in_=t_xfs[T, :, :]))
        nc.vector.tensor_tensor(out=scr[:, base + 4:base + 5],
                                in0=h2clip[:, T:T + 1],
                                in1=thr_g[:, 3:4], op=ALU.mult)
        nc.vector.tensor_tensor(out=scr[:, base + 5:base + 6],
                                in0=h2clip[:, T:T + 1],
                                in1=thr_g[:, 4:5], op=ALU.mult)
        vsp()
        sV.inc(nc.vector.tensor_scalar(out=scr[:, base + 6:base + 8],
                                       in0=scr[:, base + 4:base + 6],
                                       scalar1=1.0 / 127.0, scalar2=None,
                                       op0=ALU.mult))
        egmk[sl] = sV.mark()

    def pA_f1(T):
        sl = T % 2
        base = 40 - 8 * sl
        Xft = Xft2[sl]
        SLX[sl].wait(nc.tensor)
        wmark(nc.scalar, egmk[sl])
        evmk = [None, None]
        for ob in range(8):
            g_, u_ = pg[ob % 2], pu[ob % 2]
            wmark(nc.tensor, evmk[ob % 2])
            for c in range(NCH):
                nc.tensor.matmul(g_[:, :], Xft[:, c * 128:(c + 1) * 128],
                                 wg_t[:, c * INNER + ob * 512:
                                      c * INNER + (ob + 1) * 512],
                                 start=(c == 0), stop=(c == NCH - 1))
            for c in range(NCH):
                i = nc.tensor.matmul(u_[:, :], Xft[:, c * 128:(c + 1) * 128],
                                     wu_t[:, c * INNER + ob * 512:
                                          c * INNER + (ob + 1) * 512],
                                     start=(c == 0), stop=(c == NCH - 1))
            sT.inc(i)
            if ob == 7:
                pamk[sl] = sT.mark()
            sT.wait(nc.scalar)
            sS.inc(nc.scalar.activation(out=silu_b[ob % 2][:, :],
                                        in_=g_[:, :], func=AF.Silu,
                                        scale=scr[:, base + 6:base + 7]))
            sS.wait(nc.vector)
            nc.vector.tensor_tensor(out=graw[:, ob * 512:(ob + 1) * 512],
                                    in0=silu_b[ob % 2][:, :], in1=u_[:, :],
                                    op=ALU.mult)
            vsp()
            sV.inc(nc.vector.tensor_reduce(
                out=mx8[:, ob:ob + 1], in_=graw[:, ob * 512:(ob + 1) * 512],
                axis=AX.X, op=ALU.max, apply_absolute_value=True))
            evmk[ob % 2] = sV.mark()
        vsp()
        nc.vector.reduce_max(out=scr[:, 48:49], in_=mx8[:, 0:8], axis=AX.X)
        vsp()
        nc.vector.tensor_tensor(out=scr[:, 49:50], in0=scr[:, 48:49],
                                in1=scr[:, base + 7:base + 8], op=ALU.mult)
        vsp()
        nc.vector.tensor_scalar(out=scr[:, 50:51], in0=scr[:, 49:50],
                                scalar1=1e-5, scalar2=None, op0=ALU.max)
        vsp()
        nc.vector.reciprocal(out=scr[:, 51:52], in_=scr[:, 50:51])
        nc.vector.tensor_tensor(out=scr[:, 54:55], in0=scr[:, 50:51],
                                in1=thr_g[:, 5:6], op=ALU.mult)
        vsp()
        nc.vector.tensor_scalar(out=scr[:, 52:53], in0=scr[:, 51:52],
                                scalar1=127.0, scalar2=None, op0=ALU.mult)
        nc.vector.tensor_scalar(out=ed_buf[:, T:T + 1], in0=scr[:, 54:55],
                                scalar1=1.0 / 127.0, scalar2=None,
                                op0=ALU.mult)
        vsp()
        sV.inc(nc.vector.tensor_tensor(out=scr[:, 53:54], in0=scr[:, 52:53],
                                       in1=scr[:, base + 7:base + 8],
                                       op=ALU.mult))

    def pC_f1(T):
        nonlocal gq_mk
        sV.wait(nc.scalar)
        wmark(nc.vector, gqT_mk[0])
        gdone = [None, None]
        for ob in range(8):
            gb = graw[:, ob * 512:(ob + 1) * 512]
            r = ob % 2
            wmark(nc.scalar, gdone[r])
            nc.scalar.activation(out=qi[:, :], in_=gb, func=AF.Copy,
                                 scale=scr[:, 53:54], bias=-0.5)
            nc.scalar.activation(out=qq[2 * r][:, :], in_=gb, func=AF.Copy,
                                 scale=scr[:, 53:54])
            sS.inc(nc.scalar.activation(out=qq[2 * r + 1][:, :],
                                        in_=qi[:, :], func=AF.Copy,
                                        bias=0.5))
            sS.wait(nc.vector)
            nc.vector.tensor_tensor(out=qc[r][:, :], in0=qq[2 * r][:, :],
                                    in1=qq[2 * r + 1][:, :], op=ALU.is_gt)
            vsp()
            sV.inc(nc.vector.scalar_tensor_tensor(
                out=Gq[:, ob * 512:(ob + 1) * 512], in0=qc[r][:, :],
                scalar=-0.5, in1=qq[2 * r + 1][:, :],
                op0=ALU.add, op1=ALU.add))
            gdone[r] = sV.mark()
        gq_mk = sV.mark()

    def pB_f1(T):
        sl = T % 2
        g_out = gqf[sl]
        wmark(nc.tensor, gq_mk_t[T])
        for grp in range(4):
            pb = pgT[grp % 2]
            wmark(nc.tensor, evmk2[grp % 2])
            for cc2 in range(8):
                c = grp * 8 + cc2
                i = nc.tensor.transpose(pb[:, cc2 * 128:(cc2 + 1) * 128],
                                        Gq[:, c * 128:(c + 1) * 128],
                                        ident[:, :])
            sT.inc(i)
            if grp == 3:
                gqT_mk[0] = sT.mark()
            sT.wait(nc.scalar)
            wmark(nc.scalar, gqfree[sl])
            sS.inc(nc.scalar.copy(
                out=g_out[:, grp * 1024:(grp + 1) * 1024], in_=pb[:, :]))
            evmk2[grp % 2] = sS.mark()
            sS.wait(nc.gpsimd)
            sGQ.dinc(nc.gpsimd.dma_start(
                out=t_gqs[T, :, grp * 1024:(grp + 1) * 1024],
                in_=g_out[:, grp * 1024:(grp + 1) * 1024]))
        gqfree[sl] = sGQ.mark()

    gq_mk_t = {}
    pamk = [None, None]
    gqT_mk = [None]
    evmk2 = [None, None]
    for T in range(KNT + 2):
        if T < KNT:
            p1_f1(T)
        if 1 <= T <= KNT:
            pA_f1(T - 1)
        if 2 <= T <= KNT + 1:
            pB_f1(T - 2)
        if 1 <= T <= KNT:
            pC_f1(T - 1)
            gq_mk_t[T - 1] = gq_mk
    sGQ.wait(nc.gpsimd)
    nc.all_engine_barrier()
    stF1.close()
    if KPH <= 4:
        return nc

    stF2 = ExitStack()
    wd_t = stF2.enter_context(nc.sbuf_tensor("wd_t", [128, NO * D], BF16))
    wtsd = [stF2.enter_context(nc.sbuf_tensor(f"wtsd{i}", [128, D], F32))
            for i in range(2)]
    tcm2 = stF2.enter_context(nc.sbuf_tensor("tcm2", [128, 2048], F32))
    gq_in = [stF2.enter_context(nc.sbuf_tensor(f"gqi{i}", [128, INNER],
                                               BF16)) for i in range(2)]
    y3 = stF2.enter_context(nc.sbuf_tensor("y3", [128, D], F32))
    outt = [stF2.enter_context(nc.sbuf_tensor(f"outt{i}", [128, D], F32))
            for i in range(2)]
    bs2 = stF2.enter_context(nc.sbuf_tensor("bs2", [128, 16], F32))

    pass

    sWD = Sem("wd")
    wdfree = [None, None]
    for c in range(NO):
        s = c % 2
        wmark(nc.sync, wdfree[s])
        sTD[s].dinc(nc.sync.dma_start(out=wtsd[s][:, :],
                                      in_=t_wdT[c * 128:(c + 1) * 128, :]))
        sTD[s].wait(nc.vector)
        nc.vector.tensor_scalar(out=tcm2[:, 0:1024], in0=wtsd[s][:, :],
                                scalar1=thr_g[:, 2:3], scalar2=None,
                                op0=ALU.is_gt)
        nc.vector.tensor_scalar(out=tcm2[:, 1024:2048], in0=wtsd[s][:, :],
                                scalar1=thr_g[:, 8:9], scalar2=None,
                                op0=ALU.is_lt)
        sWD.inc(nc.vector.tensor_tensor(out=wd_t[:, c * D:(c + 1) * D],
                                        in0=tcm2[:, 0:1024],
                                        in1=tcm2[:, 1024:2048],
                                        op=ALU.subtract))
        wdfree[s] = sWD.mark()

    if tap == "B0":
        nc.vector.wait_ge(sWD.h, NO)
        vsp()
        sV.inc(nc.vector.memset(scr[:, 63:64], 0.0))
        sV.wait(nc.gpsimd)
        sFIN.dinc(nc.gpsimd.dma_start(out=t_dWD[:, :], in_=wd_t[:, :]))
        sFIN.dinc(nc.gpsimd.dma_start(out=t_dED[:, :], in_=ed_buf[:, :]))
        sFIN.dinc(nc.gpsimd.dma_start(out=t_dTH[:, :], in_=thr_g[:, 0:9]))
    pd = [stF2.enter_context(nc.psum_tensor(f"pd{i}", [128, 1024], F32))
          for i in range(2)]
    xcons = [None, None]
    gcons = [None, None]
    ofree = [None, None]
    ofin = [None, None]
    for T in range(KNT + 1):
        if T < KNT:
            sl = T % 2
            wmark(nc.sync, gcons[sl])
            SLK[sl].dinc(nc.sync.dma_start(out=gq_in[sl][:, :],
                                           in_=t_gqs[T, :, :]))
            wmark(nc.sync, xcons[sl])
            SLX[sl].dinc(nc.sync.dma_start(
                out=xin[sl][:, :], in_=t_x1s[T * 128:(T + 1) * 128, :]))
        if T == 0:
            continue
        Tp, sl = T - 1, (T - 1) % 2
        p_ = pd[sl]
        SLK[sl].wait(nc.tensor)
        wmark(nc.tensor, ofree[sl])
        gq_c = gq_in[sl]
        for c in range(NO):
            if Tp == 0:
                nc.tensor.wait_ge(sWD.h, c + 1)
            nc.tensor.matmul(p_[:, 0:512], gq_c[:, c * 128:(c + 1) * 128],
                             wd_t[:, c * D:c * D + 512],
                             start=(c == 0), stop=(c == NO - 1))
            i = nc.tensor.matmul(p_[:, 512:1024],
                                 gq_c[:, c * 128:(c + 1) * 128],
                                 wd_t[:, c * D + 512:(c + 1) * D],
                                 start=(c == 0), stop=(c == NO - 1))
        sT.inc(i)
        gcons[sl] = sT.mark()
        sT.wait(nc.scalar)
        sS.inc(nc.scalar.activation(out=y3[:, :], in_=p_[:, :],
                                    func=AF.Identity,
                                    scale=ed_buf[:, Tp:Tp + 1]))
        ofree[sl] = sS.mark()
        if tap == "B0" and Tp == 0:
            sS.wait(nc.gpsimd)
            sFIN.dinc(nc.gpsimd.dma_start(out=t_dY3[:, :], in_=y3[:, :]))
        sS.wait(nc.vector)
        nc.vector.bn_stats(bs2[:, 0:6], y3[:, 0:512])
        nc.vector.bn_stats(bs2[:, 6:12], y3[:, 512:1024])
        vsp()
        nc.vector.bn_aggr(bs2[:, 12:14], bs2[:, 0:12])
        vsp()
        sV.inc(nc.vector.memset(scr[:, 63:64], 0.0))
        sV.wait(nc.scalar)
        nc.scalar.activation(out=scr[:, 56:57], in_=bs2[:, 13:14], func=AF.Ln,
                             bias=LN_EPS, scale=1.0)
        sS.inc(nc.scalar.activation(out=scr[:, 57:58], in_=scr[:, 56:57],
                                    func=AF.Exp, scale=-0.5))
        sS.wait(nc.vector)
        SLX[sl].wait(nc.vector)
        nc.vector.tensor_scalar(out=y3[:, :], in0=y3[:, :],
                                scalar1=bs2[:, 12:13], scalar2=scr[:, 57:58],
                                op0=ALU.subtract, op1=ALU.mult)
        vsp()
        wmark(nc.vector, ofin[sl])
        sV.inc(nc.vector.tensor_tensor(out=outt[sl][:, :], in0=y3[:, :],
                                       in1=xin[sl][:, :], op=ALU.add))
        xcons[sl] = sV.mark()
        sV.wait(nc.gpsimd)
        sFIN.dinc(nc.gpsimd.dma_start(
            out=t_out[Tp * 128:(Tp + 1) * 128, :], in_=outt[sl][:, :]))
        ofin[sl] = sFIN.mark()

    if tap == "x1":
        for T in range(NT):
            sFIN.dinc(nc.sync.dma_start(
                out=t_dbg[T * 128:(T + 1) * 128, :],
                in_=t_x1s[T * 128:(T + 1) * 128, :]))
    sFIN.wait(nc.gpsimd)
    stF2.close()
    stTH.close()
    es.close()
    return nc


_NC_CACHE = {}


def _get_nc(tap):
    if tap not in _NC_CACHE:
        _NC_CACHE[tap] = _build(tap)
    return _NC_CACHE[tap]


def _host_tables():
    j = np.arange(512, dtype=np.float64)
    inv = np.exp(-C_INV * j)
    t = np.arange(S, dtype=np.float64)
    ang = t[:, None] * inv[None, :]
    ks = np.round(ang / (2 * math.pi)).astype(np.float32)
    kc = np.round((ang + PI_2) / (2 * math.pi)).astype(np.float32)
    return np.ascontiguousarray(ks), np.ascontiguousarray(kc)


def kernel(L0_attn_x, L0_attn_wq, L0_attn_wk, L0_attn_wv, L0_attn_wo,
           L0_ffn_w_gate, L0_ffn_w_up, L0_ffn_w_down):
    x = np.ascontiguousarray(np.asarray(L0_attn_x, dtype=np.float32))
    wq = np.asarray(L0_attn_wq, dtype=np.float32)
    wk = np.asarray(L0_attn_wk, dtype=np.float32)
    wv = np.asarray(L0_attn_wv, dtype=np.float32)
    wo = np.asarray(L0_attn_wo, dtype=np.float32)
    wg = np.asarray(L0_ffn_w_gate, dtype=np.float32)
    wu = np.asarray(L0_ffn_w_up, dtype=np.float32)
    wd = np.asarray(L0_ffn_w_down, dtype=np.float32)

    perm = np.concatenate([np.arange(0, D, 2), np.arange(1, D, 2)])
    common = dict(
        wqT=np.ascontiguousarray(wq[perm, :].T),
        wkT=np.ascontiguousarray(wk[perm, :].T),
        wvT=np.ascontiguousarray(wv.T),
        woT=np.ascontiguousarray(wo.T),
        wgT=np.ascontiguousarray(wg.T),
        wuT=np.ascontiguousarray(wu.T),
        wdT=np.ascontiguousarray(wd.T),
        idn=np.eye(128, dtype=np.float32),
    )
    common["ks"], common["kc"] = _host_tables()

    tap = _DEBUG_TAP
    nc = _get_nc(tap)
    in_maps = [dict(x=np.ascontiguousarray(x[b]), **common) for b in range(8)]
    res = run_bass_kernel_spmd(nc, in_maps, core_ids=list(range(8)))
    out = np.stack([np.asarray(r["out"]) for r in res.results], axis=0)
    if tap is not None:
        kernel._dbg = np.stack([np.asarray(r["dbg"]) for r in res.results], 0)
    return out


# version 27
# speedup vs baseline: 1.0908x; 1.0688x over previous
import math
import os
from contextlib import ExitStack

import numpy as np

import concourse.bass as bass
import concourse.mybir as mybir
from concourse.bass_utils import run_bass_kernel_spmd

F32 = mybir.dt.float32
BF16 = mybir.dt.bfloat16
I32 = mybir.dt.int32
AF = mybir.ActivationFunctionType
ALU = mybir.AluOpType
AX = mybir.AxisListType

S = 4096
D = 1024
INNER = 4096
NT = S // 128
NCH = D // 128
NO = INNER // 128
LN_EPS = 1e-5
INVSQ = 1.0 / 32.0
C_INV = math.log(10000.0) / 512.0
_CW1 = float(np.float32(6.28125))
_CW2 = float(np.float32(2 * math.pi - _CW1))
_CW3 = float(np.float32(2 * math.pi - _CW1 - float(_CW2)))
PI = math.pi
PI_2 = math.pi / 2.0

_DEBUG_TAP = None
KNT = int(os.environ.get("KNT", "32"))
KPH = int(os.environ.get("KPH", "9"))


def _build(tap=None):
    nc = bass.Bass("TRN2", target_bir_lowering=False,
                   detect_race_conditions=(os.environ.get("KDRC") == "1"))
    es = ExitStack()

    t_x = nc.dram_tensor("x", [S, D], F32, kind="ExternalInput").ap()
    t_wqT = nc.dram_tensor("wqT", [D, D], F32, kind="ExternalInput").ap()
    t_wkT = nc.dram_tensor("wkT", [D, D], F32, kind="ExternalInput").ap()
    t_wvT = nc.dram_tensor("wvT", [D, D], F32, kind="ExternalInput").ap()
    t_woT = nc.dram_tensor("woT", [D, D], F32, kind="ExternalInput").ap()
    t_wgT = nc.dram_tensor("wgT", [D, INNER], F32, kind="ExternalInput").ap()
    t_wuT = nc.dram_tensor("wuT", [D, INNER], F32, kind="ExternalInput").ap()
    t_wdT = nc.dram_tensor("wdT", [INNER, D], F32, kind="ExternalInput").ap()
    t_ks = nc.dram_tensor("ks", [S, 512], F32, kind="ExternalInput").ap()
    t_kc = nc.dram_tensor("kc", [S, 512], F32, kind="ExternalInput").ap()
    t_id = nc.dram_tensor("idn", [128, 128], F32, kind="ExternalInput").ap()
    t_out = nc.dram_tensor("out", [S, D], F32, kind="ExternalOutput").ap()
    _ik = "ExternalOutput" if tap == "B0" else "Internal"
    t_qfs = nc.dram_tensor("qfs", [NT, 128, D], BF16, kind=_ik).ap()
    t_x1s = nc.dram_tensor("x1s", [S, D], F32, kind=_ik).ap()
    t_gqs = nc.dram_tensor("gqs", [NT, 128, INNER], BF16, kind=_ik).ap()
    t_xfs = nc.dram_tensor("xfs", [NT, 128, D], BF16, kind=_ik).ap()
    t_dbg = (nc.dram_tensor("dbg", [S, D], F32, kind="ExternalOutput").ap()
             if tap == "x1" else None)
    if tap == "B0":
        t_dK = nc.dram_tensor("dK", [128, NCH * S], BF16, kind=_ik).ap()
        t_dV = nc.dram_tensor("dV", [128, NT * D], BF16, kind=_ik).ap()
        t_dP = nc.dram_tensor("dP", [128, S], BF16, kind=_ik).ap()
        t_dPT = nc.dram_tensor("dPT", [128, S], BF16, kind=_ik).ap()
        t_dAV = nc.dram_tensor("dAV", [128, D], BF16, kind=_ik).ap()
        t_dY = nc.dram_tensor("dY", [128, D], F32, kind=_ik).ap()
        t_dB = nc.dram_tensor("dB", [128, 10], F32, kind=_ik).ap()
        t_dWD = nc.dram_tensor("dWD", [128, NO * D], BF16, kind=_ik).ap()
        t_dED = nc.dram_tensor("dED", [128, NT], F32, kind=_ik).ap()
        t_dY3 = nc.dram_tensor("dY3", [128, D], F32, kind=_ik).ap()
        t_dTH = nc.dram_tensor("dTH", [128, 9], F32, kind=_ik).ap()
        t_dOF = nc.dram_tensor("dOF", [128, D], BF16, kind=_ik).ap()
        t_dWO = nc.dram_tensor("dWO", [128, NCH * D], BF16, kind=_ik).ap()

    class Sem:
        __slots__ = ("h", "n")

        def __init__(self, name):
            self.h = es.enter_context(nc.semaphore(name))
            self.n = 0

        def inc(self, instr, k=1):
            instr.then_inc(self.h, k)
            self.n += k
            return instr

        def dinc(self, instr):
            return self.inc(instr, 16)

        def wait(self, eng, n=None):
            eng.wait_ge(self.h, self.n if n is None else n)

        def mark(self):
            return (self, self.n)

    def wmark(eng, mk):
        if mk is not None:
            mk[0].wait(eng, mk[1])

    sX0, sX1 = Sem("x0"), Sem("x1")
    sK0, sK1 = Sem("k0"), Sem("k1")
    sW = Sem("w")
    sT = Sem("t")
    sV = Sem("v")
    sS = Sem("s")
    sG = Sem("g")
    sQF = Sem("qf")
    sX1O = Sem("x1o")
    sGQ = Sem("gq")
    sFIN = Sem("fin")
    sVSP = Sem("vsp")
    sTD = (Sem("td0"), Sem("td1"))
    sWM = (Sem("wm0"), Sem("wm1"))
    SLX = (sX0, sX1)
    SLK = (sK0, sK1)

    cpool = es.enter_context(nc.sbuf_tensor("cpool", [128, 16], F32))
    ident = es.enter_context(nc.sbuf_tensor("ident", [128, 128], BF16))
    ln1_m = es.enter_context(nc.sbuf_tensor("ln1_m", [128, NT], F32))
    ln1_rs = es.enter_context(nc.sbuf_tensor("ln1_rs", [128, NT], F32))
    ln2_m = es.enter_context(nc.sbuf_tensor("ln2_m", [128, NT], F32))
    ln2_rs = es.enter_context(nc.sbuf_tensor("ln2_rs", [128, NT], F32))
    q2s = es.enter_context(nc.sbuf_tensor("q2s", [128, NT], F32))
    ed_buf = es.enter_context(nc.sbuf_tensor("ed_buf", [128, NT], F32))
    h2clip = es.enter_context(nc.sbuf_tensor("h2clip", [128, NT], F32))
    scr = es.enter_context(nc.sbuf_tensor("scr", [128, 64], F32))
    xin = [es.enter_context(nc.sbuf_tensor(f"xin{i}", [128, D], F32))
           for i in range(2)]

    consts = {}

    def creg(v):
        v = float(v)
        if v not in consts:
            i = len(consts)
            assert i < 16
            nc.vector.memset(cpool[:, i:i + 1], v)
            nc.const_aps.register(F32, v, cpool[:, i:i + 1])
            consts[v] = i

    for v in (LN_EPS, -0.5, 0.5, INVSQ, -C_INV, PI_2, -PI, PI,
              2 * PI, 0.0, 1.0, -1.0):
        creg(v)

    def vsp():
        sVSP.inc(nc.vector.memset(scr[:, 63:64], 0.0))
        nc.vector.wait_ge(sVSP.h, sVSP.n)

    vsp()
    sV.inc(nc.vector.memset(scr[:, 63:64], 0.0))
    pass
    with nc.sbuf_tensor("idf", [128, 128], F32) as idf, \
         nc.sbuf_tensor("pst", [128, 16], F32) as pst:
        sW.dinc(nc.sync.dma_start(out=idf[:, :], in_=t_id[:, :]))
        nc.gpsimd.iota(scr[:, 0:1], pattern=[[0, 1]], base=0,
                       channel_multiplier=1,
                       allow_small_or_imprecise_dtypes=True)
        sG.inc(nc.gpsimd.engine_nop())
        sW.wait(nc.vector)
        sV.inc(nc.vector.tensor_copy(out=ident[:, :], in_=idf[:, :]))

        xcons = [None, None]
        for T in range(KNT + 1):
            if T < KNT:
                sl = T % 2
                wmark(nc.sync, xcons[sl])
                SLX[sl].dinc(nc.sync.dma_start(
                    out=xin[sl][:, :], in_=t_x[T * 128:(T + 1) * 128, :]))
            if T == 0:
                continue
            Tp, sl = T - 1, (T - 1) % 2
            SLX[sl].wait(nc.vector)
            xt = xin[sl]
            nc.vector.bn_stats(pst[:, 0:6], xt[:, 0:512])
            nc.vector.bn_stats(pst[:, 6:12], xt[:, 512:1024])
            vsp()
            nc.vector.bn_aggr(pst[:, 12:14], pst[:, 0:12])
            vsp()
            sV.inc(nc.vector.tensor_copy(out=ln1_m[:, Tp:Tp + 1],
                                         in_=pst[:, 12:13]))
            xcons[sl] = sV.mark()
            sV.wait(nc.scalar)
            nc.scalar.activation(out=pst[:, 14:15], in_=pst[:, 13:14],
                                 func=AF.Ln, bias=LN_EPS, scale=1.0)
            sS.inc(nc.scalar.activation(out=ln1_rs[:, Tp:Tp + 1],
                                        in_=pst[:, 14:15],
                                        func=AF.Exp, scale=-0.5))
        sS.wait(nc.vector)
        nc.all_engine_barrier()
    if KPH <= 0:
        return nc

    def stage_bf16(dst, src_t, n_chunks, width, stg):
        for c in range(n_chunks):
            sV.wait(nc.sync)
            sW.dinc(nc.sync.dma_start(
                out=stg[:, 0:width], in_=src_t[c * 128:(c + 1) * 128, :]))
            sW.wait(nc.vector)
            sV.inc(nc.vector.tensor_copy(out=dst[:, c * width:(c + 1) * width],
                                         in_=stg[:, 0:width]))

    stKV = ExitStack()
    K_feat = stKV.enter_context(nc.sbuf_tensor("K_feat", [128, NCH * S], BF16))

    stA = ExitStack()
    inv_row = stA.enter_context(nc.sbuf_tensor("inv_row", [128, 512], F32))
    wq_bf = stA.enter_context(nc.sbuf_tensor("wq_bf", [128, NCH * D], BF16))
    wk_bf = stA.enter_context(nc.sbuf_tensor("wk_bf", [128, NCH * D], BF16))
    kin = stA.enter_context(nc.sbuf_tensor("kin", [128, 1024], F32))
    h_bf = [stA.enter_context(nc.sbuf_tensor(f"h_bf{i}", [128, D], BF16))
            for i in range(2)]
    h_feat = stA.enter_context(nc.sbuf_tensor("h_feat", [128, D], BF16))
    q_bf = [stA.enter_context(nc.sbuf_tensor(f"q_bf{i}", [128, D], BF16))
            for i in range(2)]
    k_bf = [stA.enter_context(nc.sbuf_tensor(f"k_bf{i}", [128, D], BF16))
            for i in range(2)]
    q_rot = stA.enter_context(nc.sbuf_tensor("q_rot", [128, D], BF16))
    k_rot = stA.enter_context(nc.sbuf_tensor("k_rot", [128, D], BF16))
    q_feat = stA.enter_context(nc.sbuf_tensor("q_feat", [128, D], BF16))
    angs = stA.enter_context(nc.sbuf_tensor("angs", [128, 512], F32))
    angc = stA.enter_context(nc.sbuf_tensor("angc", [128, 512], F32))
    cwa = stA.enter_context(nc.sbuf_tensor("cwa", [128, 512], F32))
    cwb = stA.enter_context(nc.sbuf_tensor("cwb", [128, 512], F32))
    cosb = [stA.enter_context(nc.sbuf_tensor(f"cosb{i}", [128, 512], BF16))
            for i in range(3)]
    sinb = [stA.enter_context(nc.sbuf_tensor(f"sinb{i}", [128, 512], BF16))
            for i in range(3)]
    sq_bf = stA.enter_context(nc.sbuf_tensor("sq_bf", [128, D], BF16))
    m14 = stA.enter_context(nc.sbuf_tensor("m14", [128, 2048], BF16))

    sG.inc(nc.gpsimd.iota(inv_row[:, :], pattern=[[1, 512]], base=0,
                          channel_multiplier=0,
                          allow_small_or_imprecise_dtypes=True))
    sG.wait(nc.scalar)
    sS.inc(nc.scalar.activation(out=inv_row[:, :], in_=inv_row[:, :],
                                func=AF.Exp, scale=-C_INV))
    pass
    stage_bf16(wq_bf, t_wqT, NCH, D, xin[0])
    stage_bf16(wk_bf, t_wkT, NCH, D, xin[0])
    sV.wait(nc.tensor)
    sV.wait(nc.scalar)
    sV.wait(nc.sync)

    paH = stA.enter_context(nc.psum_tensor("paH", [128, 1024], BF16))
    paQ = stA.enter_context(nc.psum_tensor("paQ", [128, 1024], F32))
    paK = stA.enter_context(nc.psum_tensor("paK", [128, 1024], F32))
    paT = stA.enter_context(nc.psum_tensor("paT", [128, 1024], BF16))
    paR = stA.enter_context(nc.psum_tensor("paR", [128, 1024], BF16))

    xcons = [None, None]
    kcons = None
    qfree = None
    evdone = None

    def p1_a1(T):
        nonlocal kcons
        sl = T % 2
        wmark(nc.sync, xcons[sl])
        SLX[sl].dinc(nc.sync.dma_start(
            out=xin[sl][:, :], in_=t_x[T * 128:(T + 1) * 128, :]))
        wmark(nc.sync, kcons)
        SLK[0].dinc(nc.sync.dma_start(
            out=kin[:, 0:512], in_=t_ks[T * 128:(T + 1) * 128, :]))
        SLK[0].dinc(nc.sync.dma_start(
            out=kin[:, 512:1024], in_=t_kc[T * 128:(T + 1) * 128, :]))
        SLX[sl].wait(nc.vector)
        sV.inc(nc.vector.tensor_scalar(
            out=h_bf[sl][:, :], in0=xin[sl][:, :],
            scalar1=ln1_m[:, T:T + 1], scalar2=ln1_rs[:, T:T + 1],
            op0=ALU.subtract, op1=ALU.mult))
        hmk[sl] = sV.mark()
        xcons[sl] = sV.mark()
        tc = 1 + 3 * sl
        nc.vector.tensor_scalar(out=scr[:, tc:tc + 1], in0=scr[:, 0:1],
                                scalar1=float(128 * T), scalar2=None,
                                op0=ALU.add)
        vsp()
        sV.inc(nc.vector.memset(scr[:, 63:64], 0.0))
        sV.wait(nc.scalar)
        nc.scalar.activation(out=angs[:, :], in_=inv_row[:, :], func=AF.Copy,
                             scale=scr[:, tc:tc + 1])
        sS.inc(nc.scalar.activation(out=angc[:, :], in_=inv_row[:, :],
                                    func=AF.Identity, scale=scr[:, tc:tc + 1],
                                    bias=PI_2))
        SLK[0].wait(nc.vector)
        sS.wait(nc.vector)
        nc.vector.scalar_tensor_tensor(out=cwa[:, :], in0=kin[:, 0:512],
                                       scalar=-_CW1, in1=angs[:, :],
                                       op0=ALU.mult, op1=ALU.add)
        nc.vector.scalar_tensor_tensor(out=cwb[:, :], in0=kin[:, 512:1024],
                                       scalar=-_CW1, in1=angc[:, :],
                                       op0=ALU.mult, op1=ALU.add)
        nc.vector.scalar_tensor_tensor(out=angs[:, :], in0=kin[:, 0:512],
                                       scalar=-_CW2, in1=cwa[:, :],
                                       op0=ALU.mult, op1=ALU.add)
        nc.vector.scalar_tensor_tensor(out=angc[:, :], in0=kin[:, 512:1024],
                                       scalar=-_CW2, in1=cwb[:, :],
                                       op0=ALU.mult, op1=ALU.add)
        nc.vector.scalar_tensor_tensor(out=cwa[:, :], in0=kin[:, 0:512],
                                       scalar=-_CW3, in1=angs[:, :],
                                       op0=ALU.mult, op1=ALU.add)
        sV.inc(nc.vector.scalar_tensor_tensor(
            out=cwb[:, :], in0=kin[:, 512:1024], scalar=-_CW3,
            in1=angc[:, :], op0=ALU.mult, op1=ALU.add))
        kcons = sV.mark()
        s3 = T % 3
        sV.wait(nc.scalar)
        wmark(nc.scalar, cs_cons[s3])
        nc.scalar.activation(out=angs[:, :], in_=cwa[:, :], func=AF.Relu,
                             scale=-1.0, bias=PI)
        nc.scalar.activation(out=angc[:, :], in_=cwb[:, :], func=AF.Relu,
                             scale=-1.0, bias=PI)
        nc.scalar.activation(out=cwa[:, :], in_=angs[:, :], func=AF.Relu,
                             scale=-1.0, bias=2 * PI)
        nc.scalar.activation(out=cwb[:, :], in_=angc[:, :], func=AF.Relu,
                             scale=-1.0, bias=2 * PI)
        nc.scalar.activation(out=sinb[s3][:, :], in_=cwa[:, :], func=AF.Sin,
                             bias=-PI, scale=1.0)
        sS.inc(nc.scalar.activation(out=cosb[s3][:, :], in_=cwb[:, :],
                                    func=AF.Sin, bias=-PI, scale=1.0))
        csmk[s3] = sS.mark()

    def pM_a1(T):
        nonlocal evdone
        sl = T % 2
        wmark(nc.tensor, hmk[sl])
        wmark(nc.tensor, evdone)
        wmark(nc.tensor, ropemk[sl])
        for c in range(NCH):
            i = nc.tensor.transpose(paH[:, c * 128:(c + 1) * 128],
                                    h_bf[sl][:, c * 128:(c + 1) * 128],
                                    ident[:, :])
        sT.inc(i)
        sT.wait(nc.scalar)
        sS.inc(nc.scalar.copy(out=h_feat[:, :], in_=paH[:, :]))
        sS.wait(nc.tensor)
        for c in range(NCH):
            lhs = h_feat[:, c * 128:(c + 1) * 128]
            nc.tensor.matmul(paQ[:, 0:512], lhs, wq_bf[:, c * D:c * D + 512],
                             start=(c == 0), stop=(c == NCH - 1))
            nc.tensor.matmul(paQ[:, 512:1024], lhs,
                             wq_bf[:, c * D + 512:(c + 1) * D],
                             start=(c == 0), stop=(c == NCH - 1))
        for c in range(NCH):
            lhs = h_feat[:, c * 128:(c + 1) * 128]
            nc.tensor.matmul(paK[:, 0:512], lhs, wk_bf[:, c * D:c * D + 512],
                             start=(c == 0), stop=(c == NCH - 1))
            i = nc.tensor.matmul(paK[:, 512:1024], lhs,
                                 wk_bf[:, c * D + 512:(c + 1) * D],
                                 start=(c == 0), stop=(c == NCH - 1))
        sT.inc(i)
        sT.wait(nc.scalar)
        nc.scalar.copy(out=q_bf[sl][:, :], in_=paQ[:, :])
        sS.inc(nc.scalar.copy(out=k_bf[sl][:, :], in_=paK[:, :]))
        qkmk[sl] = sS.mark()

    def pR_a1(T):
        nonlocal qfree, evdone, qkT
        sl = T % 2
        s3 = T % 3
        wmark(nc.vector, qkmk[sl])
        wmark(nc.vector, csmk[s3])
        wmark(nc.vector, qkT)
        for srcb, dst in ((q_bf[sl], q_rot), (k_bf[sl], k_rot)):
            xr, xi = srcb[:, 0:512], srcb[:, 512:1024]
            nc.vector.tensor_tensor(out=m14[:, 0:512], in0=xr,
                                    in1=cosb[s3][:, :], op=ALU.mult)
            nc.vector.tensor_tensor(out=m14[:, 512:1024], in0=xi,
                                    in1=sinb[s3][:, :], op=ALU.mult)
            nc.vector.tensor_tensor(out=m14[:, 1024:1536], in0=xr,
                                    in1=sinb[s3][:, :], op=ALU.mult)
            nc.vector.tensor_tensor(out=m14[:, 1536:2048], in0=xi,
                                    in1=cosb[s3][:, :], op=ALU.mult)
            nc.vector.tensor_tensor(out=dst[:, 0:512], in0=m14[:, 0:512],
                                    in1=m14[:, 512:1024], op=ALU.subtract)
            i = nc.vector.tensor_tensor(out=dst[:, 512:1024],
                                        in0=m14[:, 1024:1536],
                                        in1=m14[:, 1536:2048], op=ALU.add)
        sV.inc(i)
        ropemk[sl] = sV.mark()
        cs_cons[s3] = sV.mark()
        sV.wait(nc.scalar)
        nc.scalar.activation(out=sq_bf[:, :], in_=q_rot[:, :], func=AF.Square,
                             accum_out=q2s[:, T:T + 1])
        sS.inc(nc.scalar.activation(out=sq_bf[:, :], in_=k_rot[:, :],
                                    func=AF.Square,
                                    accum_out=scr[:, 3:4]))
        sS.wait(nc.vector)
        if T == 0:
            nc.vector.tensor_copy(out=scr[:, 2:3], in_=scr[:, 3:4])
        else:
            nc.vector.tensor_tensor(out=scr[:, 2:3], in0=scr[:, 2:3],
                                    in1=scr[:, 3:4], op=ALU.max)
        vsp()
        sV.inc(nc.vector.memset(scr[:, 63:64], 0.0))
        sV.wait(nc.tensor)
        wmark(nc.tensor, evdone)
        for c in range(NCH):
            i = nc.tensor.transpose(paT[:, c * 128:(c + 1) * 128],
                                    q_rot[:, c * 128:(c + 1) * 128],
                                    ident[:, :])
        sT.inc(i)
        sT.wait(nc.scalar)
        wmark(nc.scalar, qfree)
        sS.inc(nc.scalar.copy(out=q_feat[:, :], in_=paT[:, :]))
        sS.wait(nc.tensor)
        for c in range(NCH):
            i = nc.tensor.transpose(paR[:, c * 128:(c + 1) * 128],
                                    k_rot[:, c * 128:(c + 1) * 128],
                                    ident[:, :])
        sT.inc(i)
        qkT = sT.mark()
        sT.wait(nc.scalar)
        kf = K_feat[:, :].rearrange("p (c k) -> p c k", c=NCH)
        pa = paR[:, :].rearrange("p (c k) -> p c k", c=NCH)
        sS.inc(nc.scalar.copy(out=kf[:, :, T * 128:(T + 1) * 128], in_=pa))
        evdone = sS.mark()
        sS.wait(nc.gpsimd)
        sQF.dinc(nc.gpsimd.dma_start(out=t_qfs[T, :, :], in_=q_feat[:, :]))
        qfree = sQF.mark()

    hmk = [None, None]
    csmk = [None, None, None]
    cs_cons = [None, None, None]
    qkmk = [None, None]
    ropemk = [None, None]
    qkT = None
    for T in range(KNT + 2):
        if T < KNT:
            p1_a1(T)
        if 1 <= T <= KNT:
            pM_a1(T - 1)
        if 2 <= T <= KNT + 1:
            pR_a1(T - 2)
    sQF.wait(nc.gpsimd)
    nc.all_engine_barrier()
    stA.close()
    if KPH <= 1:
        return nc

    stKV2 = ExitStack()
    V_tok = stKV2.enter_context(nc.sbuf_tensor("V_tok", [128, NT * D], BF16))
    stA2 = ExitStack()
    wv_bf = stA2.enter_context(nc.sbuf_tensor("wv_bf", [128, NCH * D], BF16))
    h2bf = stA2.enter_context(nc.sbuf_tensor("h2bf", [128, D], BF16))
    h2ft = stA2.enter_context(nc.sbuf_tensor("h2ft", [128, D], BF16))
    stage_bf16(wv_bf, t_wvT, NCH, D, xin[0])
    sV.wait(nc.sync)
    sV.wait(nc.tensor)

    paH2 = stA2.enter_context(nc.psum_tensor("paH2", [128, 1024], BF16))
    paV = stA2.enter_context(nc.psum_tensor("paV", [128, 1024], F32))
    xcons = [None, None]
    evdone = None
    hcons = None
    for T in range(KNT + 1):
        if T < KNT:
            sl = T % 2
            wmark(nc.sync, xcons[sl])
            SLX[sl].dinc(nc.sync.dma_start(
                out=xin[sl][:, :], in_=t_x[T * 128:(T + 1) * 128, :]))
        if T == 0:
            continue
        Tp, sl = T - 1, (T - 1) % 2
        SLX[sl].wait(nc.vector)
        wmark(nc.vector, hcons)
        sV.inc(nc.vector.tensor_scalar(
            out=h2bf[:, :], in0=xin[sl][:, :],
            scalar1=ln1_m[:, Tp:Tp + 1], scalar2=ln1_rs[:, Tp:Tp + 1],
            op0=ALU.subtract, op1=ALU.mult))
        xcons[sl] = sV.mark()
        sV.wait(nc.tensor)
        wmark(nc.tensor, evdone)
        for c in range(NCH):
            i = nc.tensor.transpose(paH2[:, c * 128:(c + 1) * 128],
                                    h2bf[:, c * 128:(c + 1) * 128],
                                    ident[:, :])
        sT.inc(i)
        hcons = sT.mark()
        sT.wait(nc.scalar)
        sS.inc(nc.scalar.copy(out=h2ft[:, :], in_=paH2[:, :]))
        sS.wait(nc.tensor)
        for c in range(NCH):
            lhs = h2ft[:, c * 128:(c + 1) * 128]
            nc.tensor.matmul(paV[:, 0:512], lhs, wv_bf[:, c * D:c * D + 512],
                             start=(c == 0), stop=(c == NCH - 1))
            i = nc.tensor.matmul(paV[:, 512:1024], lhs,
                                 wv_bf[:, c * D + 512:(c + 1) * D],
                                 start=(c == 0), stop=(c == NCH - 1))
        sT.inc(i)
        sT.wait(nc.scalar)
        sS.inc(nc.scalar.copy(out=V_tok[:, Tp * D:(Tp + 1) * D],
                              in_=paV[:, :]))
        evdone = sS.mark()
    sS.wait(nc.vector)
    nc.all_engine_barrier()
    if tap == "B0":
        sFIN.dinc(nc.gpsimd.dma_start(out=t_dK[:, :], in_=K_feat[:, :]))
        sFIN.dinc(nc.gpsimd.dma_start(out=t_dV[:, :], in_=V_tok[:, :]))
        sFIN.wait(nc.gpsimd)
        nc.all_engine_barrier()
    stA2.close()
    if KPH <= 2:
        return nc

    stB = ExitStack()
    wo_bf = stB.enter_context(nc.sbuf_tensor("wo_bf", [128, NCH * D], BF16))
    qf_in = [stB.enter_context(nc.sbuf_tensor(f"qf{i}", [128, D], BF16))
             for i in range(2)]
    probs = stB.enter_context(nc.sbuf_tensor("probs", [128, S], BF16))
    probsT = stB.enter_context(nc.sbuf_tensor("probsT", [128, S], BF16))
    av_bf = stB.enter_context(nc.sbuf_tensor("av_bf", [128, D], BF16))
    of_bf = stB.enter_context(nc.sbuf_tensor("of_bf", [128, D], BF16))
    y1 = stB.enter_context(nc.sbuf_tensor("y1", [128, D], F32))
    x1t = stB.enter_context(nc.sbuf_tensor("x1t", [128, D], F32))
    wmst = stB.enter_context(nc.sbuf_tensor("wmst", [128, 4096], F32))
    xqi = stB.enter_context(nc.sbuf_tensor("xqi", [128, 512], I32))
    xqc = stB.enter_context(nc.sbuf_tensor("xqc", [128, 512], BF16))
    Xb2 = stB.enter_context(nc.sbuf_tensor("Xb2", [128, D], BF16))
    bst = stB.enter_context(nc.sbuf_tensor("bst", [128, 32], F32))

    pass
    stage_bf16(wo_bf, t_woT, NCH, D, xin[0])
    sV.wait(nc.sync)

    with nc.sbuf_tensor("bc", [128, 128], BF16) as bc, \
         nc.sbuf_tensor("onr", [1, 128], F32) as onr, \
         nc.psum_tensor("pbx", [128, 128], BF16) as pbx, \
         nc.psum_tensor("pbb", [128, 1], F32) as pbb:
        nc.vector.memset(bc[:, :], 0.0)
        nc.vector.memset(onr[:, :], 1.0)
        vsp()
        sV.inc(nc.vector.memset(scr[:, 63:64], 0.0))
        sV.wait(nc.scalar)
        sS.inc(nc.scalar.copy(out=bc[:, 0:1], in_=scr[:, 2:3]))
        sS.wait(nc.tensor)
        sT.inc(nc.tensor.transpose(pbx[:, :], bc[:, :], ident[:, :]))
        sT.wait(nc.vector)
        sV.inc(nc.vector.reduce_max(out=scr[0:1, 9:10], in_=pbx[0:1, :],
                                    axis=AX.X))
        sV.wait(nc.scalar)
        nc.scalar.activation(out=scr[0:1, 10:11], in_=scr[0:1, 9:10],
                             func=AF.Ln)
        sS.inc(nc.scalar.activation(out=scr[0:1, 11:12], in_=scr[0:1, 10:11],
                                    func=AF.Exp, scale=0.5))
        sS.wait(nc.vector)
        sV.inc(nc.vector.tensor_scalar(out=scr[0:1, 12:13],
                                       in0=scr[0:1, 11:12],
                                       scalar1=-INVSQ, scalar2=None,
                                       op0=ALU.mult))
        sV.wait(nc.tensor)
        sT.inc(nc.tensor.matmul(pbb[:, :], onr[:, :], scr[0:1, 12:13],
                                start=True, stop=True))
        sT.wait(nc.vector)
        sV.inc(nc.vector.tensor_copy(out=scr[:, 13:14], in_=pbb[:, :]))

    wm_chunks = []
    for tt, col in ((t_wgT, 16), (t_wuT, 17), (t_wdT, 18)):
        rows, cols = (D, INNER) if tt is not t_wdT else (INNER, D)
        for r0 in range(0, rows, 128):
            for c0 in range(0, cols, 2048):
                w_ = min(2048, cols - c0)
                wm_chunks.append((tt[r0:r0 + 128, c0:c0 + w_], col, w_))
    nc.vector.memset(scr[:, 16:19], 0.0)
    vsp()
    sV.inc(nc.vector.memset(scr[:, 63:64], 0.0))

    psS4 = stB.enter_context(nc.psum_tensor("psS4", [128, 2048], F32))
    pbT0 = stB.enter_context(nc.psum_tensor("pbT0", [128, 1024], BF16))
    pbT1 = stB.enter_context(nc.psum_tensor("pbT1", [128, 1024], BF16))
    pbAP = stB.enter_context(nc.psum_tensor("pbAP", [128, 1024], F32))

    qcons = [None, None]
    xcons = [None, None]
    x1free = None
    expdone = None
    wmfree = [None, None]
    ebmk = [None, None]
    summk = [None]
    xsmk = [None]
    xevmk = [None]
    xffree = [None]

    def p1_b(T):
        sl = T % 2
        wmark(nc.sync, qcons[sl])
        SLK[sl].dinc(nc.sync.dma_start(out=qf_in[sl][:, :],
                                       in_=t_qfs[T, :, :]))
        wmark(nc.sync, xcons[sl])
        SLX[sl].dinc(nc.sync.dma_start(
            out=xin[sl][:, :], in_=t_x[T * 128:(T + 1) * 128, :]))
        sS.wait(nc.scalar)
        nc.scalar.activation(out=scr[:, 20:21], in_=q2s[:, T:T + 1],
                             func=AF.Ln)
        sS.inc(nc.scalar.activation(out=scr[:, 21:22], in_=scr[:, 20:21],
                                    func=AF.Exp, scale=0.5))
        sS.wait(nc.vector)
        sV.inc(nc.vector.tensor_tensor(out=bst[:, 24 + sl:25 + sl],
                                       in0=scr[:, 21:22], in1=scr[:, 13:14],
                                       op=ALU.mult))
        ebmk[sl] = sV.mark()

    def xquant_tail(Tq):
        h2B = probs[:, 2048:3072]
        xqy = probs[:, 3072:3584]
        xqb = probs[:, 3584:4096]
        sV.wait(nc.scalar)
        wmark(nc.vector, xffree[0])
        for b in range(2):
            if b == 1:
                sV.wait(nc.scalar)
            hb = h2B[:, b * 512:(b + 1) * 512]
            nc.scalar.activation(out=xqi[:, :], in_=hb, func=AF.Copy,
                                 scale=scr[:, 35:36], bias=-0.5)
            nc.scalar.activation(out=xqy, in_=hb, func=AF.Copy,
                                 scale=scr[:, 35:36])
            sS.inc(nc.scalar.activation(out=xqb, in_=xqi[:, :],
                                        func=AF.Copy, bias=0.5))
            sS.wait(nc.vector)
            nc.vector.tensor_tensor(out=xqc[:, :], in0=xqy,
                                    in1=xqb, op=ALU.is_gt)
            sV.inc(nc.vector.scalar_tensor_tensor(
                out=Xb2[:, b * 512:(b + 1) * 512], in0=xqc[:, :],
                scalar=-0.5, in1=xqb, op0=ALU.add, op1=ALU.add))
        xsmk[0] = sV.mark()

    def xpose_b(Tp):
        wmark(nc.tensor, xsmk[0])
        for c in range(NCH):
            i = nc.tensor.transpose(pbT0[:, c * 128:(c + 1) * 128],
                                    Xb2[:, c * 128:(c + 1) * 128],
                                    ident[:, :])
        sT.inc(i)
        sT.wait(nc.scalar)
        sS.inc(nc.scalar.copy(out=Xb2[:, :], in_=pbT0[:, :]))
        xevmk[0] = sS.mark()
        sS.wait(nc.gpsimd)
        sQF.dinc(nc.gpsimd.dma_start(out=t_xfs[Tp, :, :], in_=Xb2[:, :]))
        xffree[0] = sQF.mark()

    def p2_b(T):
        nonlocal x1free, expdone
        sl = T % 2
        for j in range((len(wm_chunks) * T) // NT,
                       (len(wm_chunks) * (T + 1)) // NT):
            ap, col, w_ = wm_chunks[j]
            wmark(nc.gpsimd, wmfree[j % 2])
            sWM[j % 2].dinc(nc.gpsimd.dma_start(
                out=wmst[:, 2048 * (j % 2):2048 * (j % 2) + w_], in_=ap))
        SLK[sl].wait(nc.tensor)
        qf = qf_in[sl]

        def score_half(half):
            nonlocal expdone
            wmark(nc.tensor, expdone)
            for kb2 in range(4):
                kb = half * 4 + kb2
                for c in range(NCH):
                    i = nc.tensor.matmul(
                        psS4[:, kb2 * 512:(kb2 + 1) * 512],
                        qf[:, c * 128:(c + 1) * 128],
                        K_feat[:, c * S + kb * 512:c * S + (kb + 1) * 512],
                        start=(c == 0), stop=(c == NCH - 1))
            sT.inc(i)
            if half == 1:
                qcons[sl] = sT.mark()
            sT.wait(nc.scalar)
            if half == 0:
                wmark(nc.scalar, ebmk[sl])
                wmark(nc.scalar, summk[0])
            else:
                wmark(nc.scalar, xsmk[0])
            for kb2 in range(4):
                kb = half * 4 + kb2
                i = nc.scalar.activation(
                    out=probs[:, kb * 512:(kb + 1) * 512],
                    in_=psS4[:, kb2 * 512:(kb2 + 1) * 512], func=AF.Exp,
                    bias=bst[:, 24 + sl:25 + sl], scale=INVSQ,
                    accum_out=bst[:, kb:kb + 1])
            sS.inc(i)
            expdone = sS.mark()

        score_half(0)
        if T >= 2:
            xpose_b(T - 2)
        if T >= 1:
            xquant_tail(T - 1)
        score_half(1)
        sS.wait(nc.vector)
        nc.vector.reduce_sum(out=bst[:, 8:9], in_=bst[:, 0:8], axis=AX.X)
        vsp()
        nc.vector.reciprocal(out=bst[:, 9:10], in_=bst[:, 8:9])
        vsp()
        sV.inc(nc.vector.memset(scr[:, 63:64], 0.0))
        summk[0] = sV.mark()
        if tap == "B0" and T == 0:
            sS.wait(nc.gpsimd)
            sV.wait(nc.gpsimd)
            sFIN.dinc(nc.gpsimd.dma_start(out=t_dP[:, :], in_=probs[:, :]))
            sFIN.dinc(nc.gpsimd.dma_start(out=t_dB[:, :], in_=bst[:, 0:10]))
        evmk = [None, None]
        wmark(nc.tensor, xevmk[0])
        for grp in range(4):
            pb = (pbT0, pbT1)[grp % 2]
            wmark(nc.tensor, evmk[grp % 2])
            for cc in range(8):
                c = grp * 8 + cc
                i = nc.tensor.transpose(pb[:, cc * 128:(cc + 1) * 128],
                                        probs[:, c * 128:(c + 1) * 128],
                                        ident[:, :])
            sT.inc(i)
            sT.wait(nc.scalar)
            sS.inc(nc.scalar.copy(out=probsT[:, grp * 1024:(grp + 1) * 1024],
                                  in_=pb[:, :]))
            evmk[grp % 2] = sS.mark()
        sS.wait(nc.tensor)
        for c in range(NO):
            nc.tensor.matmul(pbAP[:, 0:512],
                             probsT[:, c * 128:(c + 1) * 128],
                             V_tok[:, c * 1024:c * 1024 + 512],
                             start=(c == 0), stop=(c == NO - 1))
            i = nc.tensor.matmul(pbAP[:, 512:1024],
                                 probsT[:, c * 128:(c + 1) * 128],
                                 V_tok[:, c * 1024 + 512:(c + 1) * 1024],
                                 start=(c == 0), stop=(c == NO - 1))
        sT.inc(i)
        sT.wait(nc.scalar)
        sS.inc(nc.scalar.copy(out=av_bf[:, :], in_=pbAP[:, :]))
        sS.wait(nc.tensor)
        for c in range(NCH):
            i = nc.tensor.transpose(pbT0[:, c * 128:(c + 1) * 128],
                                    av_bf[:, c * 128:(c + 1) * 128],
                                    ident[:, :])
        sT.inc(i)
        sT.wait(nc.scalar)
        sS.inc(nc.scalar.copy(out=of_bf[:, :], in_=pbT0[:, :]))
        sS.wait(nc.tensor)
        for c in range(NCH):
            lhs = of_bf[:, c * 128:(c + 1) * 128]
            nc.tensor.matmul(pbAP[:, 0:512], lhs,
                             wo_bf[:, c * D:c * D + 512],
                             start=(c == 0), stop=(c == NCH - 1))
            i = nc.tensor.matmul(pbAP[:, 512:1024], lhs,
                                 wo_bf[:, c * D + 512:(c + 1) * D],
                                 start=(c == 0), stop=(c == NCH - 1))
        sT.inc(i)
        sT.wait(nc.scalar)
        sS.inc(nc.scalar.activation(out=y1[:, :], in_=pbAP[:, :],
                                    func=AF.Identity,
                                    scale=bst[:, 9:10]))
        if tap == "B0" and T == 0:
            sS.wait(nc.gpsimd)
            sFIN.dinc(nc.gpsimd.dma_start(out=t_dPT[:, :], in_=probsT[:, :]))
            sFIN.dinc(nc.gpsimd.dma_start(out=t_dAV[:, :], in_=av_bf[:, :]))
            sFIN.dinc(nc.gpsimd.dma_start(out=t_dY[:, :], in_=y1[:, :]))
            sFIN.dinc(nc.gpsimd.dma_start(out=t_dOF[:, :], in_=of_bf[:, :]))
            sFIN.dinc(nc.gpsimd.dma_start(out=t_dWO[:, :], in_=wo_bf[:, :]))
            sFIN.wait(nc.vector)
        sS.wait(nc.vector)
        nc.vector.bn_stats(bst[:, 10:16], y1[:, 0:512])
        nc.vector.bn_stats(bst[:, 16:22], y1[:, 512:1024])
        vsp()
        nc.vector.bn_aggr(bst[:, 22:24], bst[:, 10:22])
        vsp()
        sV.inc(nc.vector.memset(scr[:, 63:64], 0.0))
        sV.wait(nc.scalar)
        nc.scalar.activation(out=scr[:, 26:27], in_=bst[:, 23:24], func=AF.Ln,
                             bias=LN_EPS, scale=1.0)
        sS.inc(nc.scalar.activation(out=scr[:, 27:28], in_=scr[:, 26:27],
                                    func=AF.Exp, scale=-0.5))
        sS.wait(nc.vector)
        SLX[sl].wait(nc.vector)
        wmark(nc.vector, x1free)
        nc.vector.tensor_scalar(out=y1[:, :], in0=y1[:, :],
                                scalar1=bst[:, 22:23], scalar2=scr[:, 27:28],
                                op0=ALU.subtract, op1=ALU.mult)
        vsp()
        sV.inc(nc.vector.tensor_tensor(out=x1t[:, :], in0=y1[:, :],
                                       in1=xin[sl][:, :], op=ALU.add))
        xcons[sl] = sV.mark()
        nc.vector.bn_stats(bst[:, 10:16], x1t[:, 0:512])
        nc.vector.bn_stats(bst[:, 16:22], x1t[:, 512:1024])
        vsp()
        nc.vector.bn_aggr(bst[:, 22:24], bst[:, 10:22])
        vsp()
        sV.inc(nc.vector.tensor_copy(out=ln2_m[:, T:T + 1],
                                     in_=bst[:, 22:23]))
        sV.wait(nc.scalar)
        nc.scalar.activation(out=scr[:, 26:27], in_=bst[:, 23:24], func=AF.Ln,
                             bias=LN_EPS, scale=1.0)
        sS.inc(nc.scalar.activation(out=ln2_rs[:, T:T + 1],
                                    in_=scr[:, 26:27], func=AF.Exp,
                                    scale=-0.5))
        sV.wait(nc.gpsimd)
        sX1O.dinc(nc.gpsimd.dma_start(
            out=t_x1s[T * 128:(T + 1) * 128, :], in_=x1t[:, :]))
        x1free = sX1O.mark()
        h2B = probs[:, 2048:3072]
        sS.wait(nc.vector)
        nc.vector.tensor_scalar(out=h2B, in0=x1t[:, :],
                                scalar1=ln2_m[:, T:T + 1],
                                scalar2=ln2_rs[:, T:T + 1],
                                op0=ALU.subtract, op1=ALU.mult)
        vsp()
        nc.vector.tensor_reduce(out=scr[:, 33:34], in_=h2B, axis=AX.X,
                                op=ALU.max, apply_absolute_value=True)
        vsp()
        nc.vector.tensor_scalar(out=h2clip[:, T:T + 1], in0=scr[:, 33:34],
                                scalar1=1e-5, scalar2=None, op0=ALU.max)
        vsp()
        nc.vector.reciprocal(out=scr[:, 34:35], in_=h2clip[:, T:T + 1])
        vsp()
        sV.inc(nc.vector.tensor_scalar(out=scr[:, 35:36], in0=scr[:, 34:35],
                                       scalar1=127.0, scalar2=None,
                                       op0=ALU.mult))
        lo = (len(wm_chunks) * T) // NT
        hi = (len(wm_chunks) * (T + 1)) // NT
        for j in range(lo, hi):
            ap, col, w_ = wm_chunks[j]
            sWM[j % 2].wait(nc.vector)
            nc.vector.tensor_reduce(out=scr[:, 30:31],
                                    in_=wmst[:, 2048 * (j % 2):
                                             2048 * (j % 2) + w_],
                                    axis=AX.X, op=ALU.add,
                                    apply_absolute_value=True)
            vsp()
            sV.inc(nc.vector.tensor_tensor(out=scr[:, col:col + 1],
                                           in0=scr[:, col:col + 1],
                                           in1=scr[:, 30:31], op=ALU.add))
            wmfree[j % 2] = sV.mark()

    for T in range(KNT + 1):
        if T < KNT:
            p1_b(T)
        if T >= 1:
            p2_b(T - 1)
    if KNT >= 2:
        xpose_b(KNT - 2)
    xquant_tail(KNT - 1)
    xpose_b(KNT - 1)
    sX1O.wait(nc.gpsimd)
    sQF.wait(nc.gpsimd)
    nc.all_engine_barrier()
    stB.close()
    stKV2.close()
    stKV.close()
    if KPH <= 3:
        return nc

    stTH = ExitStack()
    thr_g = stTH.enter_context(nc.sbuf_tensor("thr_g", [128, 12], F32))
    with nc.sbuf_tensor("onc", [128, 1], F32) as onc, \
         nc.sbuf_tensor("onr2", [1, 128], F32) as onr2, \
         nc.psum_tensor("pth", [128, 8], F32) as pth:
        nc.vector.memset(onc[:, :], 1.0)
        nc.vector.memset(onr2[:, :], 1.0)
        vsp()
        sV.inc(nc.vector.memset(scr[:, 63:64], 0.0))
        sV.wait(nc.tensor)
        for i_, col in enumerate((16, 17, 18)):
            i = nc.tensor.matmul(pth[0:1, i_:i_ + 1], scr[:, col:col + 1],
                                 onc[:, :], start=True, stop=True)
        sT.inc(i)
        sT.wait(nc.vector)
        sV.inc(nc.vector.tensor_scalar(out=scr[0:1, 32:35], in0=pth[0:1, 0:3],
                                       scalar1=1.0 / (D * INNER),
                                       scalar2=1e-5, op0=ALU.mult,
                                       op1=ALU.max))
        sV.wait(nc.tensor)
        sT.inc(nc.tensor.matmul(pth[:, 3:6], onr2[:, :], scr[0:1, 32:35],
                                start=True, stop=True))
        sT.wait(nc.vector)
        nc.vector.tensor_copy(out=thr_g[:, 3:6], in_=pth[:, 3:6])
        vsp()
        nc.vector.tensor_scalar(out=thr_g[:, 0:3], in0=thr_g[:, 3:6],
                                scalar1=0.5, scalar2=None, op0=ALU.mult)
        nc.vector.tensor_scalar(out=thr_g[:, 6:9], in0=thr_g[:, 3:6],
                                scalar1=-0.5, scalar2=None, op0=ALU.mult)
        vsp()
        sV.inc(nc.vector.memset(scr[:, 63:64], 0.0))
    sV.wait(nc.tensor)
    sV.wait(nc.scalar)
    sV.wait(nc.sync)
    sV.wait(nc.gpsimd)

    stF1 = ExitStack()
    wg_t = stF1.enter_context(nc.sbuf_tensor("wg_t", [128, NCH * INNER],
                                             BF16))
    wu_t = stF1.enter_context(nc.sbuf_tensor("wu_t", [128, NCH * INNER],
                                             BF16))
    pass

    def ternarize(wtst, tcmp, jobs):
        free = [None, None]
        k = 0
        for dst, src_t, tc, ntc, n_chunks, width in jobs:
            for c in range(n_chunks):
                for c0 in range(0, width, 2048):
                    s = k % 2
                    wmark(nc.sync, free[s])
                    sTD[s].dinc(nc.sync.dma_start(
                        out=wtst[s][:, :],
                        in_=src_t[c * 128:(c + 1) * 128, c0:c0 + 2048]))
                    sTD[s].wait(nc.vector)
                    nc.vector.tensor_scalar(out=tcmp[:, 0:2048],
                                            in0=wtst[s][:, :],
                                            scalar1=thr_g[:, tc:tc + 1],
                                            scalar2=None, op0=ALU.is_gt)
                    nc.vector.tensor_scalar(out=tcmp[:, 2048:4096],
                                            in0=wtst[s][:, :],
                                            scalar1=thr_g[:, ntc:ntc + 1],
                                            scalar2=None, op0=ALU.is_lt)
                    sV.inc(nc.vector.tensor_tensor(
                        out=dst[:, c * width + c0:c * width + c0 + 2048],
                        in0=tcmp[:, 0:2048], in1=tcmp[:, 2048:4096],
                        op=ALU.subtract))
                    free[s] = sV.mark()
                    k += 1

    with nc.sbuf_tensor("wtstA", [128, 2048], F32) as wtstA, \
         nc.sbuf_tensor("wtstB", [128, 2048], F32) as wtstB, \
         nc.sbuf_tensor("tcmpA", [128, 4096], F32) as tcmpA:
        ternarize([wtstA, wtstB], tcmpA,
                  [(wg_t, t_wgT, 0, 6, NCH, INNER),
                   (wu_t, t_wuT, 1, 7, NCH, INNER)])
    sV.wait(nc.tensor)
    sV.wait(nc.scalar)

    graw = stF1.enter_context(nc.sbuf_tensor("graw", [128, INNER], F32))
    silu_b = [stF1.enter_context(nc.sbuf_tensor(f"silu_b{i}", [128, 512],
                                                 F32)) for i in range(2)]
    qq = [stF1.enter_context(nc.sbuf_tensor(f"qq{i}", [128, 512], F32))
          for i in range(4)]
    qi = stF1.enter_context(nc.sbuf_tensor("qi", [128, 512], I32))
    qc = [stF1.enter_context(nc.sbuf_tensor(f"qc{i}", [128, 512], BF16))
          for i in range(2)]
    Gq = stF1.enter_context(nc.sbuf_tensor("Gq", [128, INNER], BF16))
    Xft2 = [stF1.enter_context(nc.sbuf_tensor(f"Xft2{i}", [128, D], BF16))
            for i in range(2)]
    gqf = [stF1.enter_context(nc.sbuf_tensor("gqf0", [128, INNER], BF16))] * 2
    mx8 = stF1.enter_context(nc.sbuf_tensor("mx8", [128, 16], F32))

    pg = [stF1.enter_context(nc.psum_tensor(f"pg{i}", [128, 512], F32))
          for i in range(2)]
    pu = [stF1.enter_context(nc.psum_tensor(f"pu{i}", [128, 512], F32))
          for i in range(2)]
    pgT = [stF1.enter_context(nc.psum_tensor(f"pgT{i}", [128, 1024], BF16))
           for i in range(2)]

    gqfree = [None, None]
    egmk = [None, None]
    gq_mk = None

    def p1_f1(T):
        sl = T % 2
        base = 40 - 8 * sl
        wmark(nc.sync, pamk[sl])
        SLX[sl].dinc(nc.sync.dma_start(out=Xft2[sl][:, :],
                                       in_=t_xfs[T, :, :]))
        nc.vector.tensor_tensor(out=scr[:, base + 4:base + 5],
                                in0=h2clip[:, T:T + 1],
                                in1=thr_g[:, 3:4], op=ALU.mult)
        nc.vector.tensor_tensor(out=scr[:, base + 5:base + 6],
                                in0=h2clip[:, T:T + 1],
                                in1=thr_g[:, 4:5], op=ALU.mult)
        vsp()
        sV.inc(nc.vector.tensor_scalar(out=scr[:, base + 6:base + 8],
                                       in0=scr[:, base + 4:base + 6],
                                       scalar1=1.0 / 127.0, scalar2=None,
                                       op0=ALU.mult))
        egmk[sl] = sV.mark()

    def pA_f1(T):
        sl = T % 2
        base = 40 - 8 * sl
        Xft = Xft2[sl]
        SLX[sl].wait(nc.tensor)
        wmark(nc.scalar, egmk[sl])
        evmk = [None, None]
        for ob in range(8):
            g_, u_ = pg[ob % 2], pu[ob % 2]
            wmark(nc.tensor, evmk[ob % 2])
            for c in range(NCH):
                nc.tensor.matmul(g_[:, :], Xft[:, c * 128:(c + 1) * 128],
                                 wg_t[:, c * INNER + ob * 512:
                                      c * INNER + (ob + 1) * 512],
                                 start=(c == 0), stop=(c == NCH - 1))
            for c in range(NCH):
                i = nc.tensor.matmul(u_[:, :], Xft[:, c * 128:(c + 1) * 128],
                                     wu_t[:, c * INNER + ob * 512:
                                          c * INNER + (ob + 1) * 512],
                                     start=(c == 0), stop=(c == NCH - 1))
            sT.inc(i)
            if ob == 7:
                pamk[sl] = sT.mark()
            sT.wait(nc.scalar)
            sS.inc(nc.scalar.activation(out=silu_b[ob % 2][:, :],
                                        in_=g_[:, :], func=AF.Silu,
                                        scale=scr[:, base + 6:base + 7]))
            sS.wait(nc.vector)
            nc.vector.tensor_tensor(out=graw[:, ob * 512:(ob + 1) * 512],
                                    in0=silu_b[ob % 2][:, :], in1=u_[:, :],
                                    op=ALU.mult)
            vsp()
            sV.inc(nc.vector.tensor_reduce(
                out=mx8[:, ob:ob + 1], in_=graw[:, ob * 512:(ob + 1) * 512],
                axis=AX.X, op=ALU.max, apply_absolute_value=True))
            evmk[ob % 2] = sV.mark()
        vsp()
        nc.vector.reduce_max(out=scr[:, 48:49], in_=mx8[:, 0:8], axis=AX.X)
        vsp()
        nc.vector.tensor_tensor(out=scr[:, 49:50], in0=scr[:, 48:49],
                                in1=scr[:, base + 7:base + 8], op=ALU.mult)
        vsp()
        nc.vector.tensor_scalar(out=scr[:, 50:51], in0=scr[:, 49:50],
                                scalar1=1e-5, scalar2=None, op0=ALU.max)
        vsp()
        nc.vector.reciprocal(out=scr[:, 51:52], in_=scr[:, 50:51])
        nc.vector.tensor_tensor(out=scr[:, 54:55], in0=scr[:, 50:51],
                                in1=thr_g[:, 5:6], op=ALU.mult)
        vsp()
        nc.vector.tensor_scalar(out=scr[:, 52:53], in0=scr[:, 51:52],
                                scalar1=127.0, scalar2=None, op0=ALU.mult)
        nc.vector.tensor_scalar(out=ed_buf[:, T:T + 1], in0=scr[:, 54:55],
                                scalar1=1.0 / 127.0, scalar2=None,
                                op0=ALU.mult)
        vsp()
        sV.inc(nc.vector.tensor_tensor(out=scr[:, 53:54], in0=scr[:, 52:53],
                                       in1=scr[:, base + 7:base + 8],
                                       op=ALU.mult))

    def pC_f1(T):
        nonlocal gq_mk
        sV.wait(nc.scalar)
        wmark(nc.vector, gqT_mk[0])
        gdone = [None, None]
        for ob in range(8):
            gb = graw[:, ob * 512:(ob + 1) * 512]
            r = ob % 2
            wmark(nc.scalar, gdone[r])
            nc.scalar.activation(out=qi[:, :], in_=gb, func=AF.Copy,
                                 scale=scr[:, 53:54], bias=-0.5)
            nc.scalar.activation(out=qq[2 * r][:, :], in_=gb, func=AF.Copy,
                                 scale=scr[:, 53:54])
            sS.inc(nc.scalar.activation(out=qq[2 * r + 1][:, :],
                                        in_=qi[:, :], func=AF.Copy,
                                        bias=0.5))
            sS.wait(nc.vector)
            nc.vector.tensor_tensor(out=qc[r][:, :], in0=qq[2 * r][:, :],
                                    in1=qq[2 * r + 1][:, :], op=ALU.is_gt)
            vsp()
            sV.inc(nc.vector.scalar_tensor_tensor(
                out=Gq[:, ob * 512:(ob + 1) * 512], in0=qc[r][:, :],
                scalar=-0.5, in1=qq[2 * r + 1][:, :],
                op0=ALU.add, op1=ALU.add))
            gdone[r] = sV.mark()
        gq_mk = sV.mark()

    def pB_f1(T):
        sl = T % 2
        g_out = gqf[sl]
        wmark(nc.tensor, gq_mk_t[T])
        for grp in range(4):
            pb = pgT[grp % 2]
            wmark(nc.tensor, evmk2[grp % 2])
            for cc2 in range(8):
                c = grp * 8 + cc2
                i = nc.tensor.transpose(pb[:, cc2 * 128:(cc2 + 1) * 128],
                                        Gq[:, c * 128:(c + 1) * 128],
                                        ident[:, :])
            sT.inc(i)
            if grp == 3:
                gqT_mk[0] = sT.mark()
            sT.wait(nc.scalar)
            wmark(nc.scalar, gqfree[sl])
            sS.inc(nc.scalar.copy(
                out=g_out[:, grp * 1024:(grp + 1) * 1024], in_=pb[:, :]))
            evmk2[grp % 2] = sS.mark()
            sS.wait(nc.gpsimd)
            sGQ.dinc(nc.gpsimd.dma_start(
                out=t_gqs[T, :, grp * 1024:(grp + 1) * 1024],
                in_=g_out[:, grp * 1024:(grp + 1) * 1024]))
        gqfree[sl] = sGQ.mark()

    gq_mk_t = {}
    pamk = [None, None]
    gqT_mk = [None]
    evmk2 = [None, None]
    for T in range(KNT + 2):
        if T < KNT:
            p1_f1(T)
        if 1 <= T <= KNT:
            pA_f1(T - 1)
        if 2 <= T <= KNT + 1:
            pB_f1(T - 2)
        if 1 <= T <= KNT:
            pC_f1(T - 1)
            gq_mk_t[T - 1] = gq_mk
    sGQ.wait(nc.gpsimd)
    nc.all_engine_barrier()
    stF1.close()
    if KPH <= 4:
        return nc

    stF2 = ExitStack()
    wd_t = stF2.enter_context(nc.sbuf_tensor("wd_t", [128, NO * D], BF16))
    wtsd = [stF2.enter_context(nc.sbuf_tensor(f"wtsd{i}", [128, D], F32))
            for i in range(2)]
    tcm2 = stF2.enter_context(nc.sbuf_tensor("tcm2", [128, 2048], F32))
    gq_in = [stF2.enter_context(nc.sbuf_tensor(f"gqi{i}", [128, INNER],
                                               BF16)) for i in range(2)]
    y3 = stF2.enter_context(nc.sbuf_tensor("y3", [128, D], F32))
    outt = [stF2.enter_context(nc.sbuf_tensor(f"outt{i}", [128, D], F32))
            for i in range(2)]
    bs2 = stF2.enter_context(nc.sbuf_tensor("bs2", [128, 16], F32))

    pass

    sWD = Sem("wd")
    wdfree = [None, None]
    for c in range(NO):
        s = c % 2
        wmark(nc.sync, wdfree[s])
        sTD[s].dinc(nc.sync.dma_start(out=wtsd[s][:, :],
                                      in_=t_wdT[c * 128:(c + 1) * 128, :]))
        sTD[s].wait(nc.vector)
        nc.vector.tensor_scalar(out=tcm2[:, 0:1024], in0=wtsd[s][:, :],
                                scalar1=thr_g[:, 2:3], scalar2=None,
                                op0=ALU.is_gt)
        nc.vector.tensor_scalar(out=tcm2[:, 1024:2048], in0=wtsd[s][:, :],
                                scalar1=thr_g[:, 8:9], scalar2=None,
                                op0=ALU.is_lt)
        sWD.inc(nc.vector.tensor_tensor(out=wd_t[:, c * D:(c + 1) * D],
                                        in0=tcm2[:, 0:1024],
                                        in1=tcm2[:, 1024:2048],
                                        op=ALU.subtract))
        wdfree[s] = sWD.mark()

    if tap == "B0":
        nc.vector.wait_ge(sWD.h, NO)
        vsp()
        sV.inc(nc.vector.memset(scr[:, 63:64], 0.0))
        sV.wait(nc.gpsimd)
        sFIN.dinc(nc.gpsimd.dma_start(out=t_dWD[:, :], in_=wd_t[:, :]))
        sFIN.dinc(nc.gpsimd.dma_start(out=t_dED[:, :], in_=ed_buf[:, :]))
        sFIN.dinc(nc.gpsimd.dma_start(out=t_dTH[:, :], in_=thr_g[:, 0:9]))
    pd = [stF2.enter_context(nc.psum_tensor(f"pd{i}", [128, 1024], F32))
          for i in range(2)]
    xcons = [None, None]
    gcons = [None, None]
    ofree = [None, None]
    ofin = [None, None]
    for T in range(KNT + 1):
        if T < KNT:
            sl = T % 2
            wmark(nc.sync, gcons[sl])
            SLK[sl].dinc(nc.sync.dma_start(out=gq_in[sl][:, :],
                                           in_=t_gqs[T, :, :]))
            wmark(nc.sync, xcons[sl])
            SLX[sl].dinc(nc.sync.dma_start(
                out=xin[sl][:, :], in_=t_x1s[T * 128:(T + 1) * 128, :]))
        if T == 0:
            continue
        Tp, sl = T - 1, (T - 1) % 2
        p_ = pd[sl]
        SLK[sl].wait(nc.tensor)
        wmark(nc.tensor, ofree[sl])
        gq_c = gq_in[sl]
        for c in range(NO):
            if Tp == 0:
                nc.tensor.wait_ge(sWD.h, c + 1)
            nc.tensor.matmul(p_[:, 0:512], gq_c[:, c * 128:(c + 1) * 128],
                             wd_t[:, c * D:c * D + 512],
                             start=(c == 0), stop=(c == NO - 1))
            i = nc.tensor.matmul(p_[:, 512:1024],
                                 gq_c[:, c * 128:(c + 1) * 128],
                                 wd_t[:, c * D + 512:(c + 1) * D],
                                 start=(c == 0), stop=(c == NO - 1))
        sT.inc(i)
        gcons[sl] = sT.mark()
        sT.wait(nc.scalar)
        sS.inc(nc.scalar.activation(out=y3[:, :], in_=p_[:, :],
                                    func=AF.Identity,
                                    scale=ed_buf[:, Tp:Tp + 1]))
        ofree[sl] = sS.mark()
        if tap == "B0" and Tp == 0:
            sS.wait(nc.gpsimd)
            sFIN.dinc(nc.gpsimd.dma_start(out=t_dY3[:, :], in_=y3[:, :]))
        sS.wait(nc.vector)
        nc.vector.bn_stats(bs2[:, 0:6], y3[:, 0:512])
        nc.vector.bn_stats(bs2[:, 6:12], y3[:, 512:1024])
        vsp()
        nc.vector.bn_aggr(bs2[:, 12:14], bs2[:, 0:12])
        vsp()
        sV.inc(nc.vector.memset(scr[:, 63:64], 0.0))
        sV.wait(nc.scalar)
        nc.scalar.activation(out=scr[:, 56:57], in_=bs2[:, 13:14], func=AF.Ln,
                             bias=LN_EPS, scale=1.0)
        sS.inc(nc.scalar.activation(out=scr[:, 57:58], in_=scr[:, 56:57],
                                    func=AF.Exp, scale=-0.5))
        sS.wait(nc.vector)
        SLX[sl].wait(nc.vector)
        nc.vector.tensor_scalar(out=y3[:, :], in0=y3[:, :],
                                scalar1=bs2[:, 12:13], scalar2=scr[:, 57:58],
                                op0=ALU.subtract, op1=ALU.mult)
        vsp()
        wmark(nc.vector, ofin[sl])
        sV.inc(nc.vector.tensor_tensor(out=outt[sl][:, :], in0=y3[:, :],
                                       in1=xin[sl][:, :], op=ALU.add))
        xcons[sl] = sV.mark()
        sV.wait(nc.gpsimd)
        sFIN.dinc(nc.gpsimd.dma_start(
            out=t_out[Tp * 128:(Tp + 1) * 128, :], in_=outt[sl][:, :]))
        ofin[sl] = sFIN.mark()

    if tap == "x1":
        for T in range(NT):
            sFIN.dinc(nc.sync.dma_start(
                out=t_dbg[T * 128:(T + 1) * 128, :],
                in_=t_x1s[T * 128:(T + 1) * 128, :]))
    sFIN.wait(nc.gpsimd)
    stF2.close()
    stTH.close()
    es.close()
    return nc


_NC_CACHE = {}


def _get_nc(tap):
    if tap not in _NC_CACHE:
        _NC_CACHE[tap] = _build(tap)
    return _NC_CACHE[tap]


def _host_tables():
    j = np.arange(512, dtype=np.float64)
    inv = np.exp(-C_INV * j)
    t = np.arange(S, dtype=np.float64)
    ang = t[:, None] * inv[None, :]
    ks = np.round(ang / (2 * math.pi)).astype(np.float32)
    kc = np.round((ang + PI_2) / (2 * math.pi)).astype(np.float32)
    return np.ascontiguousarray(ks), np.ascontiguousarray(kc)


def kernel(L0_attn_x, L0_attn_wq, L0_attn_wk, L0_attn_wv, L0_attn_wo,
           L0_ffn_w_gate, L0_ffn_w_up, L0_ffn_w_down):
    x = np.ascontiguousarray(np.asarray(L0_attn_x, dtype=np.float32))
    wq = np.asarray(L0_attn_wq, dtype=np.float32)
    wk = np.asarray(L0_attn_wk, dtype=np.float32)
    wv = np.asarray(L0_attn_wv, dtype=np.float32)
    wo = np.asarray(L0_attn_wo, dtype=np.float32)
    wg = np.asarray(L0_ffn_w_gate, dtype=np.float32)
    wu = np.asarray(L0_ffn_w_up, dtype=np.float32)
    wd = np.asarray(L0_ffn_w_down, dtype=np.float32)

    perm = np.concatenate([np.arange(0, D, 2), np.arange(1, D, 2)])
    common = dict(
        wqT=np.ascontiguousarray(wq[perm, :].T),
        wkT=np.ascontiguousarray(wk[perm, :].T),
        wvT=np.ascontiguousarray(wv.T),
        woT=np.ascontiguousarray(wo.T),
        wgT=np.ascontiguousarray(wg.T),
        wuT=np.ascontiguousarray(wu.T),
        wdT=np.ascontiguousarray(wd.T),
        idn=np.eye(128, dtype=np.float32),
    )
    common["ks"], common["kc"] = _host_tables()

    tap = _DEBUG_TAP
    nc = _get_nc(tap)
    in_maps = [dict(x=np.ascontiguousarray(x[b]), **common) for b in range(8)]
    res = run_bass_kernel_spmd(nc, in_maps, core_ids=list(range(8)))
    out = np.stack([np.asarray(r["out"]) for r in res.results], axis=0)
    if tap is not None:
        kernel._dbg = np.stack([np.asarray(r["dbg"]) for r in res.results], 0)
    return out


# version 28
# speedup vs baseline: 1.1172x; 1.0242x over previous
import math
import os
from contextlib import ExitStack

import numpy as np

import concourse.bass as bass
import concourse.mybir as mybir
from concourse.bass_utils import run_bass_kernel_spmd

F32 = mybir.dt.float32
BF16 = mybir.dt.bfloat16
I32 = mybir.dt.int32
AF = mybir.ActivationFunctionType
ALU = mybir.AluOpType
AX = mybir.AxisListType

S = 4096
D = 1024
INNER = 4096
NT = S // 128
NCH = D // 128
NO = INNER // 128
LN_EPS = 1e-5
INVSQ = 1.0 / 32.0
C_INV = math.log(10000.0) / 512.0
_CW1 = float(np.float32(6.28125))
_CW2 = float(np.float32(2 * math.pi - _CW1))
_CW3 = float(np.float32(2 * math.pi - _CW1 - float(_CW2)))
PI = math.pi
PI_2 = math.pi / 2.0

_DEBUG_TAP = None
KNT = int(os.environ.get("KNT", "32"))
KPH = int(os.environ.get("KPH", "9"))


def _build(tap=None):
    nc = bass.Bass("TRN2", target_bir_lowering=False,
                   detect_race_conditions=(os.environ.get("KDRC") == "1"))
    es = ExitStack()

    t_x = nc.dram_tensor("x", [S, D], F32, kind="ExternalInput").ap()
    t_wqT = nc.dram_tensor("wqT", [D, D], F32, kind="ExternalInput").ap()
    t_wkT = nc.dram_tensor("wkT", [D, D], F32, kind="ExternalInput").ap()
    t_wvT = nc.dram_tensor("wvT", [D, D], F32, kind="ExternalInput").ap()
    t_woT = nc.dram_tensor("woT", [D, D], F32, kind="ExternalInput").ap()
    t_wgT = nc.dram_tensor("wgT", [D, INNER], F32, kind="ExternalInput").ap()
    t_wuT = nc.dram_tensor("wuT", [D, INNER], F32, kind="ExternalInput").ap()
    t_wdT = nc.dram_tensor("wdT", [INNER, D], F32, kind="ExternalInput").ap()
    t_ks = nc.dram_tensor("ks", [S, 512], F32, kind="ExternalInput").ap()
    t_kc = nc.dram_tensor("kc", [S, 512], F32, kind="ExternalInput").ap()
    t_id = nc.dram_tensor("idn", [128, 128], F32, kind="ExternalInput").ap()
    t_out = nc.dram_tensor("out", [S, D], F32, kind="ExternalOutput").ap()
    _ik = "ExternalOutput" if tap == "B0" else "Internal"
    t_qfs = nc.dram_tensor("qfs", [NT, 128, D], BF16, kind=_ik).ap()
    t_x1s = nc.dram_tensor("x1s", [S, D], F32, kind=_ik).ap()
    t_gqs = nc.dram_tensor("gqs", [NT, 128, INNER], BF16, kind=_ik).ap()
    t_xfs = nc.dram_tensor("xfs", [NT, 128, D], BF16, kind=_ik).ap()
    t_dbg = (nc.dram_tensor("dbg", [S, D], F32, kind="ExternalOutput").ap()
             if tap == "x1" else None)
    if tap == "B0":
        t_dK = nc.dram_tensor("dK", [128, NCH * S], BF16, kind=_ik).ap()
        t_dV = nc.dram_tensor("dV", [128, NT * D], BF16, kind=_ik).ap()
        t_dP = nc.dram_tensor("dP", [128, S], BF16, kind=_ik).ap()
        t_dPT = nc.dram_tensor("dPT", [128, S], BF16, kind=_ik).ap()
        t_dAV = nc.dram_tensor("dAV", [128, D], BF16, kind=_ik).ap()
        t_dY = nc.dram_tensor("dY", [128, D], F32, kind=_ik).ap()
        t_dB = nc.dram_tensor("dB", [128, 10], F32, kind=_ik).ap()
        t_dWD = nc.dram_tensor("dWD", [128, NO * D], BF16, kind=_ik).ap()
        t_dED = nc.dram_tensor("dED", [128, NT], F32, kind=_ik).ap()
        t_dY3 = nc.dram_tensor("dY3", [128, D], F32, kind=_ik).ap()
        t_dTH = nc.dram_tensor("dTH", [128, 9], F32, kind=_ik).ap()
        t_dOF = nc.dram_tensor("dOF", [128, D], BF16, kind=_ik).ap()
        t_dWO = nc.dram_tensor("dWO", [128, NCH * D], BF16, kind=_ik).ap()

    class Sem:
        __slots__ = ("h", "n")

        def __init__(self, name):
            self.h = es.enter_context(nc.semaphore(name))
            self.n = 0

        def inc(self, instr, k=1):
            instr.then_inc(self.h, k)
            self.n += k
            return instr

        def dinc(self, instr):
            return self.inc(instr, 16)

        def wait(self, eng, n=None):
            eng.wait_ge(self.h, self.n if n is None else n)

        def mark(self):
            return (self, self.n)

    def wmark(eng, mk):
        if mk is not None:
            mk[0].wait(eng, mk[1])

    sX0, sX1 = Sem("x0"), Sem("x1")
    sK0, sK1 = Sem("k0"), Sem("k1")
    sW = Sem("w")
    sT = Sem("t")
    sV = Sem("v")
    sS = Sem("s")
    sG = Sem("g")
    sQF = Sem("qf")
    sX1O = Sem("x1o")
    sGQ = Sem("gq")
    sFIN = Sem("fin")
    sVSP = Sem("vsp")
    sTD = (Sem("td0"), Sem("td1"))
    sWM = (Sem("wm0"), Sem("wm1"))
    SLX = (sX0, sX1)
    SLK = (sK0, sK1)

    cpool = es.enter_context(nc.sbuf_tensor("cpool", [128, 16], F32))
    ident = es.enter_context(nc.sbuf_tensor("ident", [128, 128], BF16))
    ln1_m = es.enter_context(nc.sbuf_tensor("ln1_m", [128, NT], F32))
    ln1_rs = es.enter_context(nc.sbuf_tensor("ln1_rs", [128, NT], F32))
    ln2_m = es.enter_context(nc.sbuf_tensor("ln2_m", [128, NT], F32))
    ln2_rs = es.enter_context(nc.sbuf_tensor("ln2_rs", [128, NT], F32))
    q2s = es.enter_context(nc.sbuf_tensor("q2s", [128, NT], F32))
    ed_buf = es.enter_context(nc.sbuf_tensor("ed_buf", [128, NT], F32))
    h2clip = es.enter_context(nc.sbuf_tensor("h2clip", [128, NT], F32))
    scr = es.enter_context(nc.sbuf_tensor("scr", [128, 64], F32))
    xin = [es.enter_context(nc.sbuf_tensor(f"xin{i}", [128, D], F32))
           for i in range(2)]

    consts = {}

    def creg(v):
        v = float(v)
        if v not in consts:
            i = len(consts)
            assert i < 16
            nc.vector.memset(cpool[:, i:i + 1], v)
            nc.const_aps.register(F32, v, cpool[:, i:i + 1])
            consts[v] = i

    for v in (LN_EPS, -0.5, 0.5, INVSQ, -C_INV, PI_2, -PI, PI,
              2 * PI, 0.0, 1.0, -1.0):
        creg(v)

    def vsp():
        sVSP.inc(nc.vector.memset(scr[:, 63:64], 0.0))
        nc.vector.wait_ge(sVSP.h, sVSP.n)

    vsp()
    sV.inc(nc.vector.memset(scr[:, 63:64], 0.0))
    pass
    with nc.sbuf_tensor("idf", [128, 128], F32) as idf, \
         nc.sbuf_tensor("pst", [128, 16], F32) as pst:
        sW.dinc(nc.sync.dma_start(out=idf[:, :], in_=t_id[:, :]))
        nc.gpsimd.iota(scr[:, 0:1], pattern=[[0, 1]], base=0,
                       channel_multiplier=1,
                       allow_small_or_imprecise_dtypes=True)
        sG.inc(nc.gpsimd.engine_nop())
        sW.wait(nc.vector)
        sV.inc(nc.vector.tensor_copy(out=ident[:, :], in_=idf[:, :]))

        xcons = [None, None]
        for T in range(KNT + 1):
            if T < KNT:
                sl = T % 2
                wmark(nc.sync, xcons[sl])
                SLX[sl].dinc(nc.sync.dma_start(
                    out=xin[sl][:, :], in_=t_x[T * 128:(T + 1) * 128, :]))
            if T == 0:
                continue
            Tp, sl = T - 1, (T - 1) % 2
            SLX[sl].wait(nc.vector)
            xt = xin[sl]
            nc.vector.bn_stats(pst[:, 0:6], xt[:, 0:512])
            nc.vector.bn_stats(pst[:, 6:12], xt[:, 512:1024])
            vsp()
            nc.vector.bn_aggr(pst[:, 12:14], pst[:, 0:12])
            vsp()
            sV.inc(nc.vector.tensor_copy(out=ln1_m[:, Tp:Tp + 1],
                                         in_=pst[:, 12:13]))
            xcons[sl] = sV.mark()
            sV.wait(nc.scalar)
            nc.scalar.activation(out=pst[:, 14:15], in_=pst[:, 13:14],
                                 func=AF.Ln, bias=LN_EPS, scale=1.0)
            sS.inc(nc.scalar.activation(out=ln1_rs[:, Tp:Tp + 1],
                                        in_=pst[:, 14:15],
                                        func=AF.Exp, scale=-0.5))
        sS.wait(nc.vector)
        nc.all_engine_barrier()
    if KPH <= 0:
        return nc

    def stage_bf16(dst, src_t, n_chunks, width, stg):
        for c in range(n_chunks):
            sV.wait(nc.sync)
            sW.dinc(nc.sync.dma_start(
                out=stg[:, 0:width], in_=src_t[c * 128:(c + 1) * 128, :]))
            sW.wait(nc.vector)
            sV.inc(nc.vector.tensor_copy(out=dst[:, c * width:(c + 1) * width],
                                         in_=stg[:, 0:width]))

    stKV = ExitStack()
    K_feat = stKV.enter_context(nc.sbuf_tensor("K_feat", [128, NCH * S], BF16))

    stA = ExitStack()
    inv_row = stA.enter_context(nc.sbuf_tensor("inv_row", [128, 512], F32))
    wq_bf = stA.enter_context(nc.sbuf_tensor("wq_bf", [128, NCH * D], BF16))
    wk_bf = stA.enter_context(nc.sbuf_tensor("wk_bf", [128, NCH * D], BF16))
    kin = stA.enter_context(nc.sbuf_tensor("kin", [128, 1024], F32))
    h_bf = [stA.enter_context(nc.sbuf_tensor(f"h_bf{i}", [128, D], BF16))
            for i in range(2)]
    h_feat = stA.enter_context(nc.sbuf_tensor("h_feat", [128, D], BF16))
    q_bf = [stA.enter_context(nc.sbuf_tensor(f"q_bf{i}", [128, D], BF16))
            for i in range(2)]
    k_bf = [stA.enter_context(nc.sbuf_tensor(f"k_bf{i}", [128, D], BF16))
            for i in range(2)]
    q_rot = stA.enter_context(nc.sbuf_tensor("q_rot", [128, D], BF16))
    k_rot = stA.enter_context(nc.sbuf_tensor("k_rot", [128, D], BF16))
    q_feat = stA.enter_context(nc.sbuf_tensor("q_feat", [128, D], BF16))
    angs = stA.enter_context(nc.sbuf_tensor("angs", [128, 512], F32))
    angc = stA.enter_context(nc.sbuf_tensor("angc", [128, 512], F32))
    cwa = stA.enter_context(nc.sbuf_tensor("cwa", [128, 512], F32))
    cwb = stA.enter_context(nc.sbuf_tensor("cwb", [128, 512], F32))
    cosb = [stA.enter_context(nc.sbuf_tensor(f"cosb{i}", [128, 512], BF16))
            for i in range(3)]
    sinb = [stA.enter_context(nc.sbuf_tensor(f"sinb{i}", [128, 512], BF16))
            for i in range(3)]
    sq_bf = stA.enter_context(nc.sbuf_tensor("sq_bf", [128, D], BF16))
    m14 = stA.enter_context(nc.sbuf_tensor("m14", [128, 2048], BF16))

    sG.inc(nc.gpsimd.iota(inv_row[:, :], pattern=[[1, 512]], base=0,
                          channel_multiplier=0,
                          allow_small_or_imprecise_dtypes=True))
    sG.wait(nc.scalar)
    sS.inc(nc.scalar.activation(out=inv_row[:, :], in_=inv_row[:, :],
                                func=AF.Exp, scale=-C_INV))
    pass
    stage_bf16(wq_bf, t_wqT, NCH, D, xin[0])
    stage_bf16(wk_bf, t_wkT, NCH, D, xin[0])
    sV.wait(nc.tensor)
    sV.wait(nc.scalar)
    sV.wait(nc.sync)

    paH = stA.enter_context(nc.psum_tensor("paH", [128, 1024], BF16))
    paQ = stA.enter_context(nc.psum_tensor("paQ", [128, 1024], F32))
    paK = stA.enter_context(nc.psum_tensor("paK", [128, 1024], F32))
    paT = stA.enter_context(nc.psum_tensor("paT", [128, 1024], BF16))
    paR = stA.enter_context(nc.psum_tensor("paR", [128, 1024], BF16))

    xcons = [None, None]
    kcons = None
    qfree = None
    evdone = None

    def p1_a1(T):
        nonlocal kcons
        sl = T % 2
        wmark(nc.sync, xcons[sl])
        SLX[sl].dinc(nc.sync.dma_start(
            out=xin[sl][:, :], in_=t_x[T * 128:(T + 1) * 128, :]))
        wmark(nc.sync, kcons)
        SLK[0].dinc(nc.sync.dma_start(
            out=kin[:, 0:512], in_=t_ks[T * 128:(T + 1) * 128, :]))
        SLK[0].dinc(nc.sync.dma_start(
            out=kin[:, 512:1024], in_=t_kc[T * 128:(T + 1) * 128, :]))
        SLX[sl].wait(nc.vector)
        sV.inc(nc.vector.tensor_scalar(
            out=h_bf[sl][:, :], in0=xin[sl][:, :],
            scalar1=ln1_m[:, T:T + 1], scalar2=ln1_rs[:, T:T + 1],
            op0=ALU.subtract, op1=ALU.mult))
        hmk[sl] = sV.mark()
        xcons[sl] = sV.mark()
        tc = 1 + 3 * sl
        nc.vector.tensor_scalar(out=scr[:, tc:tc + 1], in0=scr[:, 0:1],
                                scalar1=float(128 * T), scalar2=None,
                                op0=ALU.add)
        vsp()
        sV.inc(nc.vector.memset(scr[:, 63:64], 0.0))
        sV.wait(nc.scalar)
        nc.scalar.activation(out=angs[:, :], in_=inv_row[:, :], func=AF.Copy,
                             scale=scr[:, tc:tc + 1])
        sS.inc(nc.scalar.activation(out=angc[:, :], in_=inv_row[:, :],
                                    func=AF.Identity, scale=scr[:, tc:tc + 1],
                                    bias=PI_2))
        SLK[0].wait(nc.vector)
        sS.wait(nc.vector)
        nc.vector.scalar_tensor_tensor(out=cwa[:, :], in0=kin[:, 0:512],
                                       scalar=-_CW1, in1=angs[:, :],
                                       op0=ALU.mult, op1=ALU.add)
        nc.vector.scalar_tensor_tensor(out=cwb[:, :], in0=kin[:, 512:1024],
                                       scalar=-_CW1, in1=angc[:, :],
                                       op0=ALU.mult, op1=ALU.add)
        nc.vector.scalar_tensor_tensor(out=angs[:, :], in0=kin[:, 0:512],
                                       scalar=-_CW2, in1=cwa[:, :],
                                       op0=ALU.mult, op1=ALU.add)
        nc.vector.scalar_tensor_tensor(out=angc[:, :], in0=kin[:, 512:1024],
                                       scalar=-_CW2, in1=cwb[:, :],
                                       op0=ALU.mult, op1=ALU.add)
        nc.vector.scalar_tensor_tensor(out=cwa[:, :], in0=kin[:, 0:512],
                                       scalar=-_CW3, in1=angs[:, :],
                                       op0=ALU.mult, op1=ALU.add)
        sV.inc(nc.vector.scalar_tensor_tensor(
            out=cwb[:, :], in0=kin[:, 512:1024], scalar=-_CW3,
            in1=angc[:, :], op0=ALU.mult, op1=ALU.add))
        kcons = sV.mark()
        s3 = T % 3
        sV.wait(nc.scalar)
        wmark(nc.scalar, cs_cons[s3])
        nc.scalar.activation(out=angs[:, :], in_=cwa[:, :], func=AF.Relu,
                             scale=-1.0, bias=PI)
        nc.scalar.activation(out=angc[:, :], in_=cwb[:, :], func=AF.Relu,
                             scale=-1.0, bias=PI)
        nc.scalar.activation(out=cwa[:, :], in_=angs[:, :], func=AF.Relu,
                             scale=-1.0, bias=2 * PI)
        nc.scalar.activation(out=cwb[:, :], in_=angc[:, :], func=AF.Relu,
                             scale=-1.0, bias=2 * PI)
        nc.scalar.activation(out=sinb[s3][:, :], in_=cwa[:, :], func=AF.Sin,
                             bias=-PI, scale=1.0)
        sS.inc(nc.scalar.activation(out=cosb[s3][:, :], in_=cwb[:, :],
                                    func=AF.Sin, bias=-PI, scale=1.0))
        csmk[s3] = sS.mark()

    def pM_a1(T):
        nonlocal evdone
        sl = T % 2
        wmark(nc.tensor, hmk[sl])
        wmark(nc.tensor, evdone)
        wmark(nc.tensor, ropemk[sl])
        for c in range(NCH):
            i = nc.tensor.transpose(paH[:, c * 128:(c + 1) * 128],
                                    h_bf[sl][:, c * 128:(c + 1) * 128],
                                    ident[:, :])
        sT.inc(i)
        sT.wait(nc.scalar)
        sS.inc(nc.scalar.copy(out=h_feat[:, :], in_=paH[:, :]))
        sS.wait(nc.tensor)
        for c in range(NCH):
            lhs = h_feat[:, c * 128:(c + 1) * 128]
            nc.tensor.matmul(paQ[:, 0:512], lhs, wq_bf[:, c * D:c * D + 512],
                             start=(c == 0), stop=(c == NCH - 1))
            nc.tensor.matmul(paQ[:, 512:1024], lhs,
                             wq_bf[:, c * D + 512:(c + 1) * D],
                             start=(c == 0), stop=(c == NCH - 1))
        for c in range(NCH):
            lhs = h_feat[:, c * 128:(c + 1) * 128]
            nc.tensor.matmul(paK[:, 0:512], lhs, wk_bf[:, c * D:c * D + 512],
                             start=(c == 0), stop=(c == NCH - 1))
            i = nc.tensor.matmul(paK[:, 512:1024], lhs,
                                 wk_bf[:, c * D + 512:(c + 1) * D],
                                 start=(c == 0), stop=(c == NCH - 1))
        sT.inc(i)
        sT.wait(nc.scalar)
        nc.scalar.copy(out=q_bf[sl][:, :], in_=paQ[:, :])
        sS.inc(nc.scalar.copy(out=k_bf[sl][:, :], in_=paK[:, :]))
        qkmk[sl] = sS.mark()

    def pR_a1(T):
        nonlocal qfree, evdone, qkT
        sl = T % 2
        s3 = T % 3
        wmark(nc.vector, qkmk[sl])
        wmark(nc.vector, csmk[s3])
        wmark(nc.vector, qkT)
        for srcb, dst in ((q_bf[sl], q_rot), (k_bf[sl], k_rot)):
            xr, xi = srcb[:, 0:512], srcb[:, 512:1024]
            nc.vector.tensor_tensor(out=m14[:, 0:512], in0=xr,
                                    in1=cosb[s3][:, :], op=ALU.mult)
            nc.vector.tensor_tensor(out=m14[:, 512:1024], in0=xi,
                                    in1=sinb[s3][:, :], op=ALU.mult)
            nc.vector.tensor_tensor(out=m14[:, 1024:1536], in0=xr,
                                    in1=sinb[s3][:, :], op=ALU.mult)
            nc.vector.tensor_tensor(out=m14[:, 1536:2048], in0=xi,
                                    in1=cosb[s3][:, :], op=ALU.mult)
            nc.vector.tensor_tensor(out=dst[:, 0:512], in0=m14[:, 0:512],
                                    in1=m14[:, 512:1024], op=ALU.subtract)
            i = nc.vector.tensor_tensor(out=dst[:, 512:1024],
                                        in0=m14[:, 1024:1536],
                                        in1=m14[:, 1536:2048], op=ALU.add)
        sV.inc(i)
        ropemk[sl] = sV.mark()
        cs_cons[s3] = sV.mark()
        sV.wait(nc.scalar)
        nc.scalar.activation(out=sq_bf[:, :], in_=q_rot[:, :], func=AF.Square,
                             accum_out=q2s[:, T:T + 1])
        sS.inc(nc.scalar.activation(out=sq_bf[:, :], in_=k_rot[:, :],
                                    func=AF.Square,
                                    accum_out=scr[:, 3:4]))
        sS.wait(nc.vector)
        if T == 0:
            nc.vector.tensor_copy(out=scr[:, 2:3], in_=scr[:, 3:4])
        else:
            nc.vector.tensor_tensor(out=scr[:, 2:3], in0=scr[:, 2:3],
                                    in1=scr[:, 3:4], op=ALU.max)
        vsp()
        sV.inc(nc.vector.memset(scr[:, 63:64], 0.0))
        sV.wait(nc.tensor)
        wmark(nc.tensor, evdone)
        for c in range(NCH):
            i = nc.tensor.transpose(paT[:, c * 128:(c + 1) * 128],
                                    q_rot[:, c * 128:(c + 1) * 128],
                                    ident[:, :])
        sT.inc(i)
        sT.wait(nc.scalar)
        wmark(nc.scalar, qfree)
        sS.inc(nc.scalar.copy(out=q_feat[:, :], in_=paT[:, :]))
        sS.wait(nc.tensor)
        for c in range(NCH):
            i = nc.tensor.transpose(paR[:, c * 128:(c + 1) * 128],
                                    k_rot[:, c * 128:(c + 1) * 128],
                                    ident[:, :])
        sT.inc(i)
        qkT = sT.mark()
        sT.wait(nc.scalar)
        kf = K_feat[:, :].rearrange("p (c k) -> p c k", c=NCH)
        pa = paR[:, :].rearrange("p (c k) -> p c k", c=NCH)
        sS.inc(nc.scalar.copy(out=kf[:, :, T * 128:(T + 1) * 128], in_=pa))
        evdone = sS.mark()
        sS.wait(nc.gpsimd)
        sQF.dinc(nc.gpsimd.dma_start(out=t_qfs[T, :, :], in_=q_feat[:, :]))
        qfree = sQF.mark()

    hmk = [None, None]
    csmk = [None, None, None]
    cs_cons = [None, None, None]
    qkmk = [None, None]
    ropemk = [None, None]
    qkT = None
    for T in range(KNT + 2):
        if T < KNT:
            p1_a1(T)
        if 1 <= T <= KNT:
            pM_a1(T - 1)
        if 2 <= T <= KNT + 1:
            pR_a1(T - 2)
    sQF.wait(nc.gpsimd)
    nc.all_engine_barrier()
    stA.close()
    if KPH <= 1:
        return nc

    stKV2 = ExitStack()
    V_tok = stKV2.enter_context(nc.sbuf_tensor("V_tok", [128, NT * D], BF16))
    stA2 = ExitStack()
    wv_bf = stA2.enter_context(nc.sbuf_tensor("wv_bf", [128, NCH * D], BF16))
    h2bf = stA2.enter_context(nc.sbuf_tensor("h2bf", [128, D], BF16))
    h2ft = stA2.enter_context(nc.sbuf_tensor("h2ft", [128, D], BF16))
    stage_bf16(wv_bf, t_wvT, NCH, D, xin[0])
    sV.wait(nc.sync)
    sV.wait(nc.tensor)

    paH2 = stA2.enter_context(nc.psum_tensor("paH2", [128, 1024], BF16))
    paV = stA2.enter_context(nc.psum_tensor("paV", [128, 1024], F32))
    xcons = [None, None]
    evdone = None
    hcons = None
    for T in range(KNT + 1):
        if T < KNT:
            sl = T % 2
            wmark(nc.sync, xcons[sl])
            SLX[sl].dinc(nc.sync.dma_start(
                out=xin[sl][:, :], in_=t_x[T * 128:(T + 1) * 128, :]))
        if T == 0:
            continue
        Tp, sl = T - 1, (T - 1) % 2
        SLX[sl].wait(nc.vector)
        wmark(nc.vector, hcons)
        sV.inc(nc.vector.tensor_scalar(
            out=h2bf[:, :], in0=xin[sl][:, :],
            scalar1=ln1_m[:, Tp:Tp + 1], scalar2=ln1_rs[:, Tp:Tp + 1],
            op0=ALU.subtract, op1=ALU.mult))
        xcons[sl] = sV.mark()
        sV.wait(nc.tensor)
        wmark(nc.tensor, evdone)
        for c in range(NCH):
            i = nc.tensor.transpose(paH2[:, c * 128:(c + 1) * 128],
                                    h2bf[:, c * 128:(c + 1) * 128],
                                    ident[:, :])
        sT.inc(i)
        hcons = sT.mark()
        sT.wait(nc.scalar)
        sS.inc(nc.scalar.copy(out=h2ft[:, :], in_=paH2[:, :]))
        sS.wait(nc.tensor)
        for c in range(NCH):
            lhs = h2ft[:, c * 128:(c + 1) * 128]
            nc.tensor.matmul(paV[:, 0:512], lhs, wv_bf[:, c * D:c * D + 512],
                             start=(c == 0), stop=(c == NCH - 1))
            i = nc.tensor.matmul(paV[:, 512:1024], lhs,
                                 wv_bf[:, c * D + 512:(c + 1) * D],
                                 start=(c == 0), stop=(c == NCH - 1))
        sT.inc(i)
        sT.wait(nc.scalar)
        sS.inc(nc.scalar.copy(out=V_tok[:, Tp * D:(Tp + 1) * D],
                              in_=paV[:, :]))
        evdone = sS.mark()
    sS.wait(nc.vector)
    nc.all_engine_barrier()
    if tap == "B0":
        sFIN.dinc(nc.gpsimd.dma_start(out=t_dK[:, :], in_=K_feat[:, :]))
        sFIN.dinc(nc.gpsimd.dma_start(out=t_dV[:, :], in_=V_tok[:, :]))
        sFIN.wait(nc.gpsimd)
        nc.all_engine_barrier()
    stA2.close()
    if KPH <= 2:
        return nc

    stB = ExitStack()
    wo_bf = stB.enter_context(nc.sbuf_tensor("wo_bf", [128, NCH * D], BF16))
    qf_in = [stB.enter_context(nc.sbuf_tensor(f"qf{i}", [128, D], BF16))
             for i in range(2)]
    probs = stB.enter_context(nc.sbuf_tensor("probs", [128, S], BF16))
    probsT = stB.enter_context(nc.sbuf_tensor("probsT", [128, S], BF16))
    av_bf = stB.enter_context(nc.sbuf_tensor("av_bf", [128, D], BF16))
    of_bf = stB.enter_context(nc.sbuf_tensor("of_bf", [128, D], BF16))
    y1 = stB.enter_context(nc.sbuf_tensor("y1", [128, D], F32))
    x1t = stB.enter_context(nc.sbuf_tensor("x1t", [128, D], F32))
    wmst = stB.enter_context(nc.sbuf_tensor("wmst", [128, 4096], F32))
    xqi = stB.enter_context(nc.sbuf_tensor("xqi", [128, 512], I32))
    xqc = stB.enter_context(nc.sbuf_tensor("xqc", [128, 512], BF16))
    Xb2 = stB.enter_context(nc.sbuf_tensor("Xb2", [128, D], BF16))
    bst = stB.enter_context(nc.sbuf_tensor("bst", [128, 32], F32))

    pass
    stage_bf16(wo_bf, t_woT, NCH, D, xin[0])
    sV.wait(nc.sync)

    with nc.sbuf_tensor("bc", [128, 128], BF16) as bc, \
         nc.sbuf_tensor("onr", [1, 128], F32) as onr, \
         nc.psum_tensor("pbx", [128, 128], BF16) as pbx, \
         nc.psum_tensor("pbb", [128, 1], F32) as pbb:
        nc.vector.memset(bc[:, :], 0.0)
        nc.vector.memset(onr[:, :], 1.0)
        vsp()
        sV.inc(nc.vector.memset(scr[:, 63:64], 0.0))
        sV.wait(nc.scalar)
        sS.inc(nc.scalar.copy(out=bc[:, 0:1], in_=scr[:, 2:3]))
        sS.wait(nc.tensor)
        sT.inc(nc.tensor.transpose(pbx[:, :], bc[:, :], ident[:, :]))
        sT.wait(nc.vector)
        sV.inc(nc.vector.reduce_max(out=scr[0:1, 9:10], in_=pbx[0:1, :],
                                    axis=AX.X))
        sV.wait(nc.scalar)
        nc.scalar.activation(out=scr[0:1, 10:11], in_=scr[0:1, 9:10],
                             func=AF.Ln)
        sS.inc(nc.scalar.activation(out=scr[0:1, 11:12], in_=scr[0:1, 10:11],
                                    func=AF.Exp, scale=0.5))
        sS.wait(nc.vector)
        sV.inc(nc.vector.tensor_scalar(out=scr[0:1, 12:13],
                                       in0=scr[0:1, 11:12],
                                       scalar1=-INVSQ, scalar2=None,
                                       op0=ALU.mult))
        sV.wait(nc.tensor)
        sT.inc(nc.tensor.matmul(pbb[:, :], onr[:, :], scr[0:1, 12:13],
                                start=True, stop=True))
        sT.wait(nc.vector)
        sV.inc(nc.vector.tensor_copy(out=scr[:, 13:14], in_=pbb[:, :]))

    wm_chunks = []
    for tt, col in ((t_wgT, 16), (t_wuT, 17), (t_wdT, 18)):
        rows, cols = (D, INNER) if tt is not t_wdT else (INNER, D)
        for r0 in range(0, rows, 128):
            for c0 in range(0, cols, 2048):
                w_ = min(2048, cols - c0)
                wm_chunks.append((tt[r0:r0 + 128, c0:c0 + w_], col, w_))
    nc.vector.memset(scr[:, 16:19], 0.0)
    vsp()
    sV.inc(nc.vector.memset(scr[:, 63:64], 0.0))

    psS4 = stB.enter_context(nc.psum_tensor("psS4", [128, 2048], F32))
    pbT0 = stB.enter_context(nc.psum_tensor("pbT0", [128, 1024], BF16))
    pbT1 = stB.enter_context(nc.psum_tensor("pbT1", [128, 1024], BF16))
    pbAP = stB.enter_context(nc.psum_tensor("pbAP", [128, 1024], F32))

    qcons = [None, None]
    xcons = [None, None]
    x1free = None
    expdone = None
    wmfree = [None, None]
    ebmk = [None, None]
    summk = [None]
    xsmk = [None]
    xevmk = [None]
    xffree = [None]

    def p1_b(T):
        sl = T % 2
        wmark(nc.sync, qcons[sl])
        SLK[sl].dinc(nc.sync.dma_start(out=qf_in[sl][:, :],
                                       in_=t_qfs[T, :, :]))
        wmark(nc.sync, xcons[sl])
        SLX[sl].dinc(nc.sync.dma_start(
            out=xin[sl][:, :], in_=t_x[T * 128:(T + 1) * 128, :]))
        sS.wait(nc.scalar)
        nc.scalar.activation(out=scr[:, 20:21], in_=q2s[:, T:T + 1],
                             func=AF.Ln)
        sS.inc(nc.scalar.activation(out=scr[:, 21:22], in_=scr[:, 20:21],
                                    func=AF.Exp, scale=0.5))
        sS.wait(nc.vector)
        sV.inc(nc.vector.tensor_tensor(out=bst[:, 24 + sl:25 + sl],
                                       in0=scr[:, 21:22], in1=scr[:, 13:14],
                                       op=ALU.mult))
        ebmk[sl] = sV.mark()

    def xquant_tail(Tq):
        h2B = probs[:, 2048:3072]
        xqy = probs[:, 3072:3584]
        xqb = probs[:, 3584:4096]
        sV.wait(nc.scalar)
        wmark(nc.vector, xffree[0])
        for b in range(2):
            if b == 1:
                sV.wait(nc.scalar)
            hb = h2B[:, b * 512:(b + 1) * 512]
            nc.scalar.activation(out=xqi[:, :], in_=hb, func=AF.Copy,
                                 scale=scr[:, 35:36], bias=-0.5)
            nc.scalar.activation(out=xqy, in_=hb, func=AF.Copy,
                                 scale=scr[:, 35:36])
            sS.inc(nc.scalar.activation(out=xqb, in_=xqi[:, :],
                                        func=AF.Copy, bias=0.5))
            sS.wait(nc.vector)
            nc.vector.tensor_tensor(out=xqc[:, :], in0=xqy,
                                    in1=xqb, op=ALU.is_gt)
            sV.inc(nc.vector.scalar_tensor_tensor(
                out=Xb2[:, b * 512:(b + 1) * 512], in0=xqc[:, :],
                scalar=-0.5, in1=xqb, op0=ALU.add, op1=ALU.add))
        xsmk[0] = sV.mark()

    def xpose_b(Tp):
        wmark(nc.tensor, xsmk[0])
        for c in range(NCH):
            i = nc.tensor.transpose(pbT0[:, c * 128:(c + 1) * 128],
                                    Xb2[:, c * 128:(c + 1) * 128],
                                    ident[:, :])
        sT.inc(i)
        sT.wait(nc.scalar)
        sS.inc(nc.scalar.copy(out=Xb2[:, :], in_=pbT0[:, :]))
        xevmk[0] = sS.mark()
        sS.wait(nc.gpsimd)
        sQF.dinc(nc.gpsimd.dma_start(out=t_xfs[Tp, :, :], in_=Xb2[:, :]))
        xffree[0] = sQF.mark()

    def p2_b(T):
        nonlocal x1free, expdone
        sl = T % 2
        for j in range((len(wm_chunks) * T) // NT,
                       (len(wm_chunks) * (T + 1)) // NT):
            ap, col, w_ = wm_chunks[j]
            wmark(nc.gpsimd, wmfree[j % 2])
            sWM[j % 2].dinc(nc.gpsimd.dma_start(
                out=wmst[:, 2048 * (j % 2):2048 * (j % 2) + w_], in_=ap))
        SLK[sl].wait(nc.tensor)
        qf = qf_in[sl]

        def score_half(half):
            nonlocal expdone
            wmark(nc.tensor, expdone)
            for kb2 in range(4):
                kb = half * 4 + kb2
                for c in range(NCH):
                    i = nc.tensor.matmul(
                        psS4[:, kb2 * 512:(kb2 + 1) * 512],
                        qf[:, c * 128:(c + 1) * 128],
                        K_feat[:, c * S + kb * 512:c * S + (kb + 1) * 512],
                        start=(c == 0), stop=(c == NCH - 1))
            sT.inc(i)
            if half == 1:
                qcons[sl] = sT.mark()
            sT.wait(nc.scalar)
            if half == 0:
                wmark(nc.scalar, ebmk[sl])
                wmark(nc.scalar, summk[0])
            else:
                wmark(nc.scalar, xsmk[0])
            for kb2 in range(4):
                kb = half * 4 + kb2
                i = nc.scalar.activation(
                    out=probs[:, kb * 512:(kb + 1) * 512],
                    in_=psS4[:, kb2 * 512:(kb2 + 1) * 512], func=AF.Exp,
                    bias=bst[:, 24 + sl:25 + sl], scale=INVSQ,
                    accum_out=bst[:, kb:kb + 1])
            sS.inc(i)
            expdone = sS.mark()

        score_half(0)
        if T >= 2:
            xpose_b(T - 2)
        if T >= 1:
            xquant_tail(T - 1)
        score_half(1)
        sS.wait(nc.vector)
        nc.vector.reduce_sum(out=bst[:, 8:9], in_=bst[:, 0:8], axis=AX.X)
        vsp()
        nc.vector.reciprocal(out=bst[:, 9:10], in_=bst[:, 8:9])
        vsp()
        sV.inc(nc.vector.memset(scr[:, 63:64], 0.0))
        summk[0] = sV.mark()
        lo = (len(wm_chunks) * T) // NT
        hi = (len(wm_chunks) * (T + 1)) // NT
        for j in range(lo, hi):
            ap, col, w_ = wm_chunks[j]
            sWM[j % 2].wait(nc.vector)
            nc.vector.tensor_reduce(out=scr[:, 30:31],
                                    in_=wmst[:, 2048 * (j % 2):
                                             2048 * (j % 2) + w_],
                                    axis=AX.X, op=ALU.add,
                                    apply_absolute_value=True)
            vsp()
            sV.inc(nc.vector.tensor_tensor(out=scr[:, col:col + 1],
                                           in0=scr[:, col:col + 1],
                                           in1=scr[:, 30:31], op=ALU.add))
            wmfree[j % 2] = sV.mark()
        if tap == "B0" and T == 0:
            sS.wait(nc.gpsimd)
            sV.wait(nc.gpsimd)
            sFIN.dinc(nc.gpsimd.dma_start(out=t_dP[:, :], in_=probs[:, :]))
            sFIN.dinc(nc.gpsimd.dma_start(out=t_dB[:, :], in_=bst[:, 0:10]))
        evmk = [None, None]
        wmark(nc.tensor, xevmk[0])
        for grp in range(4):
            pb = (pbT0, pbT1)[grp % 2]
            wmark(nc.tensor, evmk[grp % 2])
            for cc in range(8):
                c = grp * 8 + cc
                i = nc.tensor.transpose(pb[:, cc * 128:(cc + 1) * 128],
                                        probs[:, c * 128:(c + 1) * 128],
                                        ident[:, :])
            sT.inc(i)
            sT.wait(nc.scalar)
            sS.inc(nc.scalar.copy(out=probsT[:, grp * 1024:(grp + 1) * 1024],
                                  in_=pb[:, :]))
            evmk[grp % 2] = sS.mark()
        sS.wait(nc.tensor)
        for c in range(NO):
            nc.tensor.matmul(pbAP[:, 0:512],
                             probsT[:, c * 128:(c + 1) * 128],
                             V_tok[:, c * 1024:c * 1024 + 512],
                             start=(c == 0), stop=(c == NO - 1))
            i = nc.tensor.matmul(pbAP[:, 512:1024],
                                 probsT[:, c * 128:(c + 1) * 128],
                                 V_tok[:, c * 1024 + 512:(c + 1) * 1024],
                                 start=(c == 0), stop=(c == NO - 1))
        sT.inc(i)
        sT.wait(nc.scalar)
        sS.inc(nc.scalar.copy(out=av_bf[:, :], in_=pbAP[:, :]))
        sS.wait(nc.tensor)
        for c in range(NCH):
            i = nc.tensor.transpose(pbT0[:, c * 128:(c + 1) * 128],
                                    av_bf[:, c * 128:(c + 1) * 128],
                                    ident[:, :])
        sT.inc(i)
        sT.wait(nc.scalar)
        sS.inc(nc.scalar.copy(out=of_bf[:, :], in_=pbT0[:, :]))
        sS.wait(nc.tensor)
        for c in range(NCH):
            lhs = of_bf[:, c * 128:(c + 1) * 128]
            nc.tensor.matmul(pbAP[:, 0:512], lhs,
                             wo_bf[:, c * D:c * D + 512],
                             start=(c == 0), stop=(c == NCH - 1))
            i = nc.tensor.matmul(pbAP[:, 512:1024], lhs,
                                 wo_bf[:, c * D + 512:(c + 1) * D],
                                 start=(c == 0), stop=(c == NCH - 1))
        sT.inc(i)
        sT.wait(nc.scalar)
        sS.inc(nc.scalar.activation(out=y1[:, :], in_=pbAP[:, :],
                                    func=AF.Identity,
                                    scale=bst[:, 9:10]))
        if tap == "B0" and T == 0:
            sS.wait(nc.gpsimd)
            sFIN.dinc(nc.gpsimd.dma_start(out=t_dPT[:, :], in_=probsT[:, :]))
            sFIN.dinc(nc.gpsimd.dma_start(out=t_dAV[:, :], in_=av_bf[:, :]))
            sFIN.dinc(nc.gpsimd.dma_start(out=t_dY[:, :], in_=y1[:, :]))
            sFIN.dinc(nc.gpsimd.dma_start(out=t_dOF[:, :], in_=of_bf[:, :]))
            sFIN.dinc(nc.gpsimd.dma_start(out=t_dWO[:, :], in_=wo_bf[:, :]))
            sFIN.wait(nc.vector)
        sS.wait(nc.vector)
        nc.vector.bn_stats(bst[:, 10:16], y1[:, 0:512])
        nc.vector.bn_stats(bst[:, 16:22], y1[:, 512:1024])
        vsp()
        nc.vector.bn_aggr(bst[:, 22:24], bst[:, 10:22])
        vsp()
        sV.inc(nc.vector.memset(scr[:, 63:64], 0.0))
        sV.wait(nc.scalar)
        nc.scalar.activation(out=scr[:, 26:27], in_=bst[:, 23:24], func=AF.Ln,
                             bias=LN_EPS, scale=1.0)
        sS.inc(nc.scalar.activation(out=scr[:, 27:28], in_=scr[:, 26:27],
                                    func=AF.Exp, scale=-0.5))
        sS.wait(nc.vector)
        SLX[sl].wait(nc.vector)
        wmark(nc.vector, x1free)
        nc.vector.tensor_scalar(out=y1[:, :], in0=y1[:, :],
                                scalar1=bst[:, 22:23], scalar2=scr[:, 27:28],
                                op0=ALU.subtract, op1=ALU.mult)
        vsp()
        sV.inc(nc.vector.tensor_tensor(out=x1t[:, :], in0=y1[:, :],
                                       in1=xin[sl][:, :], op=ALU.add))
        xcons[sl] = sV.mark()
        nc.vector.bn_stats(bst[:, 10:16], x1t[:, 0:512])
        nc.vector.bn_stats(bst[:, 16:22], x1t[:, 512:1024])
        vsp()
        nc.vector.bn_aggr(bst[:, 22:24], bst[:, 10:22])
        vsp()
        sV.inc(nc.vector.tensor_copy(out=ln2_m[:, T:T + 1],
                                     in_=bst[:, 22:23]))
        sV.wait(nc.scalar)
        nc.scalar.activation(out=scr[:, 26:27], in_=bst[:, 23:24], func=AF.Ln,
                             bias=LN_EPS, scale=1.0)
        sS.inc(nc.scalar.activation(out=ln2_rs[:, T:T + 1],
                                    in_=scr[:, 26:27], func=AF.Exp,
                                    scale=-0.5))
        sV.wait(nc.gpsimd)
        sX1O.dinc(nc.gpsimd.dma_start(
            out=t_x1s[T * 128:(T + 1) * 128, :], in_=x1t[:, :]))
        x1free = sX1O.mark()
        h2B = probs[:, 2048:3072]
        sS.wait(nc.vector)
        nc.vector.tensor_scalar(out=h2B, in0=x1t[:, :],
                                scalar1=ln2_m[:, T:T + 1],
                                scalar2=ln2_rs[:, T:T + 1],
                                op0=ALU.subtract, op1=ALU.mult)
        vsp()
        nc.vector.tensor_reduce(out=scr[:, 33:34], in_=h2B, axis=AX.X,
                                op=ALU.max, apply_absolute_value=True)
        vsp()
        nc.vector.tensor_scalar(out=h2clip[:, T:T + 1], in0=scr[:, 33:34],
                                scalar1=1e-5, scalar2=None, op0=ALU.max)
        vsp()
        nc.vector.reciprocal(out=scr[:, 34:35], in_=h2clip[:, T:T + 1])
        vsp()
        sV.inc(nc.vector.tensor_scalar(out=scr[:, 35:36], in0=scr[:, 34:35],
                                       scalar1=127.0, scalar2=None,
                                       op0=ALU.mult))


    for T in range(KNT + 1):
        if T < KNT:
            p1_b(T)
        if T >= 1:
            p2_b(T - 1)
    if KNT >= 2:
        xpose_b(KNT - 2)
    xquant_tail(KNT - 1)
    xpose_b(KNT - 1)
    sX1O.wait(nc.gpsimd)
    sQF.wait(nc.gpsimd)
    nc.all_engine_barrier()
    stB.close()
    stKV2.close()
    stKV.close()
    if KPH <= 3:
        return nc

    stTH = ExitStack()
    thr_g = stTH.enter_context(nc.sbuf_tensor("thr_g", [128, 12], F32))
    with nc.sbuf_tensor("onc", [128, 1], F32) as onc, \
         nc.sbuf_tensor("onr2", [1, 128], F32) as onr2, \
         nc.psum_tensor("pth", [128, 8], F32) as pth:
        nc.vector.memset(onc[:, :], 1.0)
        nc.vector.memset(onr2[:, :], 1.0)
        vsp()
        sV.inc(nc.vector.memset(scr[:, 63:64], 0.0))
        sV.wait(nc.tensor)
        for i_, col in enumerate((16, 17, 18)):
            i = nc.tensor.matmul(pth[0:1, i_:i_ + 1], scr[:, col:col + 1],
                                 onc[:, :], start=True, stop=True)
        sT.inc(i)
        sT.wait(nc.vector)
        sV.inc(nc.vector.tensor_scalar(out=scr[0:1, 32:35], in0=pth[0:1, 0:3],
                                       scalar1=1.0 / (D * INNER),
                                       scalar2=1e-5, op0=ALU.mult,
                                       op1=ALU.max))
        sV.wait(nc.tensor)
        sT.inc(nc.tensor.matmul(pth[:, 3:6], onr2[:, :], scr[0:1, 32:35],
                                start=True, stop=True))
        sT.wait(nc.vector)
        nc.vector.tensor_copy(out=thr_g[:, 3:6], in_=pth[:, 3:6])
        vsp()
        nc.vector.tensor_scalar(out=thr_g[:, 0:3], in0=thr_g[:, 3:6],
                                scalar1=0.5, scalar2=None, op0=ALU.mult)
        nc.vector.tensor_scalar(out=thr_g[:, 6:9], in0=thr_g[:, 3:6],
                                scalar1=-0.5, scalar2=None, op0=ALU.mult)
        vsp()
        sV.inc(nc.vector.memset(scr[:, 63:64], 0.0))
    sV.wait(nc.tensor)
    sV.wait(nc.scalar)
    sV.wait(nc.sync)
    sV.wait(nc.gpsimd)

    stF1 = ExitStack()
    wg_t = stF1.enter_context(nc.sbuf_tensor("wg_t", [128, NCH * INNER],
                                             BF16))
    wu_t = stF1.enter_context(nc.sbuf_tensor("wu_t", [128, NCH * INNER],
                                             BF16))
    pass

    def ternarize(wtst, tcmp, jobs):
        free = [None, None]
        k = 0
        for dst, src_t, tc, ntc, n_chunks, width in jobs:
            for c in range(n_chunks):
                for c0 in range(0, width, 2048):
                    s = k % 2
                    wmark(nc.sync, free[s])
                    sTD[s].dinc(nc.sync.dma_start(
                        out=wtst[s][:, :],
                        in_=src_t[c * 128:(c + 1) * 128, c0:c0 + 2048]))
                    sTD[s].wait(nc.vector)
                    nc.vector.tensor_scalar(out=tcmp[:, 0:2048],
                                            in0=wtst[s][:, :],
                                            scalar1=thr_g[:, tc:tc + 1],
                                            scalar2=None, op0=ALU.is_gt)
                    nc.vector.tensor_scalar(out=tcmp[:, 2048:4096],
                                            in0=wtst[s][:, :],
                                            scalar1=thr_g[:, ntc:ntc + 1],
                                            scalar2=None, op0=ALU.is_lt)
                    sV.inc(nc.vector.tensor_tensor(
                        out=dst[:, c * width + c0:c * width + c0 + 2048],
                        in0=tcmp[:, 0:2048], in1=tcmp[:, 2048:4096],
                        op=ALU.subtract))
                    free[s] = sV.mark()
                    k += 1

    with nc.sbuf_tensor("wtstA", [128, 2048], F32) as wtstA, \
         nc.sbuf_tensor("wtstB", [128, 2048], F32) as wtstB, \
         nc.sbuf_tensor("tcmpA", [128, 4096], F32) as tcmpA:
        ternarize([wtstA, wtstB], tcmpA,
                  [(wg_t, t_wgT, 0, 6, NCH, INNER),
                   (wu_t, t_wuT, 1, 7, NCH, INNER)])
    sV.wait(nc.tensor)
    sV.wait(nc.scalar)

    graw = stF1.enter_context(nc.sbuf_tensor("graw", [128, INNER], F32))
    silu_b = [stF1.enter_context(nc.sbuf_tensor(f"silu_b{i}", [128, 512],
                                                 F32)) for i in range(2)]
    qq = [stF1.enter_context(nc.sbuf_tensor(f"qq{i}", [128, 512], F32))
          for i in range(4)]
    qi = stF1.enter_context(nc.sbuf_tensor("qi", [128, 512], I32))
    qc = [stF1.enter_context(nc.sbuf_tensor(f"qc{i}", [128, 512], BF16))
          for i in range(2)]
    Gq = stF1.enter_context(nc.sbuf_tensor("Gq", [128, INNER], BF16))
    Xft2 = [stF1.enter_context(nc.sbuf_tensor(f"Xft2{i}", [128, D], BF16))
            for i in range(2)]
    gqf = [stF1.enter_context(nc.sbuf_tensor("gqf0", [128, INNER], BF16))] * 2
    mx8 = stF1.enter_context(nc.sbuf_tensor("mx8", [128, 16], F32))

    pg = [stF1.enter_context(nc.psum_tensor(f"pg{i}", [128, 512], F32))
          for i in range(2)]
    pu = [stF1.enter_context(nc.psum_tensor(f"pu{i}", [128, 512], F32))
          for i in range(2)]
    pgT = [stF1.enter_context(nc.psum_tensor(f"pgT{i}", [128, 1024], BF16))
           for i in range(2)]

    gqfree = [None, None]
    egmk = [None, None]
    gq_mk = None

    def p1_f1(T):
        sl = T % 2
        base = 40 - 8 * sl
        wmark(nc.sync, pamk[sl])
        SLX[sl].dinc(nc.sync.dma_start(out=Xft2[sl][:, :],
                                       in_=t_xfs[T, :, :]))
        nc.vector.tensor_tensor(out=scr[:, base + 4:base + 5],
                                in0=h2clip[:, T:T + 1],
                                in1=thr_g[:, 3:4], op=ALU.mult)
        nc.vector.tensor_tensor(out=scr[:, base + 5:base + 6],
                                in0=h2clip[:, T:T + 1],
                                in1=thr_g[:, 4:5], op=ALU.mult)
        vsp()
        sV.inc(nc.vector.tensor_scalar(out=scr[:, base + 6:base + 8],
                                       in0=scr[:, base + 4:base + 6],
                                       scalar1=1.0 / 127.0, scalar2=None,
                                       op0=ALU.mult))
        egmk[sl] = sV.mark()

    def pA_f1(T):
        sl = T % 2
        base = 40 - 8 * sl
        Xft = Xft2[sl]
        SLX[sl].wait(nc.tensor)
        wmark(nc.scalar, egmk[sl])
        evmk = [None, None]
        for ob in range(8):
            g_, u_ = pg[ob % 2], pu[ob % 2]
            wmark(nc.tensor, evmk[ob % 2])
            for c in range(NCH):
                nc.tensor.matmul(g_[:, :], Xft[:, c * 128:(c + 1) * 128],
                                 wg_t[:, c * INNER + ob * 512:
                                      c * INNER + (ob + 1) * 512],
                                 start=(c == 0), stop=(c == NCH - 1))
            for c in range(NCH):
                i = nc.tensor.matmul(u_[:, :], Xft[:, c * 128:(c + 1) * 128],
                                     wu_t[:, c * INNER + ob * 512:
                                          c * INNER + (ob + 1) * 512],
                                     start=(c == 0), stop=(c == NCH - 1))
            sT.inc(i)
            if ob == 7:
                pamk[sl] = sT.mark()
            sT.wait(nc.scalar)
            sS.inc(nc.scalar.activation(out=silu_b[ob % 2][:, :],
                                        in_=g_[:, :], func=AF.Silu,
                                        scale=scr[:, base + 6:base + 7]))
            sS.wait(nc.vector)
            nc.vector.tensor_tensor(out=graw[:, ob * 512:(ob + 1) * 512],
                                    in0=silu_b[ob % 2][:, :], in1=u_[:, :],
                                    op=ALU.mult)
            vsp()
            sV.inc(nc.vector.tensor_reduce(
                out=mx8[:, ob:ob + 1], in_=graw[:, ob * 512:(ob + 1) * 512],
                axis=AX.X, op=ALU.max, apply_absolute_value=True))
            evmk[ob % 2] = sV.mark()
        vsp()
        nc.vector.reduce_max(out=scr[:, 48:49], in_=mx8[:, 0:8], axis=AX.X)
        vsp()
        nc.vector.tensor_tensor(out=scr[:, 49:50], in0=scr[:, 48:49],
                                in1=scr[:, base + 7:base + 8], op=ALU.mult)
        vsp()
        nc.vector.tensor_scalar(out=scr[:, 50:51], in0=scr[:, 49:50],
                                scalar1=1e-5, scalar2=None, op0=ALU.max)
        vsp()
        nc.vector.reciprocal(out=scr[:, 51:52], in_=scr[:, 50:51])
        nc.vector.tensor_tensor(out=scr[:, 54:55], in0=scr[:, 50:51],
                                in1=thr_g[:, 5:6], op=ALU.mult)
        vsp()
        nc.vector.tensor_scalar(out=scr[:, 52:53], in0=scr[:, 51:52],
                                scalar1=127.0, scalar2=None, op0=ALU.mult)
        nc.vector.tensor_scalar(out=ed_buf[:, T:T + 1], in0=scr[:, 54:55],
                                scalar1=1.0 / 127.0, scalar2=None,
                                op0=ALU.mult)
        vsp()
        sV.inc(nc.vector.tensor_tensor(out=scr[:, 53:54], in0=scr[:, 52:53],
                                       in1=scr[:, base + 7:base + 8],
                                       op=ALU.mult))

    def pC_f1(T):
        nonlocal gq_mk
        sV.wait(nc.scalar)
        wmark(nc.vector, gqT_mk[0])
        gdone = [None, None]
        for ob in range(8):
            gb = graw[:, ob * 512:(ob + 1) * 512]
            r = ob % 2
            wmark(nc.scalar, gdone[r])
            nc.scalar.activation(out=qi[:, :], in_=gb, func=AF.Copy,
                                 scale=scr[:, 53:54], bias=-0.5)
            nc.scalar.activation(out=qq[2 * r][:, :], in_=gb, func=AF.Copy,
                                 scale=scr[:, 53:54])
            sS.inc(nc.scalar.activation(out=qq[2 * r + 1][:, :],
                                        in_=qi[:, :], func=AF.Copy,
                                        bias=0.5))
            sS.wait(nc.vector)
            nc.vector.tensor_tensor(out=qc[r][:, :], in0=qq[2 * r][:, :],
                                    in1=qq[2 * r + 1][:, :], op=ALU.is_gt)
            vsp()
            sV.inc(nc.vector.scalar_tensor_tensor(
                out=Gq[:, ob * 512:(ob + 1) * 512], in0=qc[r][:, :],
                scalar=-0.5, in1=qq[2 * r + 1][:, :],
                op0=ALU.add, op1=ALU.add))
            gdone[r] = sV.mark()
        gq_mk = sV.mark()

    def pB_f1(T):
        sl = T % 2
        g_out = gqf[sl]
        wmark(nc.tensor, gq_mk_t[T])
        for grp in range(4):
            pb = pgT[grp % 2]
            wmark(nc.tensor, evmk2[grp % 2])
            for cc2 in range(8):
                c = grp * 8 + cc2
                i = nc.tensor.transpose(pb[:, cc2 * 128:(cc2 + 1) * 128],
                                        Gq[:, c * 128:(c + 1) * 128],
                                        ident[:, :])
            sT.inc(i)
            if grp == 3:
                gqT_mk[0] = sT.mark()
            sT.wait(nc.scalar)
            wmark(nc.scalar, gqfree[sl])
            sS.inc(nc.scalar.copy(
                out=g_out[:, grp * 1024:(grp + 1) * 1024], in_=pb[:, :]))
            evmk2[grp % 2] = sS.mark()
            sS.wait(nc.gpsimd)
            sGQ.dinc(nc.gpsimd.dma_start(
                out=t_gqs[T, :, grp * 1024:(grp + 1) * 1024],
                in_=g_out[:, grp * 1024:(grp + 1) * 1024]))
        gqfree[sl] = sGQ.mark()

    gq_mk_t = {}
    pamk = [None, None]
    gqT_mk = [None]
    evmk2 = [None, None]
    for T in range(KNT + 2):
        if T < KNT:
            p1_f1(T)
        if 1 <= T <= KNT:
            pA_f1(T - 1)
        if 2 <= T <= KNT + 1:
            pB_f1(T - 2)
        if 1 <= T <= KNT:
            pC_f1(T - 1)
            gq_mk_t[T - 1] = gq_mk
    sGQ.wait(nc.gpsimd)
    nc.all_engine_barrier()
    stF1.close()
    if KPH <= 4:
        return nc

    stF2 = ExitStack()
    wd_t = stF2.enter_context(nc.sbuf_tensor("wd_t", [128, NO * D], BF16))
    wtsd = [stF2.enter_context(nc.sbuf_tensor(f"wtsd{i}", [128, D], F32))
            for i in range(2)]
    tcm2 = stF2.enter_context(nc.sbuf_tensor("tcm2", [128, 2048], F32))
    gq_in = [stF2.enter_context(nc.sbuf_tensor(f"gqi{i}", [128, INNER],
                                               BF16)) for i in range(2)]
    y3 = stF2.enter_context(nc.sbuf_tensor("y3", [128, D], F32))
    outt = [stF2.enter_context(nc.sbuf_tensor(f"outt{i}", [128, D], F32))
            for i in range(2)]
    bs2 = stF2.enter_context(nc.sbuf_tensor("bs2", [128, 16], F32))

    pass

    sWD = Sem("wd")
    wdfree = [None, None]
    for c in range(NO):
        s = c % 2
        wmark(nc.sync, wdfree[s])
        sTD[s].dinc(nc.sync.dma_start(out=wtsd[s][:, :],
                                      in_=t_wdT[c * 128:(c + 1) * 128, :]))
        sTD[s].wait(nc.vector)
        nc.vector.tensor_scalar(out=tcm2[:, 0:1024], in0=wtsd[s][:, :],
                                scalar1=thr_g[:, 2:3], scalar2=None,
                                op0=ALU.is_gt)
        nc.vector.tensor_scalar(out=tcm2[:, 1024:2048], in0=wtsd[s][:, :],
                                scalar1=thr_g[:, 8:9], scalar2=None,
                                op0=ALU.is_lt)
        sWD.inc(nc.vector.tensor_tensor(out=wd_t[:, c * D:(c + 1) * D],
                                        in0=tcm2[:, 0:1024],
                                        in1=tcm2[:, 1024:2048],
                                        op=ALU.subtract))
        wdfree[s] = sWD.mark()

    if tap == "B0":
        nc.vector.wait_ge(sWD.h, NO)
        vsp()
        sV.inc(nc.vector.memset(scr[:, 63:64], 0.0))
        sV.wait(nc.gpsimd)
        sFIN.dinc(nc.gpsimd.dma_start(out=t_dWD[:, :], in_=wd_t[:, :]))
        sFIN.dinc(nc.gpsimd.dma_start(out=t_dED[:, :], in_=ed_buf[:, :]))
        sFIN.dinc(nc.gpsimd.dma_start(out=t_dTH[:, :], in_=thr_g[:, 0:9]))
    pd = [stF2.enter_context(nc.psum_tensor(f"pd{i}", [128, 1024], F32))
          for i in range(2)]
    xcons = [None, None]
    gcons = [None, None]
    ofree = [None, None]
    ofin = [None, None]
    for T in range(KNT + 1):
        if T < KNT:
            sl = T % 2
            wmark(nc.sync, gcons[sl])
            SLK[sl].dinc(nc.sync.dma_start(out=gq_in[sl][:, :],
                                           in_=t_gqs[T, :, :]))
            wmark(nc.sync, xcons[sl])
            SLX[sl].dinc(nc.sync.dma_start(
                out=xin[sl][:, :], in_=t_x1s[T * 128:(T + 1) * 128, :]))
        if T == 0:
            continue
        Tp, sl = T - 1, (T - 1) % 2
        p_ = pd[sl]
        SLK[sl].wait(nc.tensor)
        wmark(nc.tensor, ofree[sl])
        gq_c = gq_in[sl]
        for c in range(NO):
            if Tp == 0:
                nc.tensor.wait_ge(sWD.h, c + 1)
            nc.tensor.matmul(p_[:, 0:512], gq_c[:, c * 128:(c + 1) * 128],
                             wd_t[:, c * D:c * D + 512],
                             start=(c == 0), stop=(c == NO - 1))
            i = nc.tensor.matmul(p_[:, 512:1024],
                                 gq_c[:, c * 128:(c + 1) * 128],
                                 wd_t[:, c * D + 512:(c + 1) * D],
                                 start=(c == 0), stop=(c == NO - 1))
        sT.inc(i)
        gcons[sl] = sT.mark()
        sT.wait(nc.scalar)
        sS.inc(nc.scalar.activation(out=y3[:, :], in_=p_[:, :],
                                    func=AF.Identity,
                                    scale=ed_buf[:, Tp:Tp + 1]))
        ofree[sl] = sS.mark()
        if tap == "B0" and Tp == 0:
            sS.wait(nc.gpsimd)
            sFIN.dinc(nc.gpsimd.dma_start(out=t_dY3[:, :], in_=y3[:, :]))
        sS.wait(nc.vector)
        nc.vector.bn_stats(bs2[:, 0:6], y3[:, 0:512])
        nc.vector.bn_stats(bs2[:, 6:12], y3[:, 512:1024])
        vsp()
        nc.vector.bn_aggr(bs2[:, 12:14], bs2[:, 0:12])
        vsp()
        sV.inc(nc.vector.memset(scr[:, 63:64], 0.0))
        sV.wait(nc.scalar)
        nc.scalar.activation(out=scr[:, 56:57], in_=bs2[:, 13:14], func=AF.Ln,
                             bias=LN_EPS, scale=1.0)
        sS.inc(nc.scalar.activation(out=scr[:, 57:58], in_=scr[:, 56:57],
                                    func=AF.Exp, scale=-0.5))
        sS.wait(nc.vector)
        SLX[sl].wait(nc.vector)
        nc.vector.tensor_scalar(out=y3[:, :], in0=y3[:, :],
                                scalar1=bs2[:, 12:13], scalar2=scr[:, 57:58],
                                op0=ALU.subtract, op1=ALU.mult)
        vsp()
        wmark(nc.vector, ofin[sl])
        sV.inc(nc.vector.tensor_tensor(out=outt[sl][:, :], in0=y3[:, :],
                                       in1=xin[sl][:, :], op=ALU.add))
        xcons[sl] = sV.mark()
        sV.wait(nc.gpsimd)
        sFIN.dinc(nc.gpsimd.dma_start(
            out=t_out[Tp * 128:(Tp + 1) * 128, :], in_=outt[sl][:, :]))
        ofin[sl] = sFIN.mark()

    if tap == "x1":
        for T in range(NT):
            sFIN.dinc(nc.sync.dma_start(
                out=t_dbg[T * 128:(T + 1) * 128, :],
                in_=t_x1s[T * 128:(T + 1) * 128, :]))
    sFIN.wait(nc.gpsimd)
    stF2.close()
    stTH.close()
    es.close()
    return nc


_NC_CACHE = {}


def _get_nc(tap):
    if tap not in _NC_CACHE:
        _NC_CACHE[tap] = _build(tap)
    return _NC_CACHE[tap]


def _host_tables():
    j = np.arange(512, dtype=np.float64)
    inv = np.exp(-C_INV * j)
    t = np.arange(S, dtype=np.float64)
    ang = t[:, None] * inv[None, :]
    ks = np.round(ang / (2 * math.pi)).astype(np.float32)
    kc = np.round((ang + PI_2) / (2 * math.pi)).astype(np.float32)
    return np.ascontiguousarray(ks), np.ascontiguousarray(kc)


def kernel(L0_attn_x, L0_attn_wq, L0_attn_wk, L0_attn_wv, L0_attn_wo,
           L0_ffn_w_gate, L0_ffn_w_up, L0_ffn_w_down):
    x = np.ascontiguousarray(np.asarray(L0_attn_x, dtype=np.float32))
    wq = np.asarray(L0_attn_wq, dtype=np.float32)
    wk = np.asarray(L0_attn_wk, dtype=np.float32)
    wv = np.asarray(L0_attn_wv, dtype=np.float32)
    wo = np.asarray(L0_attn_wo, dtype=np.float32)
    wg = np.asarray(L0_ffn_w_gate, dtype=np.float32)
    wu = np.asarray(L0_ffn_w_up, dtype=np.float32)
    wd = np.asarray(L0_ffn_w_down, dtype=np.float32)

    perm = np.concatenate([np.arange(0, D, 2), np.arange(1, D, 2)])
    common = dict(
        wqT=np.ascontiguousarray(wq[perm, :].T),
        wkT=np.ascontiguousarray(wk[perm, :].T),
        wvT=np.ascontiguousarray(wv.T),
        woT=np.ascontiguousarray(wo.T),
        wgT=np.ascontiguousarray(wg.T),
        wuT=np.ascontiguousarray(wu.T),
        wdT=np.ascontiguousarray(wd.T),
        idn=np.eye(128, dtype=np.float32),
    )
    common["ks"], common["kc"] = _host_tables()

    tap = _DEBUG_TAP
    nc = _get_nc(tap)
    in_maps = [dict(x=np.ascontiguousarray(x[b]), **common) for b in range(8)]
    res = run_bass_kernel_spmd(nc, in_maps, core_ids=list(range(8)))
    out = np.stack([np.asarray(r["out"]) for r in res.results], axis=0)
    if tap is not None:
        kernel._dbg = np.stack([np.asarray(r["dbg"]) for r in res.results], 0)
    return out
